# version 08
# baseline (speedup 1.0000x reference)
import os
import contextlib
import numpy as np
import ml_dtypes
import concourse.bass as bass
import concourse.mybir as mybir
from concourse.bass_utils import run_bass_kernel_spmd

F32 = mybir.dt.float32
BF16 = mybir.dt.bfloat16
AF = mybir.ActivationFunctionType
ALU = mybir.AluOpType
AX = mybir.AxisListType

D = 1024
T = 2048
NCH = 8
NTG = 4
TG = 512
LN_EPS = 1e-5
ATT_EPS = 1e-6

BF = ml_dtypes.bfloat16



def _wchunk(w):
    w = np.asarray(w, np.float32)
    k, m = w.shape
    return np.ascontiguousarray(w.reshape(k // 128, 128, m))


def _colvec(b):
    b = np.asarray(b, np.float32)
    return np.ascontiguousarray(b.reshape(-1, 128).T)



class Sem:
    def __init__(self, es, nc, name):
        self.h = es.enter_context(nc.semaphore(name))
        self.n = 0

    def inc(self, instr, by=1):
        instr.then_inc(self.h, by)
        self.n += by
        return self.n


class P:

    def __init__(self, nc, es, blk):
        self.nc = nc
        self.es = es
        self.blk = blk
        self.pe = Sem(es, nc, "s_pe")
        self.dve = Sem(es, nc, "s_dve")
        self.act = Sem(es, nc, "s_act")
        self.pool = Sem(es, nc, "s_pool")
        self.ds = Sem(es, nc, "d_sync")
        self.dg = Sem(es, nc, "d_gps")
        self.da = Sem(es, nc, "d_act")
        self.all_sems = [self.pe, self.dve, self.act, self.pool,
                         self.ds, self.dg, self.da]
        self.C = {}
        self.dram = {}
        self._uid = 0

    def uid(self):
        self._uid += 1
        return self._uid

    def sbuf(self, name, shape, dtype, es=None):
        return (es or self.es).enter_context(
            self.nc.sbuf_tensor(name, shape, dtype))

    def init_arena(self, words):
        self._arena = self.es.enter_context(
            self.nc.sbuf_tensor("ARENA", [128, words], F32))
        self._free = [(0, words)]
        self._ar_words = words

    def _alloc(self, name, n):
        for i, (o, l) in enumerate(self._free):
            if l >= n:
                if l == n:
                    self._free.pop(i)
                else:
                    self._free[i] = (o + n, l - n)
                return o
        raise AssertionError(
            f"arena overflow: {name} needs {n}; free={self._free}")

    def _release(self, off, n):
        self._free.append((off, n))
        self._free.sort()
        merged = []
        for o, l in self._free:
            if merged and merged[-1][0] + merged[-1][1] == o:
                merged[-1] = (merged[-1][0], merged[-1][1] + l)
            else:
                merged.append((o, l))
        self._free = merged

    def tile(self, name, shape, dtype, es=None):
        assert shape[0] <= 128
        nbytes = shape[1] * mybir.dt.size(dtype)
        nwords = (nbytes + 3) // 4
        off = self._alloc(name, nwords)
        if es is not None:
            es.callback(lambda o=off, n=nwords: self._release(o, n))
        if dtype == F32:
            v = self._arena[0:shape[0], off:off + shape[1]]
        elif dtype == BF16:
            v = self._arena[:, :].bitcast(BF16)[0:shape[0],
                                                2 * off:2 * off + shape[1]]
        else:
            raise ValueError(dtype)
        return _TileView(v)


    def psum(self, name, shape=(128, 512), dtype=F32):
        return self.es.enter_context(
            self.nc.psum_tensor(name, list(shape), dtype))

    def snap(self):
        return [(s, s.n) for s in self.all_sems]

    def wait_snap(self, eng, snap):
        for s, n in snap:
            if n:
                eng.wait_ge(s.h, n)


class StreamBuf:

    def __init__(self, p, name):
        self.p = p
        self.name = name
        self.es = None
        self.ap = None

    def alloc(self):
        assert self.es is None
        self.es = contextlib.ExitStack()
        self.ap = self.p.tile(self.name, [128, NCH * T], F32, self.es)[:]
        return self.ap

    def drop(self):
        self.es.close()
        self.es = None
        self.ap = None

    def spill(self, dram_name):
        emit_spill(self.p, self.ap, dram_name)
        self.drop()

    def reload(self, dram_name):
        self.alloc()
        emit_spill(self.p, self.ap, dram_name, load=True)
        return self.ap


class _TileView:
    def __init__(self, ap):
        self._ap = ap

    def __getitem__(self, idx):
        return self._ap[idx]

    def ap(self):
        return self._ap


def emit_consts(p):
    blk = p.blk
    C = p.C
    C["ones"] = p.sbuf("c_ones", [128, 128], BF16)
    C["onesf"] = p.sbuf("c_onesf", [1, 128], F32)
    C["ident"] = p.sbuf("c_ident", [128, 128], BF16)
    C["trilx4"] = p.sbuf("c_trilx4", [128, 512], F32)
    C["identf"] = p.sbuf("c_identf", [128, 128], F32)
    C["identb64"] = p.sbuf("c_identb64", [128, 64], BF16)
    C["zeros"] = p.sbuf("c_zeros", [128, 1], F32)

    @blk.vector
    def _(v):
        v.memset(C["ones"][:], 1.0)
        v.memset(C["onesf"][:], 1.0)
        v.memset(C["zeros"][:], 0.0)
        instr = v.memset(C["trilx4"][:], 1.0)
        p.dve.inc(instr)

    dn = p.dve.n

    @blk.gpsimd
    def _(g):
        g.wait_ge(p.dve.h, dn)
        g.affine_select(C["ident"][:], C["ones"][:], [[-1, 128]],
                        ALU.is_equal, 0.0, channel_multiplier=1)
        g.affine_select(C["identf"][:], C["trilx4"][:, 0:128], [[-1, 128]],
                        ALU.is_equal, 0.0, channel_multiplier=1)
        g.affine_select(C["identb64"][:], C["ones"][:, 0:64], [[-1, 64]],
                        ALU.is_equal, 0.0, channel_multiplier=1, base=-64)
        instr = g.affine_select(C["trilx4"][:], C["trilx4"][:],
                                [[0, 4], [-1, 128]], ALU.is_ge, 0.0,
                                channel_multiplier=1)
        p.pool.inc(instr)



class LnScratch:
    def __init__(self, p, tag="", es=None):
        self.xbf = p.tile(f"lnS_xbf{tag}", [128, 2 * TG], BF16, es)
        self.xsq = p.tile(f"lnS_xsq{tag}", [128, 2 * TG], BF16, es)
        self.R = p.tile(f"lnS_R{tag}", [1, 3 * TG], F32, es)
        self.AB = p.tile(f"lnS_AB{tag}", [1, 2 * TG], F32, es)
        self.HL = p.tile(f"lnS_HL{tag}", [1, 2 * T], BF16, es)
        self.LL = p.tile(f"lnS_LL{tag}", [1, 2 * T], BF16, es)
        self.t1 = p.tile(f"lnS_t1{tag}", [128, 2 * TG], F32, es)


def emit_ln(p, S, SRC, DST, DST_BF, gcol=None, becol=None):
    blk = p.blk
    C = p.C
    ds_entry = p.ds.n
    U = p.uid()
    with contextlib.ExitStack() as ps_es:
        ps_stat = [ps_es.enter_context(p.nc.psum_tensor(f"ps_lnst{i}_{U}", [128, 512], F32))
                   for i in range(8)]

        start = p.snap()
        dve0 = p.dve.n
        act0 = p.act.n
        pe0 = p.pe.n

        @blk.vector
        def _(v):
            p.wait_snap(v, start)
            for c in range(NCH):
                for tg in range(NTG):
                    idx = c * NTG + tg
                    if idx >= 2:
                        v.wait_ge(p.pe.h, pe0 + idx - 1)
                    instr = v.tensor_copy(
                        S.xbf[:, (idx % 2) * TG:(idx % 2) * TG + TG],
                        SRC[:, c * T + tg * TG:c * T + tg * TG + TG])
                    p.dve.inc(instr)

        @blk.scalar
        def _(s):
            p.wait_snap(s, start)
            for idx in range(NCH * NTG):
                s.wait_ge(p.dve.h, dve0 + idx + 1)
                instr = s.activation(
                    S.xsq[:, (idx % 2) * TG:(idx % 2) * TG + TG],
                    S.xbf[:, (idx % 2) * TG:(idx % 2) * TG + TG], AF.Square)
                p.act.inc(instr)

        @blk.tensor
        def _(t):
            for c in range(NCH):
                for tg in range(NTG):
                    idx = c * NTG + tg
                    t.wait_ge(p.act.h, act0 + idx + 1)
                    for si in range(2):
                        srcx = (S.xbf, S.xsq)[si]
                        instr = t.matmul(
                            ps_stat[si * 4 + tg][0:1, 0:TG],
                            C["ones"][:, 0:1],
                            srcx[:, (idx % 2) * TG:(idx % 2) * TG + TG],
                            start=(c == 0), stop=(c == NCH - 1))
                    p.pe.inc(instr)

        pe_stats = p.pe.n
        mu = S.R[0:1, 0:TG]
        var = S.R[0:1, TG:2 * TG]
        tmp = S.R[0:1, 2 * TG:3 * TG]
        arow = S.AB[0:1, 0:TG]
        brow = S.AB[0:1, TG:2 * TG]

        for tg in range(NTG):
            a_hi = S.HL[0:1, tg * TG:tg * TG + TG]
            b_hi = S.HL[0:1, T + tg * TG:T + tg * TG + TG]
            a_lo = S.LL[0:1, tg * TG:tg * TG + TG]
            b_lo = S.LL[0:1, T + tg * TG:T + tg * TG + TG]

            @blk.vector
            def _(v, tg=tg):
                v.wait_ge(p.pe.h, pe_stats)
                v.tensor_single_scalar(mu, ps_stat[tg][0:1, 0:TG],
                                       1.0 / D, ALU.mult)
                v.tensor_single_scalar(tmp, ps_stat[4 + tg][0:1, 0:TG],
                                       1.0 / D, ALU.mult)
                v.tensor_tensor(var, mu, mu, ALU.mult)
                v.scalar_tensor_tensor(var, var, -1.0, tmp, ALU.mult, ALU.add)
                instr = v.tensor_single_scalar(var, var, LN_EPS, ALU.add)
                p.dve.inc(instr)

            d_var = p.dve.n

            @blk.scalar
            def _(s):
                s.wait_ge(p.dve.h, d_var)
                s.activation(tmp, var, AF.Ln)
                instr = s.activation(arow, tmp, AF.Exp, scale=-0.5)
                p.act.inc(instr)

            a_rstd = p.act.n

            @blk.vector
            def _(v, a_hi=a_hi, b_hi=b_hi, a_lo=a_lo, b_lo=b_lo):
                v.wait_ge(p.act.h, a_rstd)
                v.scalar_tensor_tensor(brow, mu, -1.0, arow, ALU.mult, ALU.mult)
                v.tensor_copy(a_hi, arow)
                v.tensor_copy(b_hi, brow)
                v.tensor_copy(tmp, a_hi)
                v.tensor_tensor(a_lo, arow, tmp, ALU.subtract)
                v.tensor_copy(tmp, b_hi)
                instr = v.tensor_tensor(b_lo, brow, tmp, ALU.subtract)
                p.dve.inc(instr)

        d_rows = p.dve.n

    with contextlib.ExitStack() as ps_es:
        ps_ab = [ps_es.enter_context(p.nc.psum_tensor(f"ps_lnab{i}_{U}", [128, 512], F32))
                 for i in range(4)]
        act_marks = []
        dve_tg_marks = []
        for tg in range(NTG):
            a_hi = S.HL[0:1, tg * TG:tg * TG + TG]
            b_hi = S.HL[0:1, T + tg * TG:T + tg * TG + TG]
            a_lo = S.LL[0:1, tg * TG:tg * TG + TG]
            b_lo = S.LL[0:1, T + tg * TG:T + tg * TG + TG]
            bank_a = ps_ab[2 * (tg % 2)]
            bank_b = ps_ab[2 * (tg % 2) + 1]

            @blk.tensor
            def _(t, tg=tg, bank_a=bank_a, bank_b=bank_b,
                  a_hi=a_hi, b_hi=b_hi, a_lo=a_lo, b_lo=b_lo):
                t.wait_ge(p.dve.h, d_rows)
                if tg >= 2:
                    t.wait_ge(p.dve.h, dve_tg_marks[tg - 2])
                for bank, hi, lo in ((bank_a, a_hi, a_lo), (bank_b, b_hi, b_lo)):
                    for half, rt in ((0, hi), (1, lo)):
                        instr = t.matmul(bank[:, 0:TG], C["ones"][0:1, :], rt,
                                         start=(half == 0), stop=(half == 1))
                        if half == 1:
                            p.pe.inc(instr)

            pe_bc = p.pe.n

            for c in range(NCH):
                @blk.vector
                def _(v, c=c, tg=tg, bank_a=bank_a, bank_b=bank_b):
                    v.wait_ge(p.pe.h, pe_bc)
                    t1c = S.t1[:, (c % 2) * TG:(c % 2) * TG + TG]
                    if len(act_marks) >= 2:
                        v.wait_ge(p.act.h, act_marks[-2])
                    v.tensor_tensor(t1c, SRC[:, c * T + tg * TG:
                                             c * T + tg * TG + TG],
                                    bank_a[:, 0:TG], ALU.mult)
                    instr = v.tensor_tensor(t1c, t1c, bank_b[:, 0:TG], ALU.add)
                    p.dve.inc(instr)

                d_t = p.dve.n

                @blk.scalar
                def _(s, c=c, tg=tg):
                    s.wait_ge(p.dve.h, d_t)
                    if gcol is not None:
                        s.wait_ge(p.ds.h, ds_entry)
                    t1c = S.t1[:, (c % 2) * TG:(c % 2) * TG + TG]
                    dsl = slice(c * T + tg * TG, c * T + tg * TG + TG)
                    for dst in (DST, DST_BF):
                        if dst is None:
                            continue
                        if gcol is None:
                            instr = s.activation(dst[:, dsl], t1c, AF.Identity)
                        else:
                            instr = s.activation(dst[:, dsl], t1c, AF.Identity,
                                                 bias=becol[:, c:c + 1],
                                                 scale=gcol[:, c:c + 1])
                        p.act.inc(instr)
                act_marks.append(p.act.n)
            dve_tg_marks.append(p.dve.n)

    return dict(act_done=p.act.n, dve_done=p.dve.n)



def emit_stream_load(p, dram_name, DSTAP, f32=True):
    blk = p.blk
    x = p.dram[dram_name]

    @blk.sync
    def _(sync):
        src = x.rearrange("(c p) t -> p c t", p=128)
        dst = DSTAP.rearrange("p (c t) -> p c t", t=T)
        instr = sync.dma_start(out=dst, in_=src)
        p.ds.inc(instr, 16)
    return p.ds.n


def emit_stream_store(p, SRCAP, dram_name, waits=None):
    blk = p.blk
    y = p.dram[dram_name]

    @blk.sync
    def _(sync):
        if waits:
            p.wait_snap(sync, waits)
        dst = y.rearrange("(c p) t -> p c t", p=128)
        srcv = SRCAP.rearrange("p (c t) -> p c t", t=T)
        instr = sync.dma_start(out=dst, in_=srcv)
        p.ds.inc(instr, 16)
    return p.ds.n



def emit_castload(p, eng_sections_done, pairs):
    blk = p.blk

    @blk.gpsimd
    def _(g):
        for dst, srcap in pairs:
            instr = g.dma_start(out=dst, in_=srcap)
            p.dg.inc(instr, 16)
    return p.dg.n



def emit_l0ffn(p, XN, XNBF, xo_get, pre_e=None, pre_d=None):
    blk = p.blk
    C = p.C
    dr = p.dram
    NE = 16
    NKC = 68

    identf = C["identf"]
    zeros = C["zeros"]
    es_r = contextlib.ExitStack()
    gate_w = p.tile("ff_gatew", [128, 8 * 16], F32, es_r)
    gate_b = p.tile("ff_gateb", [16, 1], F32, es_r)
    r_fm = p.tile("ff_rfm", [16, T], F32, es_r)
    r_tm = p.tile("ff_rtm", [128, 16 * 16], F32, es_r)
    M1 = p.tile("ff_m1", [128, 16], F32, es_r)
    M2 = p.tile("ff_m2", [128, 16], F32, es_r)
    WR = p.tile("ff_wr", [128, 16], F32, es_r)
    I2 = p.tile("ff_i2", [128, 16], F32, es_r)
    TMPB = p.tile("ff_tmpb", [128, 16], F32, es_r)
    CW = p.tile("ff_cw", [128, 16 * 16], F32, es_r)
    CWF = p.tile("ff_cwf", [16, T], F32, es_r)

    start = p.snap()

    @blk.sync
    def _(sync):
        p.wait_snap(sync, start)
        instr = sync.dma_start(
            out=gate_w[:].rearrange("p (k m) -> p k m", m=16),
            in_=dr["L0_ffn_gate_w"].rearrange("k p m -> p k m"))
        p.ds.inc(instr, 16)
        instr = sync.dma_start(out=gate_b[:], in_=dr["L0_ffn_gate_b"])
        p.ds.inc(instr, 16)

    ld_gw = ld_gb = p.ds.n

    with contextlib.ExitStack() as pse:
        ps_r = [pse.enter_context(p.nc.psum_tensor(f"ps_ffr{i}", [128, 512], F32))
                for i in range(2)]
        ps_t = pse.enter_context(p.nc.psum_tensor("ps_fft", [128, 512], F32))
        ps_t2 = pse.enter_context(p.nc.psum_tensor("ps_fft2", [128, 512], F32))

        act_r0 = p.act.n

        @blk.tensor
        def _(t):
            t.wait_ge(p.ds.h, ld_gw)
            p.wait_snap(t, start)
            for tg in range(NTG):
                if tg >= 2:
                    t.wait_ge(p.act.h, act_r0 + tg - 1)
                for k in range(NCH):
                    instr = t.matmul(
                        ps_r[tg % 2][0:16, 0:TG], gate_w[:, k * 16:k * 16 + 16],
                        XN[:, k * T + tg * TG:k * T + tg * TG + TG],
                        start=(k == 0), stop=(k == NCH - 1))
                p.pe.inc(instr)

        pe_r0 = p.pe.n - NTG

        @blk.scalar
        def _(s):
            s.wait_ge(p.ds.h, ld_gb)
            for tg in range(NTG):
                s.wait_ge(p.pe.h, pe_r0 + tg + 1)
                instr = s.activation(r_fm[:, tg * TG:tg * TG + TG],
                                     ps_r[tg % 2][0:16, 0:TG], AF.Sigmoid,
                                     bias=gate_b[:])
                p.act.inc(instr)

        act_r = p.act.n
        assert act_r == act_r0 + NTG

        d_id = p.dve.n

        @blk.tensor
        def _(t):
            t.wait_ge(p.act.h, act_r)
            t.wait_ge(p.dve.h, d_id)
            for tc in range(16):
                instr = t.matmul(ps_t[:, tc * 16:tc * 16 + 16],
                                 r_fm[:, tc * 128:tc * 128 + 128],
                                 identf[0:16, 0:16], is_transpose=True,
                                 start=True, stop=True)
            p.pe.inc(instr)

        pe_tr = p.pe.n

        @blk.vector
        def _(v):
            v.wait_ge(p.pe.h, pe_tr)
            instr = v.tensor_copy(r_tm[:], ps_t[:, 0:256])
            p.dve.inc(instr)

        d_rtm = p.dve.n

        if "dbg_rtm" in dr:
            @blk.sync
            def _(sync):
                sync.wait_ge(p.dve.h, d_rtm)
                p.ds.inc(sync.dma_start(out=dr["dbg_rfm"], in_=r_fm[:]), 16)
                p.ds.inc(sync.dma_start(out=dr["dbg_rtm"], in_=r_tm[:]), 16)

        @blk.vector
        def _(v):
            v.wait_ge(p.dve.h, d_rtm)
            zb = zeros[:, 0:1].broadcast_to([128, 16])

            def S(instr):
                p.dve.inc(instr)
                v.wait_ge(p.dve.h, p.dve.n)
            for tc in range(16):
                blkr = r_tm[:, tc * 16:tc * 16 + 16]
                cwb = CW[:, tc * 16:tc * 16 + 16]
                m1 = M1[:, tc:tc + 1]
                m2 = M2[:, tc:tc + 1]
                S(v.reduce_max(m1, blkr, AX.X))
                S(v.scalar_tensor_tensor(cwb, blkr, m1, zb, ALU.subtract, ALU.is_ge))
                S(v.scalar_tensor_tensor(I2[:], cwb, -1e30, blkr, ALU.mult, ALU.add))
                S(v.reduce_max(m2, I2[:], AX.X))
                S(v.tensor_tensor(WR[:, tc:tc + 1], m1, m2, ALU.add))
                S(v.scalar_tensor_tensor(I2[:], I2[:], m2, zb, ALU.subtract, ALU.is_ge))
                S(v.scalar_tensor_tensor(TMPB[:], I2[:], m2, zb, ALU.mult, ALU.add))
                S(v.scalar_tensor_tensor(cwb, cwb, m1, TMPB[:], ALU.mult, ALU.add))
            S(v.reciprocal(WR[:], WR[:]))
            for tc in range(16):
                S(v.scalar_tensor_tensor(
                    CW[:, tc * 16:tc * 16 + 16], CW[:, tc * 16:tc * 16 + 16],
                    WR[:, tc:tc + 1], zeros[:, 0:1].broadcast_to([128, 16]),
                    ALU.mult, ALU.add))

        d_cw = p.dve.n

        if "dbg_cw" in dr:
            @blk.sync
            def _(sync):
                sync.wait_ge(p.dve.h, d_cw)
                p.ds.inc(sync.dma_start(out=dr["dbg_cw"], in_=CW[:]), 16)

        d_cwf0 = p.dve.n

        @blk.tensor
        def _(t):
            t.wait_ge(p.dve.h, d_cw)
            for tc in range(16):
                bank = (ps_t, ps_t2)[tc % 2]
                if tc >= 2:
                    t.wait_ge(p.dve.h, d_cwf0 + tc - 1)
                instr = t.matmul(bank[0:16, 0:128],
                                 CW[:, tc * 16:tc * 16 + 16], identf[:],
                                 is_transpose=True, start=True, stop=True)
                p.pe.inc(instr)

        pe_cwt0 = p.pe.n - 16

        @blk.vector
        def _(v):
            for tc in range(16):
                v.wait_ge(p.pe.h, pe_cwt0 + tc + 1)
                bank = (ps_t, ps_t2)[tc % 2]
                instr = v.tensor_copy(CWF[:, tc * 128:tc * 128 + 128],
                                      bank[0:16, 0:128])
                p.dve.inc(instr)

        d_cwf = p.dve.n
        assert d_cwf == d_cwf0 + 16

    @blk.sync
    def _(sync):
        sync.wait_ge(p.dve.h, d_cwf)
        instr = sync.dma_start(out=dr["cwd"], in_=CWF[:])
        p.ds.inc(instr, 16)

    ds_cwd = p.ds.n
    es_r.close()
    if pre_e is not None:
        pre_e()
    es_e = contextlib.ExitStack()
    CWB = p.tile("ff_cwb", [128, 2 * T], F32, es_e)
    WGU = p.tile("ff_wgu", [128, 2 * 8 * 512], BF16, es_e)
    WGU2 = p.tile("ff_wgu2", [128, 2 * 8 * 512], BF16, es_e)
    SIL = p.tile("ff_sil", [128, 4 * TG], F32, es_e)
    HP = p.tile("ff_hp", [128, 2 * 4 * TG], BF16, es_e)

    def wgu_src(e):
        if e == 0:
            return [dr["L0_ffn_shared_wg"].rearrange("k p m -> p k m"),
                    dr["L0_ffn_shared_wu"].rearrange("k p m -> p k m")]
        return [dr["L0_ffn_experts_wg"][e - 1].rearrange("k p m -> p k m"),
                dr["L0_ffn_experts_wu"][e - 1].rearrange("k p m -> p k m")]

    fence_e = p.snap()

    @blk.gpsimd
    def _(g):
        p.wait_snap(g, fence_e)
        g.wait_ge(p.ds.h, ds_cwd)
        for i, srcap in enumerate(wgu_src(0)):
            instr = g.dma_start(
                out=WGU[:, i * 4096:i * 4096 + 4096].rearrange(
                    "p (k m) -> p k m", m=512),
                in_=srcap)
            p.dg.inc(instr, 16)

    @blk.sync
    def _(sync):
        p.wait_snap(sync, fence_e)
        instr = sync.dma_start(out=CWB[:, 0:T],
                               in_=dr["cwd"][0:1, :].broadcast_to([128, T]))
        p.ds.inc(instr, 16)

    hd_marks = []
    with contextlib.ExitStack() as pse:
        ps_h = [pse.enter_context(p.nc.psum_tensor(f"ps_ffh{i}", [128, 512], F32))
                for i in range(8)]

        for e in range(17):
            ld_done = p.dg.n
            cw_done = p.ds.n
            wbuf = (WGU, WGU2)[e % 2]
            nbuf = (WGU, WGU2)[(e + 1) % 2]

            if e < 16:
                @blk.gpsimd
                def _(g, e=e, nbuf=nbuf):
                    g.wait_ge(p.pe.h, p.pe.n)
                    for i, srcap in enumerate(wgu_src(e + 1)):
                        instr = g.dma_start(
                            out=nbuf[:, i * 4096:i * 4096 + 4096].rearrange(
                                "p (k m) -> p k m", m=512),
                            in_=srcap)
                        p.dg.inc(instr, 16)

                @blk.sync
                def _(sync, e=e, dve_now=None):
                    sync.wait_ge(p.dve.h, p.dve.n)
                    instr = sync.dma_start(
                        out=CWB[:, (e % 2) * T:(e % 2) * T + T],
                        in_=dr["cwd"][e:e + 1, :].broadcast_to([128, T]))
                    p.ds.inc(instr, 16)

            for tg in range(NTG):
                d_hp0 = p.dve.n

                @blk.tensor
                def _(t, e=e, tg=tg, wbuf=wbuf, ld_done=ld_done, d_hp0=d_hp0):
                    t.wait_ge(p.dg.h, ld_done)
                    t.wait_ge(p.dve.h, d_hp0)
                    t.wait_ge(p.act.h, p.act.n)
                    for half in range(2):
                        for m in range(4):
                            for k in range(NCH):
                                instr = t.matmul(
                                    ps_h[half * 4 + m][:, 0:TG],
                                    wbuf[:, half * 4096 + k * 512 + m * 128:
                                         half * 4096 + k * 512 + m * 128 + 128],
                                    XNBF[:, k * T + tg * TG:k * T + tg * TG + TG],
                                    start=(k == 0), stop=(k == NCH - 1))
                            if half == 1 and m == 3:
                                p.pe.inc(instr)

                pe_h = p.pe.n

                for m in range(4):
                    @blk.scalar
                    def _(s, m=m, pe_h=pe_h):
                        s.wait_ge(p.pe.h, pe_h)
                        instr = s.activation(SIL[:, m * TG:m * TG + TG],
                                             ps_h[m][:, 0:TG], AF.Silu)
                        p.act.inc(instr)

                    a_sil = p.act.n

                    @blk.vector
                    def _(v, e=e, tg=tg, m=m, a_sil=a_sil, cw_done=cw_done):
                        v.wait_ge(p.act.h, a_sil)
                        if m == 0 and len(hd_marks) >= 2:
                            v.wait_ge(p.ds.h, hd_marks[-2])
                        silm = SIL[:, m * TG:m * TG + TG]
                        hp = HP[:, (tg % 2) * 2048 + m * TG:
                                (tg % 2) * 2048 + m * TG + TG]
                        if e == 0:
                            instr = v.tensor_tensor(hp, silm, ps_h[4 + m][:, 0:TG],
                                                    ALU.mult)
                        else:
                            v.wait_ge(p.ds.h, cw_done)
                            v.tensor_tensor(silm, silm, ps_h[4 + m][:, 0:TG],
                                            ALU.mult)
                            instr = v.tensor_tensor(
                                hp, silm,
                                CWB[:, ((e - 1) % 2) * T + tg * TG:
                                    ((e - 1) % 2) * T + tg * TG + TG], ALU.mult)
                        p.dve.inc(instr)

                d_hp = p.dve.n

                @blk.sync
                def _(sync, e=e, tg=tg, d_hp=d_hp):
                    sync.wait_ge(p.dve.h, d_hp)
                    instr = sync.dma_start(
                        out=dr["hd"][4 * e:4 * e + 4, :, tg * TG:tg * TG + TG]
                        .rearrange("m p t -> p m t"),
                        in_=HP[:, (tg % 2) * 2048:(tg % 2) * 2048 + 2048]
                        .rearrange("p (m t) -> p m t", m=4))
                    p.ds.inc(instr, 16)
                hd_marks.append(p.ds.n)

    hd_done = p.ds.n
    es_e.close()
    if pre_d is not None:
        pre_d()

    def wd_src(kc, mh):
        e, sub = kc // 4, kc % 4
        base = (dr["L0_ffn_shared_wd"] if e == 0
                else dr["L0_ffn_experts_wd"][e - 1])
        return base[sub, :, mh * 512:mh * 512 + 512]

    es_d = contextlib.ExitStack()
    HB = p.tile("ff_hb", [128, 8 * TG], BF16, es_d)
    WD = p.tile("ff_wd", [128, NKC * 512], BF16, es_d)
    with contextlib.ExitStack() as pse:
        ps_y = [pse.enter_context(p.nc.psum_tensor(f"ps_ffy{i}", [128, 512], F32))
                for i in range(4)]
        for mh in range(2):
            @blk.gpsimd
            def _(g, mh=mh):
                g.wait_ge(p.pe.h, p.pe.n)
                g.wait_ge(p.dve.h, p.dve.n)
                g.wait_ge(p.act.h, p.act.n)
                g.wait_ge(p.ds.h, hd_done)
                for kc in range(NKC):
                    instr = g.dma_start(out=WD[:, kc * 512:kc * 512 + 512],
                                        in_=wd_src(kc, mh))
                    p.dg.inc(instr, 16)

            wd_ld = p.dg.n
            for tg in range(NTG):
                ds_h0 = p.ds.n

                pe_y0 = p.pe.n

                @blk.sync
                def _(sync, tg=tg, ds_h0=ds_h0, mh=mh, pe_y0=pe_y0):
                    sync.wait_ge(p.ds.h, hd_done)
                    sync.wait_ge(p.dve.h, p.dve.n)
                    for kc in range(NKC):
                        if kc >= 8:
                            sync.wait_ge(p.pe.h, pe_y0 + kc // 4 - 1)
                        instr = sync.dma_start(
                            out=HB[:, (kc % 8) * TG:(kc % 8) * TG + TG],
                            in_=dr["hd"][kc, :, tg * TG:tg * TG + TG])
                        p.ds.inc(instr, 16)

                @blk.tensor
                def _(t, tg=tg, mh=mh, wd_ld=wd_ld, ds_h0=ds_h0, pe_y0=pe_y0,
                      d_prev=None):
                    t.wait_ge(p.dg.h, wd_ld)
                    t.wait_ge(p.dve.h, p.dve.n)
                    for kc in range(NKC):
                        t.wait_ge(p.ds.h, ds_h0 + 16 * (kc + 1))
                        for m in range(4):
                            instr = t.matmul(
                                ps_y[m][:, 0:TG],
                                WD[:, kc * 512 + m * 128:kc * 512 + m * 128 + 128],
                                HB[:, (kc % 8) * TG:(kc % 8) * TG + TG],
                                start=(kc == 0), stop=(kc == NKC - 1))
                        if kc % 4 == 3 or kc == NKC - 1:
                            p.pe.inc(instr)

                pe_y = p.pe.n

                @blk.vector
                def _(v, tg=tg, mh=mh, pe_y=pe_y):
                    v.wait_ge(p.pe.h, pe_y)
                    XOUT = xo_get()
                    for m in range(4):
                        mc = mh * 4 + m
                        sl = slice(mc * T + tg * TG, mc * T + tg * TG + TG)
                        instr = v.tensor_tensor(XOUT[:, sl], XOUT[:, sl],
                                                ps_y[m][:, 0:TG], ALU.add)
                        p.dve.inc(instr)

    es_d.close()
    return dict(done=True)



def emit_l0attn_layer(p, INNER, XBF, lname, S, GBl, phases=None):
    PH = set(phases) if phases else {"scan", "oproj", "ln1", "ffn1",
                                     "ffn2", "ln2"}
    if "scan" in PH:
        PH |= {"qkv", "chunks", "c_at", "c_num", "c_den", "c_afm"}
    if "chunks" in PH:
        pass
    blk = p.blk
    C = p.C
    dr = p.dram
    U = p.uid()
    es_a = contextlib.ExitStack()
    PHI_Q = p.tile(f"la_q{U}", [128, 2 * T], BF16, es_a)
    PHI_K = p.tile(f"la_k{U}", [128, 2 * T], BF16, es_a)
    VV = p.tile(f"la_v{U}", [128, 2 * T], BF16, es_a)
    ATT = p.tile(f"la_att{U}", [128, NCH * T], BF16, es_a)
    WQKV = p.tile(f"la_wqkv{U}", [128, 2 * 3 * 1024], BF16, es_a)
    WO = p.tile(f"la_wo{U}", [128, 2 * 1024], BF16, es_a)
    BIAS = p.tile(f"la_bias{U}", [128, 4 * NCH], F32, es_a)
    KV_S = p.tile(f"la_kvs{U}", [128, 65], F32, es_a)
    KV_B = p.tile(f"la_kvb{U}", [128, 65], BF16, es_a)
    VA = p.tile(f"la_va{U}", [128, 130], BF16, es_a)
    KTM = p.tile(f"la_ktm{U}", [128, 128], BF16, es_a)
    AM = p.tile(f"la_am{U}", [128, 256], BF16, es_a)
    ATMP = p.tile(f"la_atmp{U}", [128, 128], BF16, es_a)
    DH_ = p.tile(f"la_den{U}", [128, 8], F32, es_a)
    T0 = WO[:, 0:TG]
    T1 = WO[:, TG:2 * TG]
    T2 = WO[:, 2 * TG:3 * TG]

    start = p.snap()
    zb = C["zeros"][:, 0:1]

    @blk.sync
    def _(sync):
        p.wait_snap(sync, start)
        for i, nmb in enumerate(("bq", "bk", "bv", "bo")):
            instr = sync.dma_start(out=BIAS[:, i * NCH:(i + 1) * NCH],
                                   in_=dr[f"{lname}_{nmb}"])
            p.ds.inc(instr, 16)

    ld_bias = p.ds.n

    @blk.vector
    def _(v):
        p.wait_snap(v, start)
        v.memset(VA[:, 64:65], 1.0)
        v.memset(VA[:, 129:130], 1.0)
        v.memset(KV_S[:], 0.0)
        instr = v.memset(KV_B[:], 0.0)
        p.dve.inc(instr)

    d_init = p.dve.n

    with contextlib.ExitStack() as pse:
        ps_a = [pse.enter_context(p.nc.psum_tensor(f"ps_laa{i}_{U}",
                                                   [128, 512], F32))
                for i in range(2)]
        ps_tr = pse.enter_context(p.nc.psum_tensor(f"ps_latr{U}",
                                                   [128, 512], F32))
        ps_at = pse.enter_context(p.nc.psum_tensor(f"ps_laat{U}",
                                                   [128, 512], F32))
        ps_num = pse.enter_context(p.nc.psum_tensor(f"ps_lanum{U}",
                                                    [128, 512], F32))
        ps_kvu = pse.enter_context(p.nc.psum_tensor(f"ps_lakvu{U}",
                                                    [128, 512], F32))
        ps_at2 = pse.enter_context(p.nc.psum_tensor(f"ps_laat2{U}",
                                                    [128, 512], F32))

        qkv_pe_prev = [p.pe.n]
        qkv_d_prev = [p.dve.n]
        qkv_a_prev = [p.act.n]
        t_aer = [p.act.n]
        t_mrg = [p.dve.n]

        def emit_qkv_steps(pp):
            @blk.gpsimd
            def _(g, pp=pp, pe_prev=qkv_pe_prev[0]):
                if pp == 0:
                    p.wait_snap(g, start)
                g.wait_ge(p.pe.h, pe_prev)
                for i, wn in enumerate(("wq", "wk", "wv")):
                    instr = g.dma_start(
                        out=WQKV[:, (pp % 2) * 3072 + i * 1024:
                               (pp % 2) * 3072 + (i + 1) * 1024]
                        .rearrange("p (k m) -> p k m", m=128),
                        in_=dr[f"{lname}_{wn}"].rearrange("k p m -> p k m")
                        [:, :, pp * 128:(pp + 1) * 128])
                    p.dg.inc(instr, 16)

            ld_w = p.dg.n
            d_ev0 = qkv_d_prev[0]
            a_ev0 = qkv_a_prev[0]
            evac_marks = []

            def step(nn, pp=pp, ld_w=ld_w, d_ev0=d_ev0, a_ev0=a_ev0,
                     evac_marks=evac_marks):
                i, tg = nn // NTG, nn % NTG
                tsl = slice((pp % 2) * T + tg * TG,
                            (pp % 2) * T + tg * TG + TG)

                @blk.tensor
                def _(t, pp=pp, i=i, tg=tg, nn=nn):
                    if nn == 0:
                        t.wait_ge(p.dg.h, ld_w)
                        t.wait_ge(p.dve.h, d_ev0)
                        t.wait_ge(p.act.h, a_ev0)
                    if nn >= 2:
                        t.wait_ge(p.dve.h, evac_marks[nn - 2])
                    for k in range(NCH):
                        instr = t.matmul(
                            ps_a[nn % 2][:, 0:TG],
                            WQKV[:, (pp % 2) * 3072 + i * 1024 + k * 128:
                                 (pp % 2) * 3072 + i * 1024 + k * 128 + 128],
                            XBF[:, k * T + tg * TG:k * T + tg * TG + TG],
                            start=(k == 0), stop=(k == NCH - 1))
                    p.pe.inc(instr)

                pe_mm = p.pe.n
                if i < 2:
                    dst = (PHI_Q, PHI_K)[i]

                    @blk.vector
                    def _(v, pp=pp, i=i, nn=nn, pe_mm=pe_mm,
                          aer=t_aer[0]):
                        v.wait_ge(p.pe.h, pe_mm)
                        v.wait_ge(p.ds.h, ld_bias)
                        v.wait_ge(p.act.h, aer)
                        instr = v.scalar_tensor_tensor(
                            T0, ps_a[nn % 2][:, 0:TG],
                            BIAS[:, i * NCH + pp:i * NCH + pp + 1],
                            zb.broadcast_to([128, TG]), ALU.add, ALU.min)
                        p.dve.inc(instr)

                    d_t0 = p.dve.n

                    @blk.scalar
                    def _(s, nn=nn, d_t0=d_t0, pp=pp, i=i, mrg=t_mrg[0]):
                        s.wait_ge(p.dve.h, d_t0)
                        s.wait_ge(p.dve.h, mrg)
                        s.activation(T1, T0, AF.Exp)
                        instr = s.activation(
                            T2, ps_a[nn % 2][:, 0:TG], AF.Relu,
                            bias=BIAS[:, i * NCH + pp:i * NCH + pp + 1])
                        p.act.inc(instr)

                    a_er = p.act.n
                    t_aer[0] = a_er

                    @blk.vector
                    def _(v, dst=dst, tsl=tsl, nn=nn, a_er=a_er):
                        v.wait_ge(p.act.h, a_er)
                        instr = v.tensor_tensor(
                            dst[:, tsl], T2, T1, ALU.add)
                        p.dve.inc(instr)

                    t_mrg[0] = p.dve.n
                else:
                    @blk.vector
                    def _(v, pp=pp, nn=nn, tsl=tsl, pe_mm=pe_mm):
                        v.wait_ge(p.pe.h, pe_mm)
                        v.wait_ge(p.ds.h, ld_bias)
                        instr = v.scalar_tensor_tensor(
                            VV[:, tsl], ps_a[nn % 2][:, 0:TG],
                            BIAS[:, 2 * NCH + pp:2 * NCH + pp + 1],
                            zb.broadcast_to([128, TG]), ALU.add, ALU.add)
                        p.dve.inc(instr)
                evac_marks.append(p.dve.n)

            return [(lambda nn=nn: step(nn)) for nn in range(12)]

        def finish_qkv():
            qkv_pe_prev[0] = p.pe.n
            qkv_d_prev[0] = p.dve.n
            qkv_a_prev[0] = p.act.n

            @blk.vector
            def _(v):
                v.wait_ge(p.pe.h, p.pe.n)
                v.memset(KV_S[:], 0.0)
                instr = v.memset(KV_B[:], 0.0)
                p.dve.inc(instr)

        def emit_scan(pp, inter):
            for ni, n in enumerate(reversed(range(16)) if "chunks" in PH
                                   else []):
                csl = slice((pp % 2) * T + n * 128,
                            (pp % 2) * T + n * 128 + 128)
                d_pre = p.dve.n

                @blk.tensor
                def _(t, pp=pp, n=n, csl=csl, d_pre=d_pre):
                    t.wait_ge(p.dve.h, d_pre)
                    trv = ps_tr[:, 0:512].bitcast(BF16)
                    t.matmul(trv[:, 0:128], PHI_K[:, csl], C["ident"][:],
                             is_transpose=True, start=True, stop=True)
                    instr = t.matmul(trv[:, 128:256], VV[:, csl],
                                     C["ident"][:], is_transpose=True,
                                     start=True, stop=True)
                    p.pe.inc(instr)

                pe_tr = p.pe.n

                @blk.vector
                def _(v, pe_tr=pe_tr):
                    v.wait_ge(p.pe.h, pe_tr)
                    trv = ps_tr[:, 0:512].bitcast(BF16)
                    v.tensor_copy(KTM[:, 0:128], trv[:, 0:128])
                    v.tensor_copy(VA[:, 0:64], trv[:, 128:192])
                    instr = v.tensor_copy(VA[:, 65:129], trv[:, 192:256])
                    p.dve.inc(instr)

                d_tr = p.dve.n
                if "c_at" not in PH:
                    continue

                @blk.tensor
                def _(t, pp=pp, csl=csl, d_tr=d_tr):
                    t.wait_ge(p.dve.h, d_tr)
                    t.matmul(ps_at[:, 0:128], PHI_K[0:64, csl],
                             PHI_Q[0:64, csl], start=True, stop=True)
                    instr = t.matmul(ps_at2[:, 0:128], PHI_K[64:128, csl],
                                     PHI_Q[64:128, csl], start=True, stop=True)
                    p.pe.inc(instr)

                pe_at = p.pe.n

                @blk.vector
                def _(v, pe_at=pe_at):
                    v.wait_ge(p.pe.h, pe_at)
                    v.tensor_tensor(AM[:, 0:128], ps_at[:, 0:128],
                                    C["trilx4"][:, 0:128], ALU.mult)
                    instr = v.tensor_tensor(AM[:, 128:256], ps_at2[:, 0:128],
                                            C["trilx4"][:, 128:256], ALU.mult)
                    p.dve.inc(instr)

                d_am = p.dve.n
                if "c_num" not in PH:
                    continue

                @blk.tensor
                def _(t, pp=pp, csl=csl, d_am=d_am):
                    t.wait_ge(p.dve.h, d_am)
                    for par in range(2):
                        rows = slice(0, 64) if par == 0 else slice(64, 128)
                        t.matmul(ps_num[:, par * 65:par * 65 + 65],
                                 AM[:, par * 128:par * 128 + 128],
                                 VA[:, par * 65:par * 65 + 65],
                                 start=True, stop=False)
                        instr = t.matmul(ps_num[:, par * 65:par * 65 + 65],
                                         PHI_Q[rows, csl], KV_B[rows, 0:65],
                                         start=False, stop=True)
                    for par in range(2):
                        rows = slice(0, 64) if par == 0 else slice(64, 128)
                        instr2 = t.matmul(ps_kvu[rows, 0:65],
                                          KTM[:, par * 64:par * 64 + 64],
                                          VA[:, par * 65:par * 65 + 65],
                                          start=True, stop=True)
                    p.pe.inc(instr)
                    p.pe.inc(instr2)

                pe_num = p.pe.n
                if "c_den" not in PH:
                    continue

                @blk.vector
                def _(v, pe_num=pe_num):
                    v.wait_ge(p.pe.h, pe_num)
                    v.tensor_single_scalar(DH_[:, 0:1], ps_num[:, 64:65],
                                           ATT_EPS, ALU.add)
                    instr = v.tensor_single_scalar(DH_[:, 1:2],
                                                   ps_num[:, 129:130],
                                                   ATT_EPS, ALU.add)
                    p.dve.inc(instr)
                    v.wait_ge(p.dve.h, p.dve.n)
                    p.dve.inc(v.reciprocal(DH_[:, 2:4], DH_[:, 0:2]))
                    v.wait_ge(p.dve.h, p.dve.n)
                    for par in range(2):
                        v.scalar_tensor_tensor(
                            ATMP[:, par * 64:par * 64 + 64],
                            ps_num[:, par * 65:par * 65 + 64],
                            DH_[:, 2 + par:3 + par],
                            C["zeros"][:, 0:1].broadcast_to([128, 64]),
                            ALU.mult, ALU.add)
                    v.tensor_tensor(KV_S[:, 0:65], KV_S[:, 0:65],
                                    ps_kvu[:, 0:65], ALU.add)
                    instr = v.tensor_copy(KV_B[:, 0:65], KV_S[:, 0:65])
                    p.dve.inc(instr)

                d_scan = p.dve.n
                if "c_afm" not in PH:
                    continue

                @blk.tensor
                def _(t, pp=pp, n=n, d_scan=d_scan):
                    t.wait_ge(p.dve.h, d_scan)
                    instr = t.matmul(
                        ps_tr[:, 0:512].bitcast(BF16)[:, 256:384],
                        ATMP[:, 0:128], C["ident"][:, 0:128],
                        is_transpose=True, start=True, stop=True)
                    p.pe.inc(instr)

                pe_afm = p.pe.n

                @blk.vector
                def _(v, pp=pp, n=n, pe_afm=pe_afm):
                    v.wait_ge(p.pe.h, pe_afm)
                    instr = v.tensor_copy(
                        ATT[:, pp * T + n * 128:pp * T + n * 128 + 128],
                        ps_tr[:, 0:512].bitcast(BF16)[:, 256:384])
                    p.dve.inc(instr)

                if inter:
                    inter.pop(0)()

        if "scan" in PH or "qkv" in PH:
            steps = emit_qkv_steps(0)
            for st in steps:
                st()
            steps = []
            finish_qkv()
            for pp in range(1, 9):
                nxt = emit_qkv_steps(pp) if pp < 8 else []
                emit_scan(pp - 1, nxt)
                for st in nxt:
                    st()
                if pp < 8:
                    finish_qkv()
        if "oproj" not in PH:
            es_a.close()
            return
        d_att = p.dve.n
        if "dbg_latt" in dr:
            @blk.sync
            def _(sync):
                sync.wait_ge(p.dve.h, d_att)
                p.ds.inc(sync.dma_start(out=dr["dbg_latt"], in_=ATT[:]), 16)
        pe_o0 = p.pe.n
        ld_wo0 = p.dg.n

        @blk.gpsimd
        def _(g):
            g.wait_ge(p.dve.h, d_att)
            for m in range(NCH):
                if m >= 2:
                    g.wait_ge(p.pe.h, pe_o0 + 4 * (m - 1))
                instr = g.dma_start(
                    out=WO[:, (m % 2) * 1024:(m % 2) * 1024 + 1024]
                    .rearrange("p (k c) -> p k c", c=128),
                    in_=dr[f"{lname}_wo"].rearrange("k p m -> p k m")
                    [:, :, m * 128:(m + 1) * 128])
                p.dg.inc(instr, 16)

        @blk.tensor
        def _(t):
            t.wait_ge(p.dve.h, d_att)
            n = 0
            for m in range(NCH):
                t.wait_ge(p.dg.h, ld_wo0 + 16 * (m + 1))
                for tg in range(NTG):
                    if n >= 2:
                        t.wait_ge(p.dve.h, d_att + n - 1)
                    for k in range(NCH):
                        instr = t.matmul(
                            ps_a[n % 2][:, 0:TG],
                            WO[:, (m % 2) * 1024 + k * 128:
                               (m % 2) * 1024 + k * 128 + 128],
                            ATT[:, k * T + tg * TG:k * T + tg * TG + TG],
                            start=(k == 0), stop=(k == NCH - 1))
                    p.pe.inc(instr)
                    n += 1

        @blk.vector
        def _(v):
            n = 0
            for m in range(NCH):
                for tg in range(NTG):
                    v.wait_ge(p.pe.h, pe_o0 + n + 1)
                    sl = slice(m * T + tg * TG, m * T + tg * TG + TG)
                    instr = v.scalar_tensor_tensor(
                        INNER[:, sl], ps_a[n % 2][:, 0:TG],
                        BIAS[:, 3 * NCH + m:3 * NCH + m + 1],
                        INNER[:, sl], ALU.add, ALU.add)
                    p.dve.inc(instr)
                    n += 1

    es_a.close()
    if "dbg_xa" in dr:
        emit_spill(p, INNER, "dbg_xa")

    if "ln1" not in PH:
        return
    emit_ln_params(p, S, INNER, INNER, XBF, f"{lname}_g1", f"{lname}_be1", GBl)

    if "ffn1" not in PH:
        return
    es_f = contextlib.ExitStack()
    W1B = p.tile(f"la_w1{U}", [128, 2 * 8 * 128], BF16, es_f)
    B1C = p.tile(f"la_b1{U}", [128, 32], F32, es_f)
    HROW = p.tile(f"la_hrow{U}", [128, 2 * T], BF16, es_f)

    startf = p.snap()

    @blk.sync
    def _(sync):
        p.wait_snap(sync, startf)
        instr = sync.dma_start(out=B1C[:], in_=dr[f"{lname}_b1"])
        p.ds.inc(instr, 16)

    ld_b1 = p.ds.n
    ld_w10 = p.dg.n
    hd_marks = []
    pe_f0 = p.pe.n

    @blk.gpsimd
    def _(g):
        p.wait_snap(g, startf)
        for m in range(32):
            if m >= 2:
                g.wait_ge(p.pe.h, pe_f0 + 4 * (m - 1))
            instr = g.dma_start(
                out=W1B[:, (m % 2) * 1024:(m % 2) * 1024 + 1024]
                .rearrange("p (k c) -> p k c", c=128),
                in_=dr[f"{lname}_w1"].rearrange("k p m -> p k m")
                [:, :, m * 128:(m + 1) * 128])
            p.dg.inc(instr, 16)
    with contextlib.ExitStack() as pse:
        ps_f = [pse.enter_context(p.nc.psum_tensor(f"ps_laf{i}_{U}",
                                                   [128, 512], F32))
                for i in range(2)]
        a_f0 = p.act.n

        @blk.tensor
        def _(t):
            n = 0
            for m in range(32):
                t.wait_ge(p.dg.h, ld_w10 + 16 * (m + 1))
                for tg in range(NTG):
                    if n >= 2:
                        t.wait_ge(p.act.h, a_f0 + n - 1)
                    for k in range(NCH):
                        instr = t.matmul(
                            ps_f[n % 2][:, 0:TG],
                            W1B[:, (m % 2) * 1024 + k * 128:
                                (m % 2) * 1024 + k * 128 + 128],
                            XBF[:, k * T + tg * TG:k * T + tg * TG + TG],
                            start=(k == 0), stop=(k == NCH - 1))
                    p.pe.inc(instr)
                    n += 1

        pe_fq0 = pe_f0

        @blk.scalar
        def _(s):
            n = 0
            for m in range(32):
                for tg in range(NTG):
                    s.wait_ge(p.pe.h, pe_fq0 + n + 1)
                    s.wait_ge(p.ds.h, ld_b1)
                    if tg == 0 and len(hd_marks) >= 2:
                        s.wait_ge(p.ds.h, hd_marks[-2])
                    instr = s.activation(
                        HROW[:, (m % 2) * T + tg * TG:(m % 2) * T + tg * TG + TG],
                        ps_f[n % 2][:, 0:TG], AF.Gelu,
                        bias=B1C[:, m:m + 1])
                    p.act.inc(instr)
                    n += 1

        for m in range(32):
            a_need = a_f0 + 4 * (m + 1)

            @blk.sync
            def _(sync, m=m, a_need=a_need):
                sync.wait_ge(p.act.h, a_need)
                instr = sync.dma_start(
                    out=dr["hd"][m, :, :],
                    in_=HROW[:, (m % 2) * T:(m % 2) * T + T])
                p.ds.inc(instr, 16)
            hd_marks.append(p.ds.n)

    hd_done = p.ds.n
    es_f.close()

    if "ffn2" not in PH:
        return
    es_d = contextlib.ExitStack()
    HB = p.tile(f"la_hb{U}", [128, 8 * TG], BF16, es_d)
    WD = p.tile(f"la_wd{U}", [128, 32 * 512], BF16, es_d)
    B2C = p.tile(f"la_b2{U}", [128, NCH], F32, es_d)

    @blk.sync
    def _(sync):
        instr = sync.dma_start(out=B2C[:], in_=dr[f"{lname}_b2"])
        p.ds.inc(instr, 16)

    ld_b2 = p.ds.n

    with contextlib.ExitStack() as pse:
        ps_y = [pse.enter_context(p.nc.psum_tensor(f"ps_lay{i}_{U}",
                                                   [128, 512], F32))
                for i in range(4)]
        pe_dp0 = p.pe.n
        gl = [0]
        for mh in range(2):
            @blk.gpsimd
            def _(g, mh=mh):
                g.wait_ge(p.pe.h, p.pe.n)
                g.wait_ge(p.dve.h, p.dve.n)
                g.wait_ge(p.act.h, p.act.n)
                g.wait_ge(p.ds.h, hd_done)
                for kc in range(32):
                    instr = g.dma_start(
                        out=WD[:, kc * 512:kc * 512 + 512],
                        in_=dr[f"{lname}_w2"][kc, :, mh * 512:mh * 512 + 512])
                    p.dg.inc(instr, 16)

            wd_ld = p.dg.n
            for tg in range(NTG):
                @blk.sync
                def _(sync, tg=tg, mh=mh):
                    sync.wait_ge(p.ds.h, hd_done)
                    for kc in range(32):
                        L = gl[0]
                        if L >= 8:
                            sync.wait_ge(p.pe.h, pe_dp0 + L // 4 - 1)
                        instr = sync.dma_start(
                            out=HB[:, (kc % 8) * TG:(kc % 8) * TG + TG],
                            in_=dr["hd"][kc, :, tg * TG:tg * TG + TG])
                        p.ds.inc(instr, 16)
                        gl[0] += 1

                ds_h0 = p.ds.n - 16 * 32

                @blk.tensor
                def _(t, tg=tg, mh=mh, wd_ld=wd_ld, ds_h0=ds_h0):
                    t.wait_ge(p.dg.h, wd_ld)
                    t.wait_ge(p.dve.h, p.dve.n)
                    for kc in range(32):
                        t.wait_ge(p.ds.h, ds_h0 + 16 * (kc + 1))
                        for m in range(4):
                            instr = t.matmul(
                                ps_y[m][:, 0:TG],
                                WD[:, kc * 512 + m * 128:kc * 512 + m * 128 + 128],
                                HB[:, (kc % 8) * TG:(kc % 8) * TG + TG],
                                start=(kc == 0), stop=(kc == 31))
                        if kc % 4 == 3:
                            p.pe.inc(instr)

                pe_y = p.pe.n

                @blk.vector
                def _(v, tg=tg, mh=mh, pe_y=pe_y):
                    v.wait_ge(p.pe.h, pe_y)
                    v.wait_ge(p.ds.h, ld_b2)
                    for m in range(4):
                        mc = mh * 4 + m
                        sl = slice(mc * T + tg * TG, mc * T + tg * TG + TG)
                        instr = v.scalar_tensor_tensor(
                            INNER[:, sl], ps_y[m][:, 0:TG],
                            B2C[:, mc:mc + 1], INNER[:, sl],
                            ALU.add, ALU.add)
                        p.dve.inc(instr)
    es_d.close()

    if "ln2" not in PH:
        return
    emit_ln_params(p, S, INNER, INNER, XBF, f"{lname}_g2", f"{lname}_be2", GBl)


def emit_l0attn(p, INNER, XBF, GB):
    for l in range(6):
        with contextlib.ExitStack() as es_s:
            S = LnScratch(p, f"l0a{l}", es_s)
            emit_l0attn_layer(p, INNER, XBF, f"A{l}", S, GB)
    with contextlib.ExitStack() as es_s:
        S = LnScratch(p, "l0af", es_s)
        emit_ln_params(p, S, INNER, INNER, None, "A_gf", "A_bf", GB)






def emit_l1moe(p, XN, XNBF, XOUT, between=None):
    blk = p.blk
    C = p.C
    dr = p.dram
    NE = 8
    NKC = 32
    identf = C["identf"]
    zeros = C["zeros"]
    es_r = contextlib.ExitStack()
    gate_w = p.tile("mo_gatew", [128, 8 * 8], F32, es_r)
    r_fm = p.tile("mo_rfm", [8, T], F32, es_r)
    r_tm = p.tile("mo_rtm", [128, 16 * 8], F32, es_r)
    M1 = p.tile("mo_m1", [128, 16], F32, es_r)
    M2 = p.tile("mo_m2", [128, 16], F32, es_r)
    P1 = p.tile("mo_p1", [128, 16], F32, es_r)
    P2 = p.tile("mo_p2", [128, 16], F32, es_r)
    I2 = p.tile("mo_i2", [128, 8], F32, es_r)
    TMPB = p.tile("mo_tmpb", [128, 8], F32, es_r)
    CW = p.tile("mo_cw", [128, 16 * 8], F32, es_r)
    CWF = p.tile("mo_cwf", [8, T], F32, es_r)

    start = p.snap()

    @blk.sync
    def _(sync):
        p.wait_snap(sync, start)
        instr = sync.dma_start(
            out=gate_w[:].rearrange("p (k m) -> p k m", m=8),
            in_=dr["L1_ffn_Wg"].rearrange("k p m -> p k m"))
        p.ds.inc(instr, 16)

    ld_gw = p.ds.n

    with contextlib.ExitStack() as pse:
        ps_r = [pse.enter_context(p.nc.psum_tensor(f"ps_mor{i}", [128, 512], F32))
                for i in range(2)]
        ps_t = pse.enter_context(p.nc.psum_tensor("ps_mot", [128, 512], F32))
        ps_t2 = pse.enter_context(p.nc.psum_tensor("ps_mot2", [128, 512], F32))

        d_ev0 = p.dve.n

        @blk.tensor
        def _(t):
            t.wait_ge(p.ds.h, ld_gw)
            p.wait_snap(t, start)
            for tg in range(NTG):
                if tg >= 2:
                    t.wait_ge(p.dve.h, d_ev0 + tg - 1)
                for k in range(NCH):
                    instr = t.matmul(
                        ps_r[tg % 2][0:8, 0:TG], gate_w[:, k * 8:k * 8 + 8],
                        XN[:, k * T + tg * TG:k * T + tg * TG + TG],
                        start=(k == 0), stop=(k == NCH - 1))
                p.pe.inc(instr)

        pe_r0 = p.pe.n - NTG

        @blk.vector
        def _(v):
            for tg in range(NTG):
                v.wait_ge(p.pe.h, pe_r0 + tg + 1)
                instr = v.tensor_copy(r_fm[:, tg * TG:tg * TG + TG],
                                      ps_r[tg % 2][0:8, 0:TG])
                p.dve.inc(instr)

        d_r = p.dve.n

        @blk.tensor
        def _(t):
            t.wait_ge(p.dve.h, d_r)
            for tc in range(16):
                instr = t.matmul(ps_t[:, tc * 8:tc * 8 + 8],
                                 r_fm[:, tc * 128:tc * 128 + 128],
                                 identf[0:8, 0:8], is_transpose=True,
                                 start=True, stop=True)
            p.pe.inc(instr)

        pe_tr = p.pe.n

        @blk.vector
        def _(v):
            v.wait_ge(p.pe.h, pe_tr)
            instr = v.tensor_copy(r_tm[:], ps_t[:, 0:128])
            p.dve.inc(instr)

        d_rtm = p.dve.n

        @blk.vector
        def _(v):
            v.wait_ge(p.dve.h, d_rtm)
            zb = zeros[:, 0:1].broadcast_to([128, 8])

            def S(instr):
                p.dve.inc(instr)
                v.wait_ge(p.dve.h, p.dve.n)
            for tc in range(16):
                blkr = r_tm[:, tc * 8:tc * 8 + 8]
                cwb = CW[:, tc * 8:tc * 8 + 8]
                m1 = M1[:, tc:tc + 1]
                m2 = M2[:, tc:tc + 1]
                S(v.reduce_max(m1, blkr, AX.X))
                S(v.scalar_tensor_tensor(cwb, blkr, m1, zb, ALU.subtract,
                                         ALU.is_ge))
                S(v.scalar_tensor_tensor(I2[:], cwb, -1e30, blkr, ALU.mult,
                                         ALU.add))
                S(v.reduce_max(m2, I2[:], AX.X))
                S(v.scalar_tensor_tensor(I2[:], I2[:], m2, zb, ALU.subtract,
                                         ALU.is_ge))
                S(v.tensor_copy(TMPB[:], I2[:]))
                S(v.scalar_tensor_tensor(CW[:, tc * 8:tc * 8 + 8], cwb, 1.0,
                                         TMPB[:], ALU.mult, ALU.bypass)
                  if False else v.engine_nop())
                S(v.scalar_tensor_tensor(cwb, TMPB[:], 2.0, cwb, ALU.mult,
                                         ALU.add))
            instr = v.tensor_tensor(P2[:], M1[:], M2[:], ALU.subtract)
            p.dve.inc(instr)

        d_dif = p.dve.n

        @blk.scalar
        def _(s):
            s.wait_ge(p.dve.h, d_dif)
            instr = s.activation(P1[:], P2[:], AF.Sigmoid)
            p.act.inc(instr)

        a_sig = p.act.n

        @blk.vector
        def _(v):
            v.wait_ge(p.act.h, a_sig)

            def S(instr):
                p.dve.inc(instr)
                v.wait_ge(p.dve.h, p.dve.n)
            zb = zeros[:, 0:1].broadcast_to([128, 8])
            S(v.tensor_single_scalar(P2[:], P1[:], -1.0, ALU.mult))
            S(v.tensor_single_scalar(P2[:], P2[:], 1.0, ALU.add))
            for tc in range(16):
                cwb = CW[:, tc * 8:tc * 8 + 8]
                S(v.tensor_single_scalar(I2[:], cwb, 2.0, ALU.is_equal))
                S(v.tensor_single_scalar(TMPB[:], cwb, 1.0, ALU.is_equal))
                S(v.scalar_tensor_tensor(I2[:], I2[:], P2[:, tc:tc + 1], zb,
                                         ALU.mult, ALU.add))
                S(v.scalar_tensor_tensor(cwb, TMPB[:], P1[:, tc:tc + 1],
                                         I2[:], ALU.mult, ALU.add))

        d_cw = p.dve.n
        d_cwf0 = p.dve.n

        @blk.tensor
        def _(t):
            t.wait_ge(p.dve.h, d_cw)
            for tc in range(16):
                bank = (ps_t, ps_t2)[tc % 2]
                if tc >= 2:
                    t.wait_ge(p.dve.h, d_cwf0 + tc - 1)
                instr = t.matmul(bank[0:8, 0:128],
                                 CW[:, tc * 8:tc * 8 + 8], identf[:],
                                 is_transpose=True, start=True, stop=True)
                p.pe.inc(instr)

        pe_cwt0 = p.pe.n - 16

        @blk.vector
        def _(v):
            for tc in range(16):
                v.wait_ge(p.pe.h, pe_cwt0 + tc + 1)
                bank = (ps_t, ps_t2)[tc % 2]
                instr = v.tensor_copy(CWF[:, tc * 128:tc * 128 + 128],
                                      bank[0:8, 0:128])
                p.dve.inc(instr)

        d_cwf = p.dve.n

    @blk.sync
    def _(sync):
        sync.wait_ge(p.dve.h, d_cwf)
        instr = sync.dma_start(out=dr["cwd"][0:8, :], in_=CWF[:])
        p.ds.inc(instr, 16)

    ds_cwd = p.ds.n
    es_r.close()
    if between is not None:
        between()

    es_e = contextlib.ExitStack()
    CWB = p.tile("mo_cwb", [128, 2 * T], F32, es_e)
    W1B = p.tile("mo_w1b", [128, 2 * 8 * 512], BF16, es_e)
    HP = p.tile("mo_hp", [128, 2 * 4 * TG], BF16, es_e)

    def w1_src(e):
        return dr["L1_ffn_W1"][e].rearrange("k p m -> p k m")

    fence = p.snap()

    @blk.gpsimd
    def _(g):
        p.wait_snap(g, fence)
        g.wait_ge(p.ds.h, ds_cwd)
        instr = g.dma_start(
            out=W1B[:, 0:4096].rearrange("p (k m) -> p k m", m=512),
            in_=w1_src(0))
        p.dg.inc(instr, 16)

    @blk.sync
    def _(sync):
        p.wait_snap(sync, fence)
        instr = sync.dma_start(out=CWB[:, 0:T],
                               in_=dr["cwd"][0:1, :].broadcast_to([128, T]))
        p.ds.inc(instr, 16)

    hd_marks = []
    with contextlib.ExitStack() as pse:
        ps_h = [pse.enter_context(p.nc.psum_tensor(f"ps_moh{i}", [128, 512], F32))
                for i in range(8)]
        for e in range(NE):
            ld_done = p.dg.n
            cw_done = p.ds.n
            wbuf = slice((e % 2) * 4096, (e % 2) * 4096 + 4096)
            if e < NE - 1:
                @blk.gpsimd
                def _(g, e=e):
                    g.wait_ge(p.pe.h, p.pe.n)
                    instr = g.dma_start(
                        out=W1B[:, ((e + 1) % 2) * 4096:
                               ((e + 1) % 2) * 4096 + 4096]
                        .rearrange("p (k m) -> p k m", m=512),
                        in_=w1_src(e + 1))
                    p.dg.inc(instr, 16)

                @blk.sync
                def _(sync, e=e):
                    sync.wait_ge(p.dve.h, p.dve.n)
                    instr = sync.dma_start(
                        out=CWB[:, ((e + 1) % 2) * T:((e + 1) % 2) * T + T],
                        in_=dr["cwd"][e + 1:e + 2, :].broadcast_to([128, T]))
                    p.ds.inc(instr, 16)

            for tg in range(NTG):
                d_hp0 = p.dve.n

                @blk.tensor
                def _(t, e=e, tg=tg, wbuf=wbuf, ld_done=ld_done, d_hp0=d_hp0,
                      a_now=None):
                    t.wait_ge(p.dg.h, ld_done)
                    t.wait_ge(p.dve.h, d_hp0)
                    t.wait_ge(p.act.h, p.act.n)
                    for m in range(4):
                        for k in range(NCH):
                            instr = t.matmul(
                                ps_h[(tg % 2) * 4 + m][:, 0:TG],
                                W1B[:, wbuf.start + k * 512 + m * 128:
                                   wbuf.start + k * 512 + m * 128 + 128],
                                XNBF[:, k * T + tg * TG:k * T + tg * TG + TG],
                                start=(k == 0), stop=(k == NCH - 1))
                        if m == 3:
                            p.pe.inc(instr)

                pe_h = p.pe.n

                for m in range(4):
                    @blk.scalar
                    def _(s, tg=tg, m=m, pe_h=pe_h):
                        s.wait_ge(p.pe.h, pe_h)
                        if len(hd_marks) >= 2 and m == 0:
                            s.wait_ge(p.ds.h, hd_marks[-2])
                        instr = s.activation(
                            HP[:, (tg % 2) * 2048 + m * TG:
                               (tg % 2) * 2048 + m * TG + TG],
                            ps_h[(tg % 2) * 4 + m][:, 0:TG], AF.Relu)
                        p.act.inc(instr)

                a_relu = p.act.n

                @blk.vector
                def _(v, e=e, tg=tg, a_relu=a_relu, cw_done=cw_done):
                    v.wait_ge(p.act.h, a_relu)
                    v.wait_ge(p.ds.h, cw_done)
                    for m in range(4):
                        hp = HP[:, (tg % 2) * 2048 + m * TG:
                                (tg % 2) * 2048 + m * TG + TG]
                        instr = v.tensor_tensor(
                            hp, hp,
                            CWB[:, (e % 2) * T + tg * TG:
                                (e % 2) * T + tg * TG + TG], ALU.mult)
                        p.dve.inc(instr)

                d_hp = p.dve.n

                @blk.sync
                def _(sync, e=e, tg=tg, d_hp=d_hp):
                    sync.wait_ge(p.dve.h, d_hp)
                    instr = sync.dma_start(
                        out=dr["hd"][4 * e:4 * e + 4, :, tg * TG:tg * TG + TG]
                        .rearrange("m p t -> p m t"),
                        in_=HP[:, (tg % 2) * 2048:(tg % 2) * 2048 + 2048]
                        .rearrange("p (m t) -> p m t", m=4))
                    p.ds.inc(instr, 16)
                hd_marks.append(p.ds.n)

    hd_done = p.ds.n
    es_e.close()

    def w2_src(kc, mh):
        e, sub = kc // 4, kc % 4
        return dr["L1_ffn_W2"][e][sub, :, mh * 512:mh * 512 + 512]

    es_d = contextlib.ExitStack()
    HB = p.tile("mo_hb", [128, 8 * TG], BF16, es_d)
    WD = p.tile("mo_wd", [128, NKC * 512], BF16, es_d)
    with contextlib.ExitStack() as pse:
        ps_y = [pse.enter_context(p.nc.psum_tensor(f"ps_moy{i}", [128, 512], F32))
                for i in range(4)]
        pe_dp0 = p.pe.n
        gl = [0]
        for mh in range(2):
            @blk.gpsimd
            def _(g, mh=mh):
                g.wait_ge(p.pe.h, p.pe.n)
                g.wait_ge(p.dve.h, p.dve.n)
                g.wait_ge(p.act.h, p.act.n)
                g.wait_ge(p.ds.h, hd_done)
                for kc in range(NKC):
                    instr = g.dma_start(out=WD[:, kc * 512:kc * 512 + 512],
                                        in_=w2_src(kc, mh))
                    p.dg.inc(instr, 16)

            wd_ld = p.dg.n
            for tg in range(NTG):
                pe_y0 = p.pe.n

                @blk.sync
                def _(sync, tg=tg, mh=mh, pe_y0=pe_y0):
                    sync.wait_ge(p.ds.h, hd_done)
                    for kc in range(NKC):
                        L = gl[0]
                        if L >= 8:
                            sync.wait_ge(p.pe.h, pe_dp0 + L // 4 - 1)
                        instr = sync.dma_start(
                            out=HB[:, (kc % 8) * TG:(kc % 8) * TG + TG],
                            in_=dr["hd"][kc, :, tg * TG:tg * TG + TG])
                        p.ds.inc(instr, 16)
                        gl[0] += 1

                ds_h0 = p.ds.n - 16 * NKC

                @blk.tensor
                def _(t, tg=tg, mh=mh, wd_ld=wd_ld, ds_h0=ds_h0):
                    t.wait_ge(p.dg.h, wd_ld)
                    t.wait_ge(p.dve.h, p.dve.n)
                    for kc in range(NKC):
                        t.wait_ge(p.ds.h, ds_h0 + 16 * (kc + 1))
                        for m in range(4):
                            instr = t.matmul(
                                ps_y[m][:, 0:TG],
                                WD[:, kc * 512 + m * 128:kc * 512 + m * 128 + 128],
                                HB[:, (kc % 8) * TG:(kc % 8) * TG + TG],
                                start=(kc == 0), stop=(kc == NKC - 1))
                        if kc % 4 == 3 or kc == NKC - 1:
                            p.pe.inc(instr)

                pe_y = p.pe.n

                @blk.vector
                def _(v, tg=tg, mh=mh, pe_y=pe_y):
                    v.wait_ge(p.pe.h, pe_y)
                    for m in range(4):
                        mc = mh * 4 + m
                        sl = slice(mc * T + tg * TG, mc * T + tg * TG + TG)
                        instr = v.tensor_tensor(XOUT[:, sl], XOUT[:, sl],
                                                ps_y[m][:, 0:TG], ALU.add)
                        p.dve.inc(instr)
    es_d.close()
    return dict(done=p.dve.n)



def emit_mha(p, XNBF, XOUT, wq, wk, wv, wo, bq, bk, bv, bo, scale, alibi,
             tag=""):
    blk = p.blk
    C = p.C
    dr = p.dram
    NH = 16
    DH = 64
    NQC = 16
    NKB = 4

    es_m = contextlib.ExitStack()
    QT = [p.tile(f"mh_qt{i}{tag}", [128, T], BF16, es_m) for i in range(2)]
    KT = [p.tile(f"mh_kt{i}{tag}", [128, T], BF16, es_m) for i in range(2)]
    VF = p.tile(f"mh_vf{tag}", [128, T], BF16, es_m)
    VTM = p.tile(f"mh_vtm{tag}", [128, 2 * 16 * DH], BF16, es_m)
    PSB = p.tile(f"mh_psb{tag}", [128, 4 * T], BF16, es_m)
    PTS = p.tile(f"mh_pts{tag}", [128, 16 * 512], BF16, es_m)
    ATT = p.tile(f"mh_att{tag}", [128, NCH * T], BF16, es_m)
    WQ = p.tile(f"mh_wq{tag}", [128, 3 * 1024], BF16, es_m)
    WO = p.tile(f"mh_wo{tag}", [128, 2 * 1024], BF16, es_m)
    BIAS = p.tile(f"mh_bias{tag}", [128, 4 * NCH], F32, es_m)
    SLO = p.tile(f"mh_slo{tag}", [128, 16], F32, es_m)

    DIAG = p.tile(f"mh_diag{tag}", [128, 2 * 128], BF16, es_m)
    DEN = p.tile(f"mh_den{tag}", [128, 24], F32, es_m)
    scol = p.tile(f"mh_scol{tag}", [128, 1], F32, es_m)

    start = p.snap()

    @blk.vector
    def _(v):
        p.wait_snap(v, start)
        v.memset(scol[:], scale)
        v.memset(QT[0][64:128, :], 0.0)
        v.memset(QT[0][64:65, :], 1.0)
        v.memset(QT[1][0:64, :], 0.0)
        v.memset(QT[1][0:1, :], 1.0)
        v.memset(KT[0][64:128, :], 0.0)
        instr = v.memset(KT[1][0:64, :], 0.0)
        p.dve.inc(instr)

    d_const = p.dve.n
    pl_iota = p.pool.n

    @blk.sync
    def _(sync):
        sync.wait_ge(p.dve.h, d_const)
        instr = sync.dma_start(out=QT[0][65:66, :], in_=QT[0][64:65, :])
        p.ds.inc(instr, 16)
        instr = sync.dma_start(out=QT[1][1:2, :], in_=QT[1][0:1, :])
        p.ds.inc(instr, 16)
    ds_qaug = p.ds.n

    @blk.sync
    def _(sync):
        p.wait_snap(sync, start)
        for i, nmb in enumerate((bq, bk, bv, bo)):
            if nmb is not None:
                instr = sync.dma_start(out=BIAS[:, i * NCH:(i + 1) * NCH],
                                       in_=dr[nmb])
                p.ds.inc(instr, 16)
        if alibi is not None:
            instr = sync.dma_start(out=SLO[0:1, :],
                                   in_=dr[alibi].rearrange("h one -> one h"))
            p.ds.inc(instr, 16)
            instr = sync.dma_start(out=SLO[64:65, :],
                                   in_=dr[alibi].rearrange("h one -> one h"))
            p.ds.inc(instr, 16)

    ld_bias = p.ds.n

    zb512 = C["zeros"][:, 0:1].broadcast_to([128, 512])

    with contextlib.ExitStack() as pse:
        ps_qkv = [pse.enter_context(p.nc.psum_tensor(f"ps_mhq0{tag}",
                                                     [128, 512], F32))] * 2
        ps_s = pse.enter_context(p.nc.psum_tensor(f"ps_mhs{tag}",
                                                  [128, 2048], F32))
        ps_pt = pse.enter_context(p.nc.psum_tensor(f"ps_mhpt{tag}",
                                                   [128, 512], F32))
        ps_pt2 = pse.enter_context(p.nc.psum_tensor(f"ps_mhpt2{tag}",
                                                    [128, 512], F32))
        ps_ctx = pse.enter_context(p.nc.psum_tensor(f"ps_mhctx{tag}",
                                                    [128, 512], F32))

        QCN = [0]
        qmeta = {}
        smark = {}
        nm = {}
        em = {}
        emA = {}
        dgm = {}
        trb = {}
        trslot = {}
        slot_ev = {}
        cxm = {}

        for pp in range(8):
            ld0 = p.dg.n

            @blk.gpsimd
            def _(g, pp=pp, fstart=start):
                if pp == 0:
                    p.wait_snap(g, fstart)
                g.wait_ge(p.pe.h, p.pe.n)
                for i, wn in enumerate((wq, wk, wv)):
                    instr = g.dma_start(
                        out=WQ[:, i * 1024:(i + 1) * 1024]
                        .rearrange("p (k m) -> p k m", m=128),
                        in_=dr[wn].rearrange("k p m -> p k m")
                        [:, :, pp * 128:(pp + 1) * 128])
                    p.dg.inc(instr, 16)

            ld_w = p.dg.n

            d_ev0 = p.dve.n
            pe_q0 = p.pe.n

            @blk.tensor
            def _(t, pp=pp, ld_w=ld_w, d_ev0=d_ev0):
                t.wait_ge(p.dg.h, ld_w)
                p.wait_snap(t, start)
                t.wait_ge(p.dve.h, d_ev0)
                n = 0
                for i in range(3):
                    for tg in range(NTG):
                        if n >= 1:
                            t.wait_ge(p.dve.h, d_ev0 + 2 * n)
                        for k in range(NCH):
                            instr = t.matmul(
                                ps_qkv[n % 2][:, 0:TG],
                                WQ[:, i * 1024 + k * 128:
                                   i * 1024 + k * 128 + 128],
                                XNBF[:, k * T + tg * TG:k * T + tg * TG + TG],
                                start=(k == 0), stop=(k == NCH - 1))
                        p.pe.inc(instr)
                        n += 1

            @blk.vector
            def _(v, pp=pp, pe_q0=pe_q0):
                v.wait_ge(p.ds.h, ld_bias)
                n = 0
                for i in range(3):
                    for tg in range(NTG):
                        v.wait_ge(p.pe.h, pe_q0 + n + 1)
                        ps = ps_qkv[n % 2]
                        tsl = slice(tg * TG, tg * TG + TG)
                        for par in range(2):
                            rows = slice(0, 64) if par == 0 else slice(64, 128)
                            pr = ps[rows, 0:TG]
                            if i == 0:
                                if bq is not None:
                                    instr = v.scalar_tensor_tensor(
                                        QT[par][rows, tsl], pr,
                                        BIAS[rows, 0 * NCH + pp:0 * NCH + pp + 1],
                                        scol[rows, 0:1].broadcast_to([64, TG]),
                                        ALU.add, ALU.mult)
                                else:
                                    instr = v.scalar_tensor_tensor(
                                        QT[par][rows, tsl], pr, float(scale),
                                        zb512[rows, :], ALU.mult, ALU.add)
                            elif i == 1:
                                if bk is not None:
                                    instr = v.scalar_tensor_tensor(
                                        KT[par][rows, tsl], pr,
                                        BIAS[rows, 1 * NCH + pp:1 * NCH + pp + 1],
                                        zb512[rows, :], ALU.add, ALU.add)
                                else:
                                    instr = v.tensor_copy(KT[par][rows, tsl], pr)
                            else:
                                if bv is not None:
                                    instr = v.scalar_tensor_tensor(
                                        VF[rows, tsl], pr,
                                        BIAS[rows, 2 * NCH + pp:2 * NCH + pp + 1],
                                        zb512[rows, :], ALU.add, ALU.add)
                                else:
                                    instr = v.tensor_copy(VF[rows, tsl], pr)
                            p.dve.inc(instr)
                        n += 1

            d_qkv = p.dve.n

            if alibi is not None:
                d_qkv2 = p.dve.n

                @blk.gpsimd
                def _(g, pp=pp, d_qkv2=d_qkv2):
                    g.wait_ge(p.dve.h, d_qkv2)
                    scr = PSB[:].bitcast(F32)
                    g.iota(scr[0:1, T:2 * T], [[1, T]], channel_multiplier=0,
                           allow_small_or_imprecise_dtypes=True)
                    instr = g.iota(scr[64:65, T:2 * T], [[1, T]],
                                   channel_multiplier=0,
                                   allow_small_or_imprecise_dtypes=True)
                    p.pool.inc(instr)

                pl_io = p.pool.n

                @blk.vector
                def _(v, pp=pp, pl_io=pl_io):
                    v.wait_ge(p.ds.h, ld_bias)
                    v.wait_ge(p.pool.h, pl_io)
                    v.wait_ge(p.dve.h, d_qkv)
                    scr = PSB[:].bitcast(F32)
                    for par, rr in ((0, slice(64, 65)), (1, slice(0, 1))):
                        h = 2 * pp + par
                        kt = KT[par]
                        hi_row = kt[64:65, :] if par == 0 else kt[0:1, :]
                        v.scalar_tensor_tensor(
                            scr[rr, 0:T], scr[rr, T:2 * T],
                            SLO[rr, h:h + 1],
                            C["zeros"][rr, 0:1].broadcast_to([1, T]),
                            ALU.mult, ALU.add)
                        v.tensor_copy(hi_row, scr[rr, 0:T])
                        instr = v.scalar_tensor_tensor(
                            VTM[rr, 0:T], hi_row, -1.0, scr[rr, 0:T],
                            ALU.mult, ALU.add)
                        p.dve.inc(instr)

                d_srow_v = p.dve.n

                @blk.sync
                def _(sync, pp=pp, d_srow_v=0):
                    sync.wait_ge(p.dve.h, p.dve.n)
                    instr = sync.dma_start(out=KT[0][65:66, :],
                                           in_=VTM[64:65, 0:T])
                    p.ds.inc(instr, 16)
                    instr = sync.dma_start(out=KT[1][1:2, :],
                                           in_=VTM[0:1, 0:T])
                    p.ds.inc(instr, 16)

            d_srow = p.dve.n
            ds_srow = p.ds.n

            pe_vt0 = p.pe.n
            d_vt0 = p.dve.n

            @blk.tensor
            def _(t, pp=pp, d_qkv=d_qkv, d_vt0=d_vt0):
                t.wait_ge(p.dve.h, d_qkv)
                for half in range(2):
                    if half == 1:
                        t.wait_ge(p.dve.h, d_vt0 + 1)
                    for j in range(8):
                        kc = half * 8 + j
                        instr = t.matmul(
                            ps_pt[:, 0:512].bitcast(BF16)
                            [:, j * 128:j * 128 + 128],
                            VF[:, kc * 128:kc * 128 + 128],
                            C["ident"][:],
                            is_transpose=True, start=True, stop=True)
                    p.pe.inc(instr)

            @blk.vector
            def _(v, pp=pp, pe_vt0=pe_vt0):
                v.wait_ge(p.ds.h, p.ds.n)
                for half in range(2):
                    v.wait_ge(p.pe.h, pe_vt0 + half + 1)
                    instr = v.tensor_copy(
                        VTM[:, half * 1024:half * 1024 + 1024],
                        ps_pt[:, 0:512].bitcast(BF16))
                    p.dve.inc(instr)

            d_vtm = p.dve.n
            slot_ev[0] = ("dve", d_vtm)

            def emit_S(qc, first_of_pair, d_vtm=d_vtm, ds_srow=ds_srow,
                       ds_qaug=ds_qaug):
                hh_, qg_, qi_, pp_, gg_, ql_ = qmeta[qc]
                rows_ap = slice(0, 66) if hh_ == 0 else slice(0, 128)
                pe_s0 = p.pe.n
                warA = emA.get(qc - 1, 0)
                warB = em.get(qc - 1, 0)

                @blk.tensor
                def _(t, qc=qc, hh_=hh_, ql_=ql_, rows_ap=rows_ap, warA=warA,
                      fop=first_of_pair):
                    if fop:
                        t.wait_ge(p.dve.h, d_vtm)
                        t.wait_ge(p.ds.h, ds_srow)
                        t.wait_ge(p.ds.h, ds_qaug)
                    if warA:
                        t.wait_ge(p.act.h, warA)
                    for kb in range(2):
                        instr = t.matmul(
                            ps_s[:, kb * 512:kb * 512 + 512],
                            QT[hh_][rows_ap, ql_ * 128:ql_ * 128 + 128],
                            KT[hh_][rows_ap, kb * 512:kb * 512 + 512],
                            start=True, stop=True)
                        p.pe.inc(instr)

                @blk.tensor
                def _(t, qc=qc, hh_=hh_, ql_=ql_, rows_ap=rows_ap, warB=warB):
                    if warB:
                        t.wait_ge(p.act.h, warB)
                    for kb in range(2, NKB):
                        instr = t.matmul(
                            ps_s[:, kb * 512:kb * 512 + 512],
                            QT[hh_][rows_ap, ql_ * 128:ql_ * 128 + 128],
                            KT[hh_][rows_ap, kb * 512:kb * 512 + 512],
                            start=True, stop=True)
                        p.pe.inc(instr)

                smark[qc] = [pe_s0 + kb + 1 for kb in range(NKB)]

            def emit_max(qc):
                par = qc % 2
                base = par * 12

                @blk.vector
                def _(v, qc=qc, base=base, sm=smark[qc]):
                    for kb in range(NKB):
                        v.wait_ge(p.pe.h, sm[kb])
                        p.dve.inc(v.reduce_max(DEN[:, base + kb:base + kb + 1],
                                               ps_s[:, kb * 512:kb * 512 + 512],
                                               AX.X))
                    v.wait_ge(p.dve.h, p.dve.n)
                    p.dve.inc(v.reduce_max(DEN[:, base + 4:base + 5],
                                           DEN[:, base + 0:base + 4], AX.X))
                    v.wait_ge(p.dve.h, p.dve.n)
                    instr = v.tensor_single_scalar(
                        DEN[:, base + 5:base + 6], DEN[:, base + 4:base + 5],
                        -1.0, ALU.mult)
                    p.dve.inc(instr)

                nm[qc] = p.dve.n

            def emit_exp(qc):
                hh_, qg_, qi_, pp_, gg_, ql_ = qmeta[qc]
                par = qc % 2
                base = par * 12
                slot_war = trslot.get(qi_, 0)
                a0 = p.act.n

                @blk.scalar
                def _(s, qc=qc, qi_=qi_, base=base, slot_war=slot_war):
                    s.wait_ge(p.dve.h, nm[qc])
                    if slot_war:
                        s.wait_ge(p.pe.h, slot_war)
                    for kb in range(NKB):
                        instr = s.activation(
                            PSB[:, qi_ * T + kb * 512:qi_ * T + kb * 512 + 512],
                            ps_s[:, kb * 512:kb * 512 + 512],
                            AF.Exp, bias=DEN[:, base + 5:base + 6],
                            accum_out=DEN[:, base + 6 + kb:base + 7 + kb])
                        p.act.inc(instr)

                emA[qc] = a0 + 2
                em[qc] = p.act.n

            def emit_tail(qc):
                par = qc % 2
                base = par * 12
                war_pe = trb.get(qc - 2, [0])[-1]

                @blk.vector
                def _(v, qc=qc, par=par, base=base, war_pe=war_pe):
                    v.wait_ge(p.act.h, em[qc])
                    if war_pe:
                        v.wait_ge(p.pe.h, war_pe)
                    p.dve.inc(v.reduce_sum(DEN[:, base + 10:base + 11],
                                           DEN[:, base + 6:base + 10], AX.X))
                    v.wait_ge(p.dve.h, p.dve.n)
                    p.dve.inc(v.reciprocal(DEN[:, base + 11:base + 12],
                                           DEN[:, base + 10:base + 11]))
                    v.wait_ge(p.dve.h, p.dve.n)
                    instr = v.scalar_tensor_tensor(
                        DIAG[:, par * 128:par * 128 + 128],
                        C["ident"][:, 0:128], DEN[:, base + 11:base + 12],
                        C["zeros"][:, 0:1].broadcast_to([128, 128]),
                        ALU.mult, ALU.add)
                    p.dve.inc(instr)

                dgm[qc] = p.dve.n

            def emit_tr(qc):
                hh_, qg_, qi_, pp_, gg_, ql_ = qmeta[qc]
                par = qc % 2
                marks = []
                ctx_war = cxm.get(gg_ - 1, 0) if qi_ == 0 else 0
                for bb in range(4):
                    bank = (ps_pt, ps_pt2)[bb % 2]
                    ev_eng, ev_cnt = slot_ev.get(bb % 2, ("dve", 0))

                    @blk.tensor
                    def _(t, qc=qc, bb=bb, par=par, qi_=qi_, bank=bank,
                          ev_eng=ev_eng, ev_cnt=ev_cnt):
                        if bb == 0:
                            t.wait_ge(p.dve.h, dgm[qc])
                        if ev_cnt:
                            t.wait_ge((p.dve if ev_eng == "dve" else p.act).h,
                                      ev_cnt)
                        for j in range(4):
                            kc = bb * 4 + j
                            instr = t.matmul(
                                bank[:, j * 128:j * 128 + 128],
                                PSB[:, qi_ * T + kc * 128:
                                    qi_ * T + kc * 128 + 128],
                                DIAG[:, par * 128:par * 128 + 128],
                                start=True, stop=True)
                        p.pe.inc(instr)

                    marks.append(p.pe.n)
                    pe_need = p.pe.n
                    dst3 = PTS[:].rearrange("p (kc q) -> p kc q", q=512)
                    src3 = bank[:, 0:512].rearrange("p (j q) -> p j q", q=128)

                    @blk.vector
                    def _(v, bb=bb, qi_=qi_, pe_need=pe_need,
                          ctx_war=ctx_war, dst3=dst3, src3=src3):
                        if ctx_war:
                            v.wait_ge(p.pe.h, ctx_war)
                        v.wait_ge(p.pe.h, pe_need)
                        instr = v.tensor_copy(
                            dst3[:, bb * 4:bb * 4 + 4,
                                 qi_ * 128:qi_ * 128 + 128], src3)
                        p.dve.inc(instr)

                    slot_ev[bb % 2] = ("dve", p.dve.n)
                trb[qc] = marks
                trslot[qi_] = marks[-1]

            def emit_ctx(gg, hh_, pp_, qg_):
                d_need = p.dve.n
                a_need = p.act.n

                @blk.tensor
                def _(t, hh_=hh_, d_need=d_need, a_need=a_need):
                    t.wait_ge(p.dve.h, d_need)
                    t.wait_ge(p.act.h, a_need)
                    base = slice(0, 64) if hh_ == 0 else slice(64, 128)
                    for kc in range(16):
                        instr = t.matmul(
                            ps_ctx[base, 0:512],
                            VTM[:, kc * 128 + hh_ * 64:
                                kc * 128 + hh_ * 64 + 64],
                            PTS[:, kc * 512:kc * 512 + 512],
                            start=(kc == 0), stop=(kc == 15))
                    p.pe.inc(instr)

                cxm[gg] = p.pe.n
                pe_c = p.pe.n

                @blk.vector
                def _(v, hh_=hh_, pp_=pp_, qg_=qg_, pe_c=pe_c):
                    v.wait_ge(p.pe.h, pe_c)
                    base = slice(0, 64) if hh_ == 0 else slice(64, 128)
                    instr = v.tensor_copy(
                        ATT[base, pp_ * T + qg_ * 512:pp_ * T + qg_ * 512 + 512],
                        ps_ctx[base, 0:512])
                    p.dve.inc(instr)

            for hh in range(2):
                for qg in range(4):
                    gg = pp * 8 + hh * 4 + qg
                    for qi in range(4):
                        ql = qg * 4 + qi
                        qc = QCN[0]
                        QCN[0] += 1
                        qmeta[qc] = (hh, qg, qi, pp, gg, ql)
                        emit_S(qc, first_of_pair=(qc % 32 == 0))
                        emit_max(qc)
                        emit_exp(qc)
                        if qi > 0:
                            emit_tail(qc - 1)
                            emit_tr(qc - 1)
                    emit_tail(qc)
                    emit_tr(qc)
                    emit_ctx(gg, hh, pp, qg)

        d_att = p.dve.n
        a_att = p.act.n
        if "dbg_att" in dr:
            @blk.sync
            def _(sync):
                sync.wait_ge(p.dve.h, d_att)
                sync.wait_ge(p.act.h, a_att)
                p.ds.inc(sync.dma_start(out=dr["dbg_att"], in_=ATT[:]), 16)
                p.ds.inc(sync.dma_start(out=dr["dbg_qt0"], in_=QT[0][:]), 16)
                p.ds.inc(sync.dma_start(out=dr["dbg_kt0"], in_=KT[0][:]), 16)
                p.ds.inc(sync.dma_start(out=dr["dbg_qt1"], in_=QT[1][:]), 16)
                p.ds.inc(sync.dma_start(out=dr["dbg_kt1"], in_=KT[1][:]), 16)
                p.ds.inc(sync.dma_start(out=dr["dbg_vf"], in_=VF[:]), 16)
                p.ds.inc(sync.dma_start(out=dr["dbg_vtm"], in_=VTM[:]), 16)
                p.ds.inc(sync.dma_start(out=dr["dbg_psb"], in_=PSB[:]), 16)
        pe_o0 = p.pe.n
        ld_wo0 = p.dg.n

        @blk.gpsimd
        def _(g):
            g.wait_ge(p.dve.h, d_att)
            g.wait_ge(p.act.h, a_att)
            for m in range(NCH):
                if m >= 2:
                    g.wait_ge(p.pe.h, pe_o0 + 4 * (m - 1))
                instr = g.dma_start(
                    out=WO[:, (m % 2) * 1024:(m % 2) * 1024 + 1024]
                    .rearrange("p (k c) -> p k c", c=128),
                    in_=dr[wo].rearrange("k p m -> p k m")
                    [:, :, m * 128:(m + 1) * 128])
                p.dg.inc(instr, 16)

        @blk.tensor
        def _(t):
            t.wait_ge(p.dve.h, d_att)
            t.wait_ge(p.act.h, a_att)
            n = 0
            for m in range(NCH):
                t.wait_ge(p.dg.h, ld_wo0 + 16 * (m + 1))
                for tg in range(NTG):
                    if n >= 1:
                        t.wait_ge(p.dve.h, d_att + n)
                    for k in range(NCH):
                        instr = t.matmul(
                            ps_qkv[n % 2][:, 0:TG],
                            WO[:, (m % 2) * 1024 + k * 128:
                               (m % 2) * 1024 + k * 128 + 128],
                            ATT[:, k * T + tg * TG:k * T + tg * TG + TG],
                            start=(k == 0), stop=(k == NCH - 1))
                    p.pe.inc(instr)
                    n += 1

        @blk.vector
        def _(v):
            n = 0
            for m in range(NCH):
                for tg in range(NTG):
                    v.wait_ge(p.pe.h, pe_o0 + n + 1)
                    sl = slice(m * T + tg * TG, m * T + tg * TG + TG)
                    if bo is not None:
                        instr = v.scalar_tensor_tensor(
                            XOUT[:, sl], ps_qkv[n % 2][:, 0:TG],
                            BIAS[:, 3 * NCH + m:3 * NCH + m + 1],
                            XOUT[:, sl], ALU.add, ALU.add)
                    else:
                        instr = v.tensor_tensor(
                            XOUT[:, sl], XOUT[:, sl], ps_qkv[n % 2][:, 0:TG],
                            ALU.add)
                    p.dve.inc(instr)
                    n += 1

    es_m.close()
    return dict(done=p.dve.n)






def build_ln_only():
    nc = bass.Bass("TRN2", target_bir_lowering=False)
    es = contextlib.ExitStack()
    with es:
        xd = nc.dram_tensor("x", [D, T], F32, kind="ExternalInput")
        od = nc.dram_tensor("out", [D, T], F32, kind="ExternalOutput")
        blk = es.enter_context(nc.Block())
        p = P(nc, es, blk)
        p.dram["x"] = xd.ap()
        p.dram["out"] = od.ap()
        X = p.sbuf("X", [128, NCH * T], F32)
        XN = X
        XNBF = p.sbuf("XNBF", [128, NCH * T], BF16)
        emit_consts(p)
        emit_stream_load(p, "x", X[:])
        n = p.ds.n

        @blk.vector
        def _(v):
            v.wait_ge(p.ds.h, n)

        @blk.scalar
        def _(s):
            s.wait_ge(p.ds.h, n)

        @blk.tensor
        def _(t):
            t.wait_ge(p.ds.h, n)

        S = LnScratch(p)
        emit_ln(p, S, X[:], XN[:], XNBF[:])
        emit_stream_store(p, XN[:], "out", waits=p.snap())
        fin = p.ds.n

        @blk.sync
        def _(sync):
            sync.wait_ge(p.ds.h, fin)
    return nc


def build_l0ffn_dev():
    nc = bass.Bass("TRN2", target_bir_lowering=False)
    es = contextlib.ExitStack()
    with es:
        nm = dict(
            x=([D, T], F32, "ExternalInput"),
            out=([D, T], F32, "ExternalOutput"),
            L0_ffn_gate_w=([8, 128, 16], F32, "ExternalInput"),
            L0_ffn_gate_b=([16, 1], F32, "ExternalInput"),
            L0_ffn_shared_wg=([8, 128, 512], F32, "ExternalInput"),
            L0_ffn_shared_wu=([8, 128, 512], F32, "ExternalInput"),
            L0_ffn_shared_wd=([4, 128, 1024], F32, "ExternalInput"),
            L0_ffn_experts_wg=([16, 8, 128, 512], F32, "ExternalInput"),
            L0_ffn_experts_wu=([16, 8, 128, 512], F32, "ExternalInput"),
            L0_ffn_experts_wd=([16, 4, 128, 1024], F32, "ExternalInput"),
            cwd=([16, T], F32, "ExternalOutput"),
            dbg_rfm=([16, T], F32, "ExternalOutput"),
            dbg_rtm=([128, 256], F32, "ExternalOutput"),
            dbg_cw=([128, 256], F32, "ExternalOutput"),
            hd=([68, 128, T], BF16, "ExternalOutput"),
        )
        blk = es.enter_context(nc.Block())
        p = P(nc, es, blk)
        for k, (shp, dt, kind) in nm.items():
            p.dram[k] = nc.dram_tensor(k, shp, dt, kind=kind).ap()
        X = p.sbuf("X", [128, NCH * T], F32)
        XNBF = p.sbuf("XNBF", [128, NCH * T], BF16)
        emit_consts(p)
        emit_stream_load(p, "x", X[:])
        n = p.ds.n

        @blk.vector
        def _(v):
            v.wait_ge(p.ds.h, n)
            for c in range(NCH):
                instr = v.tensor_copy(XNBF[:, c * T:(c + 1) * T],
                                      X[:, c * T:(c + 1) * T])
                p.dve.inc(instr)

        emit_l0ffn(p, X[:], XNBF[:], lambda: X[:])
        emit_stream_store(p, X[:], "out", waits=p.snap())
        fin = p.ds.n

        @blk.sync
        def _(sync):
            sync.wait_ge(p.ds.h, fin)
    return nc


def build_mha_dev(alibi=True):
    nc = bass.Bass("TRN2", target_bir_lowering=False)
    es = contextlib.ExitStack()
    with es:
        nm = dict(
            x=([D, T], F32, "ExternalInput"),
            out=([D, T], F32, "ExternalOutput"),
            wq=([8, 128, 1024], F32, "ExternalInput"),
            wk=([8, 128, 1024], F32, "ExternalInput"),
            wv=([8, 128, 1024], F32, "ExternalInput"),
            wo=([8, 128, 1024], F32, "ExternalInput"),
            bq=([128, 8], F32, "ExternalInput"),
            bk=([128, 8], F32, "ExternalInput"),
            bv=([128, 8], F32, "ExternalInput"),
            bo=([128, 8], F32, "ExternalInput"),
            slopes=([16, 1], F32, "ExternalInput"),
            dbg_att=([128, NCH * T], BF16, "ExternalOutput"),
            dbg_qt0=([128, T], BF16, "ExternalOutput"),
            dbg_kt0=([128, T], BF16, "ExternalOutput"),
            dbg_qt1=([128, T], BF16, "ExternalOutput"),
            dbg_kt1=([128, T], BF16, "ExternalOutput"),
            dbg_vf=([128, T], BF16, "ExternalOutput"),
            dbg_vtm=([128, 2048], BF16, "ExternalOutput"),
            dbg_psb=([128, 4 * T], BF16, "ExternalOutput"),
        )
        blk = es.enter_context(nc.Block())
        p = P(nc, es, blk)
        for k, (shp, dt, kind) in nm.items():
            p.dram[k] = nc.dram_tensor(k, shp, dt, kind=kind).ap()
        X = p.sbuf("X", [128, NCH * T], F32)
        XNBF = p.sbuf("XNBF", [128, NCH * T], BF16)
        emit_consts(p)
        emit_stream_load(p, "x", X[:])
        n = p.ds.n

        @blk.vector
        def _(v):
            v.wait_ge(p.ds.h, n)
            for c in range(NCH):
                instr = v.tensor_copy(XNBF[:, c * T:(c + 1) * T],
                                      X[:, c * T:(c + 1) * T])
                p.dve.inc(instr)
            for c in range(NCH):
                instr = v.memset(X[:, c * T:(c + 1) * T], 0.0)
                p.dve.inc(instr)

        emit_mha(p, XNBF[:], X[:], "wq", "wk", "wv", "wo",
                 "bq", "bk", "bv", "bo", 0.125,
                 "slopes" if alibi else None)
        emit_stream_store(p, X[:], "out", waits=p.snap())
        fin = p.ds.n

        @blk.sync
        def _(sync):
            sync.wait_ge(p.ds.h, fin)
    return nc


def emit_stream_add(p, DST, SRC):
    blk = p.blk
    start = p.snap()

    @blk.vector
    def _(v):
        p.wait_snap(v, start)
        for c in range(NCH):
            instr = v.tensor_tensor(DST[:, c * T:(c + 1) * T],
                                    DST[:, c * T:(c + 1) * T],
                                    SRC[:, c * T:(c + 1) * T], ALU.add)
            p.dve.inc(instr)
    return p.dve.n


def emit_spill(p, SRC, dram_name, load=False):
    blk = p.blk
    start = p.snap()

    @blk.sync
    def _(sync):
        p.wait_snap(sync, start)
        if load:
            instr = sync.dma_start(out=SRC, in_=p.dram[dram_name])
        else:
            instr = sync.dma_start(out=p.dram[dram_name], in_=SRC)
        p.ds.inc(instr, 16)
    return p.ds.n


def emit_ln_params(p, S, SRC, DST, DST_BF, gname, bname, gb_tile):
    blk = p.blk
    start = p.snap()

    @blk.sync
    def _(sync):
        p.wait_snap(sync, start)
        instr = sync.dma_start(out=gb_tile[:, 0:NCH], in_=p.dram[gname])
        p.ds.inc(instr, 16)
        instr = sync.dma_start(out=gb_tile[:, NCH:2 * NCH], in_=p.dram[bname])
        p.ds.inc(instr, 16)

    emit_ln(p, S, SRC, DST, DST_BF, gcol=gb_tile[:, 0:NCH],
            becol=gb_tile[:, NCH:2 * NCH])


def emit_block1(p, XOB, XNBF, GB):
    inn = StreamBuf(p, "b1_inner")
    INNER = inn.alloc()
    with contextlib.ExitStack() as es_s:
        S = LnScratch(p, "b1", es_s)
        emit_ln(p, S, XOB.ap, INNER, XNBF)
    inn.spill("xnspill")
    emit_mha(p, XNBF, XOB.ap, "L1_attn_Wq", "L1_attn_Wk", "L1_attn_Wv",
             "L1_attn_Wo", "L1_attn_bq", "L1_attn_bk", "L1_attn_bv",
             "L1_attn_bo", 0.125, "L1_attn_slopes", tag="a1")
    XOB.spill("xospill")
    INNER = inn.reload("xnspill")
    with contextlib.ExitStack() as es_s:
        S = LnScratch(p, "b1b", es_s)
        emit_ln_params(p, S, INNER, None, XNBF, "L1_ffn_g1", "L1_ffn_b1", GB)
    emit_mha(p, XNBF, INNER, "L1_ffn_Wq", "L1_ffn_Wk", "L1_ffn_Wv",
             "L1_ffn_Wo", None, None, None, None, 0.125, None, tag="f1")
    if "dbg_x1" in p.dram:
        emit_spill(p, INNER, "dbg_x1")
    XO = XOB.reload("xospill")
    emit_stream_add(p, XO, INNER)
    with contextlib.ExitStack() as es_s:
        S = LnScratch(p, "b1c", es_s)
        emit_ln_params(p, S, INNER, INNER, XNBF, "L1_ffn_g2", "L1_ffn_b2", GB)
    if "dbg_xn2" in p.dram:
        emit_spill(p, INNER, "dbg_xn2")
    emit_l1moe(p, INNER, XNBF, XO, between=inn.drop)


def build_block1_dev():
    nc = bass.Bass("TRN2", target_bir_lowering=False)
    es = contextlib.ExitStack()
    with es:
        nm = dict(
            x=([D, T], F32, "ExternalInput"),
            out=([D, T], F32, "ExternalOutput"),
            L1_attn_Wq=([8, 128, 1024], F32, "ExternalInput"),
            L1_attn_Wk=([8, 128, 1024], F32, "ExternalInput"),
            L1_attn_Wv=([8, 128, 1024], F32, "ExternalInput"),
            L1_attn_Wo=([8, 128, 1024], F32, "ExternalInput"),
            L1_attn_bq=([128, 8], F32, "ExternalInput"),
            L1_attn_bk=([128, 8], F32, "ExternalInput"),
            L1_attn_bv=([128, 8], F32, "ExternalInput"),
            L1_attn_bo=([128, 8], F32, "ExternalInput"),
            L1_attn_slopes=([16, 1], F32, "ExternalInput"),
            L1_ffn_Wq=([8, 128, 1024], F32, "ExternalInput"),
            L1_ffn_Wk=([8, 128, 1024], F32, "ExternalInput"),
            L1_ffn_Wv=([8, 128, 1024], F32, "ExternalInput"),
            L1_ffn_Wo=([8, 128, 1024], F32, "ExternalInput"),
            L1_ffn_g1=([128, 8], F32, "ExternalInput"),
            L1_ffn_b1=([128, 8], F32, "ExternalInput"),
            L1_ffn_g2=([128, 8], F32, "ExternalInput"),
            L1_ffn_b2=([128, 8], F32, "ExternalInput"),
            L1_ffn_Wg=([8, 128, 8], F32, "ExternalInput"),
            L1_ffn_W1=([8, 8, 128, 512], F32, "ExternalInput"),
            L1_ffn_W2=([8, 4, 128, 1024], F32, "ExternalInput"),
            xnspill=([128, NCH * T], F32, "ExternalOutput"),
            xospill=([128, NCH * T], F32, "ExternalOutput"),
            dbg_x1=([128, NCH * T], F32, "ExternalOutput"),
            dbg_xn2=([128, NCH * T], F32, "ExternalOutput"),
            cwd=([16, T], F32, "ExternalOutput"),
            hd=([68, 128, T], BF16, "ExternalOutput"),
        )
        blk = es.enter_context(nc.Block())
        p = P(nc, es, blk)
        for k, (shp, dt, kind) in nm.items():
            p.dram[k] = nc.dram_tensor(k, shp, dt, kind=kind).ap()
        XNBF = p.sbuf("XNBF", [128, NCH * T], BF16)
        GB = p.sbuf("GB", [128, 2 * NCH], F32)
        p.init_arena(44050)
        XOB = StreamBuf(p, "XO")
        XOB.alloc()
        emit_consts(p)
        emit_stream_load(p, "x", XOB.ap)
        emit_block1(p, XOB, XNBF[:], GB)
        emit_stream_store(p, XOB.ap, "out", waits=p.snap())
        fin = p.ds.n

        @blk.sync
        def _(sync):
            sync.wait_ge(p.ds.h, fin)
    return nc


def build_l0a_dev(nlayers=1):
    nc = bass.Bass("TRN2", target_bir_lowering=False)
    es = contextlib.ExitStack()
    with es:
        nm = dict(
            x=([D, T], F32, "ExternalInput"),
            out=([D, T], F32, "ExternalOutput"),
            hd=([68, 128, T], BF16, "Internal"),
            dbg_xa=([128, NCH * T], F32, "ExternalOutput"),
            dbg_latt=([128, NCH * T], BF16, "ExternalOutput"),
        )
        for l in range(nlayers):
            nm.update({
                f"A{l}_wq": ([8, 128, 1024], F32, "ExternalInput"),
                f"A{l}_wk": ([8, 128, 1024], F32, "ExternalInput"),
                f"A{l}_wv": ([8, 128, 1024], F32, "ExternalInput"),
                f"A{l}_wo": ([8, 128, 1024], F32, "ExternalInput"),
                f"A{l}_bq": ([128, 8], F32, "ExternalInput"),
                f"A{l}_bk": ([128, 8], F32, "ExternalInput"),
                f"A{l}_bv": ([128, 8], F32, "ExternalInput"),
                f"A{l}_bo": ([128, 8], F32, "ExternalInput"),
                f"A{l}_g1": ([128, 8], F32, "ExternalInput"),
                f"A{l}_be1": ([128, 8], F32, "ExternalInput"),
                f"A{l}_g2": ([128, 8], F32, "ExternalInput"),
                f"A{l}_be2": ([128, 8], F32, "ExternalInput"),
                f"A{l}_w1": ([8, 128, 4096], F32, "ExternalInput"),
                f"A{l}_b1": ([128, 32], F32, "ExternalInput"),
                f"A{l}_w2": ([32, 128, 1024], F32, "ExternalInput"),
                f"A{l}_b2": ([128, 8], F32, "ExternalInput"),
            })
        blk = es.enter_context(nc.Block())
        p = P(nc, es, blk)
        for k, (shp, dt, kind) in nm.items():
            p.dram[k] = nc.dram_tensor(k, shp, dt, kind=kind).ap()
        XBF = p.sbuf("XBF", [128, NCH * T], BF16)
        GB = p.sbuf("GB", [128, 2 * NCH], F32)
        p.init_arena(44050)
        inn = StreamBuf(p, "INNER")
        INNER = inn.alloc()
        emit_consts(p)
        emit_stream_load(p, "x", INNER)
        n = p.ds.n

        @blk.vector
        def _(v):
            v.wait_ge(p.ds.h, n)
            for c in range(NCH):
                instr = v.tensor_copy(XBF[:, c * T:(c + 1) * T],
                                      INNER[:, c * T:(c + 1) * T])
                p.dve.inc(instr)

        import os as _os
        _ph = _os.environ.get("L0A_PH")
        phset = set(_ph.split(",")) if _ph else None
        for l in range(nlayers):
            with contextlib.ExitStack() as es_s:
                S = LnScratch(p, f"l0a{l}", es_s)
                emit_l0attn_layer(p, INNER, XBF[:], f"A{l}", S, GB, phset)
        emit_stream_store(p, INNER, "out", waits=p.snap())
        fin = p.ds.n

        @blk.sync
        def _(sync):
            sync.wait_ge(p.ds.h, fin)
    return nc


def emit_block0(p, XOB, XNBF, GB):
    inn = StreamBuf(p, "b0_inner")
    INNER = inn.alloc()
    with contextlib.ExitStack() as es_s:
        S = LnScratch(p, "b0", es_s)
        emit_ln(p, S, XOB.ap, INNER, XNBF)
    emit_stream_add(p, XOB.ap, INNER)

    def pre_e():
        XOB.spill("xospill")
        inn.spill("xnspill")

    emit_l0ffn(p, INNER, XNBF, xo_get=lambda: XOB.ap,
               pre_e=pre_e,
               pre_d=lambda: XOB.reload("xospill"))
    XOB.spill("xospill")
    INNER = inn.reload("xnspill")
    emit_l0attn(p, INNER, XNBF, GB)
    XOB.reload("xospill")
    emit_stream_add(p, XOB.ap, INNER)
    inn.drop()


def _dram_specs(part):
    nm = dict(
        x=([D, T], F32, "ExternalInput"),
        out=([D, T], F32, "ExternalOutput"),
        xnspill=([128, NCH * T], F32, "Internal"),
        xospill=([128, NCH * T], F32, "Internal"),
        cwd=([16, T], F32, "Internal"),
        hd=([68, 128, T], BF16, "Internal"),
    )
    _dbg = os.environ.get("MK_DBG", "")
    if _dbg in ("1", "x1"):
        nm["dbg_x1"] = ([128, NCH * T], F32, "Internal")
    if _dbg in ("1", "xn2"):
        nm["dbg_xn2"] = ([128, NCH * T], F32, "Internal")
    if part == "b0":
        nm["x"] = ([D, T], F32, "ExternalInput")
    if part in ("all", "b0"):
        for l in range(6):
            nm.update({
                f"A{l}_wq": ([8, 128, 1024], F32, "ExternalInput"),
                f"A{l}_wk": ([8, 128, 1024], F32, "ExternalInput"),
                f"A{l}_wv": ([8, 128, 1024], F32, "ExternalInput"),
                f"A{l}_wo": ([8, 128, 1024], F32, "ExternalInput"),
                f"A{l}_bq": ([128, 8], F32, "ExternalInput"),
                f"A{l}_bk": ([128, 8], F32, "ExternalInput"),
                f"A{l}_bv": ([128, 8], F32, "ExternalInput"),
                f"A{l}_bo": ([128, 8], F32, "ExternalInput"),
                f"A{l}_g1": ([128, 8], F32, "ExternalInput"),
                f"A{l}_be1": ([128, 8], F32, "ExternalInput"),
                f"A{l}_g2": ([128, 8], F32, "ExternalInput"),
                f"A{l}_be2": ([128, 8], F32, "ExternalInput"),
                f"A{l}_w1": ([8, 128, 4096], F32, "ExternalInput"),
                f"A{l}_b1": ([128, 32], F32, "ExternalInput"),
                f"A{l}_w2": ([32, 128, 1024], F32, "ExternalInput"),
                f"A{l}_b2": ([128, 8], F32, "ExternalInput"),
            })
        nm.update({
            "A_gf": ([128, 8], F32, "ExternalInput"),
            "A_bf": ([128, 8], F32, "ExternalInput"),
            "L0_ffn_gate_w": ([8, 128, 16], F32, "ExternalInput"),
            "L0_ffn_gate_b": ([16, 1], F32, "ExternalInput"),
            "L0_ffn_shared_wg": ([8, 128, 512], F32, "ExternalInput"),
            "L0_ffn_shared_wu": ([8, 128, 512], F32, "ExternalInput"),
            "L0_ffn_shared_wd": ([4, 128, 1024], F32, "ExternalInput"),
            "L0_ffn_experts_wg": ([16, 8, 128, 512], F32, "ExternalInput"),
            "L0_ffn_experts_wu": ([16, 8, 128, 512], F32, "ExternalInput"),
            "L0_ffn_experts_wd": ([16, 4, 128, 1024], F32, "ExternalInput"),
        })
    if part in ("all", "b1"):
        nm.update({
            "L1_attn_Wq": ([8, 128, 1024], F32, "ExternalInput"),
            "L1_attn_Wk": ([8, 128, 1024], F32, "ExternalInput"),
            "L1_attn_Wv": ([8, 128, 1024], F32, "ExternalInput"),
            "L1_attn_Wo": ([8, 128, 1024], F32, "ExternalInput"),
            "L1_attn_bq": ([128, 8], F32, "ExternalInput"),
            "L1_attn_bk": ([128, 8], F32, "ExternalInput"),
            "L1_attn_bv": ([128, 8], F32, "ExternalInput"),
            "L1_attn_bo": ([128, 8], F32, "ExternalInput"),
            "L1_attn_slopes": ([16, 1], F32, "ExternalInput"),
            "L1_ffn_Wq": ([8, 128, 1024], F32, "ExternalInput"),
            "L1_ffn_Wk": ([8, 128, 1024], F32, "ExternalInput"),
            "L1_ffn_Wv": ([8, 128, 1024], F32, "ExternalInput"),
            "L1_ffn_Wo": ([8, 128, 1024], F32, "ExternalInput"),
            "L1_ffn_g1": ([128, 8], F32, "ExternalInput"),
            "L1_ffn_b1": ([128, 8], F32, "ExternalInput"),
            "L1_ffn_g2": ([128, 8], F32, "ExternalInput"),
            "L1_ffn_b2": ([128, 8], F32, "ExternalInput"),
            "L1_ffn_Wg": ([8, 128, 8], F32, "ExternalInput"),
            "L1_ffn_W1": ([8, 8, 128, 512], F32, "ExternalInput"),
            "L1_ffn_W2": ([8, 4, 128, 1024], F32, "ExternalInput"),
        })
    return nm


def build_full(part="all"):
    nc = bass.Bass("TRN2", target_bir_lowering=False)
    es = contextlib.ExitStack()
    with es:
        blk = es.enter_context(nc.Block())
        p = P(nc, es, blk)
        for k, (shp, dt, kind) in _dram_specs(part).items():
            p.dram[k] = nc.dram_tensor(k, shp, dt, kind=kind).ap()
        XNBF = p.sbuf("XNBF", [128, NCH * T], BF16)
        GB = p.sbuf("GB", [128, 2 * NCH], F32)
        p.init_arena(44050)
        XOB = StreamBuf(p, "XO")
        XOB.alloc()
        emit_consts(p)
        emit_stream_load(p, "x", XOB.ap)
        if part in ("all", "b0"):
            emit_block0(p, XOB, XNBF[:], GB)
        if part in ("all", "b1"):
            emit_block1(p, XOB, XNBF[:], GB)
        emit_stream_store(p, XOB.ap, "out", waits=p.snap())
        fin = p.ds.n

        @blk.sync
        def _(sync):
            sync.wait_ge(p.ds.h, fin)
    return nc


def _prep_weights(inputs):
    I = inputs
    w = {}
    for l in range(6):
        w[f"A{l}_wq"] = _wchunk(I["L0_attn_Wq"][l])
        w[f"A{l}_wk"] = _wchunk(I["L0_attn_Wk"][l])
        w[f"A{l}_wv"] = _wchunk(I["L0_attn_Wv"][l])
        w[f"A{l}_wo"] = _wchunk(I["L0_attn_Wo"][l])
        w[f"A{l}_bq"] = _colvec(I["L0_attn_bq"][l])
        w[f"A{l}_bk"] = _colvec(I["L0_attn_bk"][l])
        w[f"A{l}_bv"] = _colvec(I["L0_attn_bv"][l])
        w[f"A{l}_bo"] = _colvec(I["L0_attn_bo"][l])
        w[f"A{l}_g1"] = _colvec(I["L0_attn_g1"][l])
        w[f"A{l}_be1"] = _colvec(I["L0_attn_be1"][l])
        w[f"A{l}_g2"] = _colvec(I["L0_attn_g2"][l])
        w[f"A{l}_be2"] = _colvec(I["L0_attn_be2"][l])
        w[f"A{l}_w1"] = _wchunk(I["L0_attn_W1"][l])
        w[f"A{l}_b1"] = _colvec(I["L0_attn_b1"][l])
        w[f"A{l}_w2"] = _wchunk(I["L0_attn_W2"][l])
        w[f"A{l}_b2"] = _colvec(I["L0_attn_b2"][l])
    w["A_gf"] = _colvec(I["L0_attn_gf"])
    w["A_bf"] = _colvec(I["L0_attn_bf"])
    w["L0_ffn_gate_w"] = _wchunk(I["L0_ffn_gate_w"])
    w["L0_ffn_gate_b"] = np.ascontiguousarray(
        np.asarray(I["L0_ffn_gate_b"], np.float32)[:, None])
    w["L0_ffn_shared_wg"] = _wchunk(I["L0_ffn_shared_wg"])
    w["L0_ffn_shared_wu"] = _wchunk(I["L0_ffn_shared_wu"])
    w["L0_ffn_shared_wd"] = _wchunk(I["L0_ffn_shared_wd"])
    for nm2 in ("wg", "wu", "wd"):
        w[f"L0_ffn_experts_{nm2}"] = np.stack(
            [_wchunk(np.asarray(I[f"L0_ffn_experts_{nm2}"][e]))
             for e in range(16)])
    w["L1_attn_Wq"] = _wchunk(np.asarray(I["L1_attn_Wq"], np.float32).T)
    w["L1_attn_Wk"] = _wchunk(np.asarray(I["L1_attn_Wk"], np.float32).T)
    w["L1_attn_Wv"] = _wchunk(np.asarray(I["L1_attn_Wv"], np.float32).T)
    w["L1_attn_Wo"] = _wchunk(np.asarray(I["L1_attn_Wo"], np.float32).T)
    w["L1_attn_bq"] = _colvec(I["L1_attn_bq"])
    w["L1_attn_bk"] = _colvec(I["L1_attn_bk"])
    w["L1_attn_bv"] = _colvec(I["L1_attn_bv"])
    w["L1_attn_bo"] = _colvec(I["L1_attn_bo"])
    w["L1_attn_slopes"] = np.ascontiguousarray(
        np.asarray(I["L1_attn_slopes"], np.float32)[:, None])
    for nm2 in ("Wq", "Wk", "Wv", "Wo"):
        w[f"L1_ffn_{nm2}"] = _wchunk(I[f"L1_ffn_{nm2}"])
    w["L1_ffn_g1"] = _colvec(I["L1_ffn_g1"])
    w["L1_ffn_b1"] = _colvec(I["L1_ffn_b1"])
    w["L1_ffn_g2"] = _colvec(I["L1_ffn_g2"])
    w["L1_ffn_b2"] = _colvec(I["L1_ffn_b2"])
    w["L1_ffn_Wg"] = _wchunk(I["L1_ffn_Wg"])
    w["L1_ffn_W1"] = np.stack(
        [_wchunk(np.asarray(I["L1_ffn_W1"][e])) for e in range(8)])
    w["L1_ffn_W2"] = np.stack(
        [_wchunk(np.asarray(I["L1_ffn_W2"][e])) for e in range(8)])
    return w


def kernel(**inputs):
    from concourse.bass_utils import run_bass_kernel_spmd as _run
    w = _prep_weights(inputs)
    x = np.asarray(inputs["L0_attn_x"], np.float32)
    xcores = [np.ascontiguousarray(x[i].T) for i in range(8)]
    mode = int(os.environ.get("MK_LAUNCHES", "1"))
    if mode == 1:
        nc = build_full("all")
        in_maps = [dict(w, x=xcores[i]) for i in range(8)]
        res = _run(nc, in_maps, core_ids=list(range(8)))
        return np.stack([np.asarray(r["out"]).T for r in res.results])
    else:
        nc0 = build_full("b0")
        keys0 = {k for k, (s, d, kd) in _dram_specs("b0").items()
                 if kd == "ExternalInput"}
        in_maps = [{k: (xcores[i] if k == "x" else w[k]) for k in keys0}
                   for i in range(8)]
        res0 = _run(nc0, in_maps, core_ids=list(range(8)))
        mids = [np.asarray(r["out"]) for r in res0.results]
        nc1 = build_full("b1")
        keys1 = {k for k, (s, d, kd) in _dram_specs("b1").items()
                 if kd == "ExternalInput"}
        in_maps = [{k: (mids[i] if k == "x" else w[k]) for k in keys1}
                   for i in range(8)]
        res1 = _run(nc1, in_maps, core_ids=list(range(8)))
        return np.stack([np.asarray(r["out"]).T for r in res1.results])


# version 10
# speedup vs baseline: 1.2676x; 1.2676x over previous
import os
import contextlib
import numpy as np
import ml_dtypes
import concourse.bass as bass
import concourse.mybir as mybir
from concourse.bass_utils import run_bass_kernel_spmd

F32 = mybir.dt.float32
BF16 = mybir.dt.bfloat16
AF = mybir.ActivationFunctionType
ALU = mybir.AluOpType
AX = mybir.AxisListType

D = 1024
T = 2048
NCH = 8
NTG = 4
TG = 512
LN_EPS = 1e-5
ATT_EPS = 1e-6

BF = ml_dtypes.bfloat16



def _wchunk(w):
    w = np.asarray(w, np.float32)
    k, m = w.shape
    return np.ascontiguousarray(w.reshape(k // 128, 128, m))


def _colvec(b):
    b = np.asarray(b, np.float32)
    return np.ascontiguousarray(b.reshape(-1, 128).T)



class Sem:
    def __init__(self, es, nc, name):
        self.h = es.enter_context(nc.semaphore(name))
        self.n = 0

    def inc(self, instr, by=1):
        instr.then_inc(self.h, by)
        self.n += by
        return self.n


class P:

    def __init__(self, nc, es, blk):
        self.nc = nc
        self.es = es
        self.blk = blk
        self.pe = Sem(es, nc, "s_pe")
        self.dve = Sem(es, nc, "s_dve")
        self.act = Sem(es, nc, "s_act")
        self.pool = Sem(es, nc, "s_pool")
        self.ds = Sem(es, nc, "d_sync")
        self.dg = Sem(es, nc, "d_gps")
        self.da = Sem(es, nc, "d_act")
        self.all_sems = [self.pe, self.dve, self.act, self.pool,
                         self.ds, self.dg, self.da]
        self.C = {}
        self.dram = {}
        self._uid = 0

    def uid(self):
        self._uid += 1
        return self._uid

    def sbuf(self, name, shape, dtype, es=None):
        return (es or self.es).enter_context(
            self.nc.sbuf_tensor(name, shape, dtype))

    def init_arena(self, words):
        self._arena = self.es.enter_context(
            self.nc.sbuf_tensor("ARENA", [128, words], F32))
        self._free = [(0, words)]
        self._ar_words = words

    def _alloc(self, name, n):
        for i, (o, l) in enumerate(self._free):
            if l >= n:
                if l == n:
                    self._free.pop(i)
                else:
                    self._free[i] = (o + n, l - n)
                return o
        raise AssertionError(
            f"arena overflow: {name} needs {n}; free={self._free}")

    def _release(self, off, n):
        self._free.append((off, n))
        self._free.sort()
        merged = []
        for o, l in self._free:
            if merged and merged[-1][0] + merged[-1][1] == o:
                merged[-1] = (merged[-1][0], merged[-1][1] + l)
            else:
                merged.append((o, l))
        self._free = merged

    def tile(self, name, shape, dtype, es=None):
        assert shape[0] <= 128
        nbytes = shape[1] * mybir.dt.size(dtype)
        nwords = (nbytes + 3) // 4
        off = self._alloc(name, nwords)
        if es is not None:
            es.callback(lambda o=off, n=nwords: self._release(o, n))
        if dtype == F32:
            v = self._arena[0:shape[0], off:off + shape[1]]
        elif dtype == BF16:
            v = self._arena[:, :].bitcast(BF16)[0:shape[0],
                                                2 * off:2 * off + shape[1]]
        else:
            raise ValueError(dtype)
        return _TileView(v)


    def psum(self, name, shape=(128, 512), dtype=F32):
        return self.es.enter_context(
            self.nc.psum_tensor(name, list(shape), dtype))

    def snap(self):
        return [(s, s.n) for s in self.all_sems]

    def wait_snap(self, eng, snap):
        for s, n in snap:
            if n:
                eng.wait_ge(s.h, n)


class StreamBuf:

    def __init__(self, p, name):
        self.p = p
        self.name = name
        self.es = None
        self.ap = None

    def alloc(self):
        assert self.es is None
        self.es = contextlib.ExitStack()
        self.ap = self.p.tile(self.name, [128, NCH * T], F32, self.es)[:]
        return self.ap

    def drop(self):
        self.es.close()
        self.es = None
        self.ap = None

    def spill(self, dram_name):
        emit_spill(self.p, self.ap, dram_name)
        self.drop()

    def reload(self, dram_name):
        self.alloc()
        emit_spill(self.p, self.ap, dram_name, load=True)
        return self.ap


class _TileView:
    def __init__(self, ap):
        self._ap = ap

    def __getitem__(self, idx):
        return self._ap[idx]

    def ap(self):
        return self._ap


def emit_consts(p):
    blk = p.blk
    C = p.C
    C["ones"] = p.sbuf("c_ones", [128, 128], BF16)
    C["onesf"] = p.sbuf("c_onesf", [1, 128], F32)
    C["ident"] = p.sbuf("c_ident", [128, 128], BF16)
    C["trilx4"] = p.sbuf("c_trilx4", [128, 512], F32)
    C["identf"] = p.sbuf("c_identf", [128, 128], F32)
    C["identb64"] = p.sbuf("c_identb64", [128, 64], BF16)
    C["zeros"] = p.sbuf("c_zeros", [128, 1], F32)

    @blk.vector
    def _(v):
        v.memset(C["ones"][:], 1.0)
        v.memset(C["onesf"][:], 1.0)
        v.memset(C["zeros"][:], 0.0)
        instr = v.memset(C["trilx4"][:], 1.0)
        p.dve.inc(instr)

    dn = p.dve.n

    @blk.gpsimd
    def _(g):
        g.wait_ge(p.dve.h, dn)
        g.affine_select(C["ident"][:], C["ones"][:], [[-1, 128]],
                        ALU.is_equal, 0.0, channel_multiplier=1)
        instr = g.affine_select(C["identf"][:], C["trilx4"][:, 0:128],
                                [[-1, 128]], ALU.is_equal, 0.0,
                                channel_multiplier=1)
        p.pool.inc(instr)
        g.affine_select(C["identb64"][:], C["ones"][:, 0:64], [[-1, 64]],
                        ALU.is_equal, 0.0, channel_multiplier=1, base=-64)
        g.wait_ge(p.pool.h, p.pool.n)
        instr = g.affine_select(C["trilx4"][:], C["trilx4"][:],
                                [[0, 4], [-1, 128]], ALU.is_ge, 0.0,
                                channel_multiplier=1)
        p.pool.inc(instr)



class LnScratch:
    def __init__(self, p, tag="", es=None):
        self.xbf = p.tile(f"lnS_xbf{tag}", [128, 2 * TG], BF16, es)
        self.xsq = p.tile(f"lnS_xsq{tag}", [128, 2 * TG], BF16, es)
        self.R = p.tile(f"lnS_R{tag}", [1, 3 * TG], F32, es)
        self.AB = p.tile(f"lnS_AB{tag}", [1, 2 * TG], F32, es)
        self.HL = p.tile(f"lnS_HL{tag}", [1, 2 * T], BF16, es)
        self.LL = p.tile(f"lnS_LL{tag}", [1, 2 * T], BF16, es)
        self.t1 = p.tile(f"lnS_t1{tag}", [128, 2 * TG], F32, es)


def emit_ln(p, S, SRC, DST, DST_BF, gcol=None, becol=None):
    blk = p.blk
    C = p.C
    ds_entry = p.ds.n
    U = p.uid()
    with contextlib.ExitStack() as ps_es:
        ps_stat = [ps_es.enter_context(p.nc.psum_tensor(f"ps_lnst{i}_{U}", [128, 512], F32))
                   for i in range(8)]

        start = p.snap()
        dve0 = p.dve.n
        act0 = p.act.n
        pe0 = p.pe.n

        @blk.vector
        def _(v):
            p.wait_snap(v, start)
            for c in range(NCH):
                for tg in range(NTG):
                    idx = c * NTG + tg
                    if idx >= 2:
                        v.wait_ge(p.pe.h, pe0 + idx - 1)
                    instr = v.tensor_copy(
                        S.xbf[:, (idx % 2) * TG:(idx % 2) * TG + TG],
                        SRC[:, c * T + tg * TG:c * T + tg * TG + TG])
                    p.dve.inc(instr)

        @blk.scalar
        def _(s):
            p.wait_snap(s, start)
            for idx in range(NCH * NTG):
                s.wait_ge(p.dve.h, dve0 + idx + 1)
                instr = s.activation(
                    S.xsq[:, (idx % 2) * TG:(idx % 2) * TG + TG],
                    S.xbf[:, (idx % 2) * TG:(idx % 2) * TG + TG], AF.Square)
                p.act.inc(instr)

        @blk.tensor
        def _(t):
            for c in range(NCH):
                for tg in range(NTG):
                    idx = c * NTG + tg
                    t.wait_ge(p.act.h, act0 + idx + 1)
                    for si in range(2):
                        srcx = (S.xbf, S.xsq)[si]
                        instr = t.matmul(
                            ps_stat[si * 4 + tg][0:1, 0:TG],
                            C["ones"][:, 0:1],
                            srcx[:, (idx % 2) * TG:(idx % 2) * TG + TG],
                            start=(c == 0), stop=(c == NCH - 1))
                    p.pe.inc(instr)

        pe_stats = p.pe.n
        mu = S.R[0:1, 0:TG]
        var = S.R[0:1, TG:2 * TG]
        tmp = S.R[0:1, 2 * TG:3 * TG]
        arow = S.AB[0:1, 0:TG]
        brow = S.AB[0:1, TG:2 * TG]

        for tg in range(NTG):
            a_hi = S.HL[0:1, tg * TG:tg * TG + TG]
            b_hi = S.HL[0:1, T + tg * TG:T + tg * TG + TG]
            a_lo = S.LL[0:1, tg * TG:tg * TG + TG]
            b_lo = S.LL[0:1, T + tg * TG:T + tg * TG + TG]

            @blk.vector
            def _(v, tg=tg):
                v.wait_ge(p.pe.h, pe_stats)
                v.tensor_single_scalar(mu, ps_stat[tg][0:1, 0:TG],
                                       1.0 / D, ALU.mult)
                v.tensor_single_scalar(tmp, ps_stat[4 + tg][0:1, 0:TG],
                                       1.0 / D, ALU.mult)
                v.tensor_tensor(var, mu, mu, ALU.mult)
                v.scalar_tensor_tensor(var, var, -1.0, tmp, ALU.mult, ALU.add)
                instr = v.tensor_single_scalar(var, var, LN_EPS, ALU.add)
                p.dve.inc(instr)

            d_var = p.dve.n

            @blk.scalar
            def _(s):
                s.wait_ge(p.dve.h, d_var)
                s.activation(tmp, var, AF.Ln)
                instr = s.activation(arow, tmp, AF.Exp, scale=-0.5)
                p.act.inc(instr)

            a_rstd = p.act.n

            @blk.vector
            def _(v, a_hi=a_hi, b_hi=b_hi, a_lo=a_lo, b_lo=b_lo):
                v.wait_ge(p.act.h, a_rstd)
                v.scalar_tensor_tensor(brow, mu, -1.0, arow, ALU.mult, ALU.mult)
                v.tensor_copy(a_hi, arow)
                v.tensor_copy(b_hi, brow)
                v.tensor_copy(tmp, a_hi)
                v.tensor_tensor(a_lo, arow, tmp, ALU.subtract)
                v.tensor_copy(tmp, b_hi)
                instr = v.tensor_tensor(b_lo, brow, tmp, ALU.subtract)
                p.dve.inc(instr)

        d_rows = p.dve.n

    with contextlib.ExitStack() as ps_es:
        ps_ab = [ps_es.enter_context(p.nc.psum_tensor(f"ps_lnab{i}_{U}", [128, 512], F32))
                 for i in range(4)]
        act_marks = []
        dve_tg_marks = []
        for tg in range(NTG):
            a_hi = S.HL[0:1, tg * TG:tg * TG + TG]
            b_hi = S.HL[0:1, T + tg * TG:T + tg * TG + TG]
            a_lo = S.LL[0:1, tg * TG:tg * TG + TG]
            b_lo = S.LL[0:1, T + tg * TG:T + tg * TG + TG]
            bank_a = ps_ab[2 * (tg % 2)]
            bank_b = ps_ab[2 * (tg % 2) + 1]

            @blk.tensor
            def _(t, tg=tg, bank_a=bank_a, bank_b=bank_b,
                  a_hi=a_hi, b_hi=b_hi, a_lo=a_lo, b_lo=b_lo):
                t.wait_ge(p.dve.h, d_rows)
                if tg >= 2:
                    t.wait_ge(p.dve.h, dve_tg_marks[tg - 2])
                for bank, hi, lo in ((bank_a, a_hi, a_lo), (bank_b, b_hi, b_lo)):
                    for half, rt in ((0, hi), (1, lo)):
                        instr = t.matmul(bank[:, 0:TG], C["ones"][0:1, :], rt,
                                         start=(half == 0), stop=(half == 1))
                        if half == 1:
                            p.pe.inc(instr)

            pe_bc = p.pe.n

            for c in range(NCH):
                @blk.vector
                def _(v, c=c, tg=tg, bank_a=bank_a, bank_b=bank_b):
                    v.wait_ge(p.pe.h, pe_bc)
                    t1c = S.t1[:, (c % 2) * TG:(c % 2) * TG + TG]
                    if len(act_marks) >= 2:
                        v.wait_ge(p.act.h, act_marks[-2])
                    v.tensor_tensor(t1c, SRC[:, c * T + tg * TG:
                                             c * T + tg * TG + TG],
                                    bank_a[:, 0:TG], ALU.mult)
                    instr = v.tensor_tensor(t1c, t1c, bank_b[:, 0:TG], ALU.add)
                    p.dve.inc(instr)

                d_t = p.dve.n

                @blk.scalar
                def _(s, c=c, tg=tg):
                    s.wait_ge(p.dve.h, d_t)
                    if gcol is not None:
                        s.wait_ge(p.ds.h, ds_entry)
                    t1c = S.t1[:, (c % 2) * TG:(c % 2) * TG + TG]
                    dsl = slice(c * T + tg * TG, c * T + tg * TG + TG)
                    for dst in (DST, DST_BF):
                        if dst is None:
                            continue
                        if gcol is None:
                            instr = s.activation(dst[:, dsl], t1c, AF.Identity)
                        else:
                            instr = s.activation(dst[:, dsl], t1c, AF.Identity,
                                                 bias=becol[:, c:c + 1],
                                                 scale=gcol[:, c:c + 1])
                        p.act.inc(instr)
                act_marks.append(p.act.n)
            dve_tg_marks.append(p.dve.n)

    return dict(act_done=p.act.n, dve_done=p.dve.n)



def emit_stream_load(p, dram_name, DSTAP, f32=True):
    blk = p.blk
    x = p.dram[dram_name]

    @blk.sync
    def _(sync):
        src = x.rearrange("(c p) t -> p c t", p=128)
        dst = DSTAP.rearrange("p (c t) -> p c t", t=T)
        instr = sync.dma_start(out=dst, in_=src)
        p.ds.inc(instr, 16)
    return p.ds.n


def emit_stream_store(p, SRCAP, dram_name, waits=None):
    blk = p.blk
    y = p.dram[dram_name]

    @blk.sync
    def _(sync):
        if waits:
            p.wait_snap(sync, waits)
        dst = y.rearrange("(c p) t -> p c t", p=128)
        srcv = SRCAP.rearrange("p (c t) -> p c t", t=T)
        instr = sync.dma_start(out=dst, in_=srcv)
        p.ds.inc(instr, 16)
    return p.ds.n



def emit_castload(p, eng_sections_done, pairs):
    blk = p.blk

    @blk.gpsimd
    def _(g):
        for dst, srcap in pairs:
            instr = g.dma_start(out=dst, in_=srcap)
            p.dg.inc(instr, 16)
    return p.dg.n



def emit_l0ffn(p, XN, XNBF, xo_get, pre_e=None, pre_d=None):
    blk = p.blk
    C = p.C
    dr = p.dram
    NE = 16
    NKC = 68

    identf = C["identf"]
    zeros = C["zeros"]
    es_r = contextlib.ExitStack()
    gate_w = p.tile("ff_gatew", [128, 8 * 16], F32, es_r)
    gate_b = p.tile("ff_gateb", [16, 1], F32, es_r)
    r_fm = p.tile("ff_rfm", [16, T], F32, es_r)
    r_tm = p.tile("ff_rtm", [128, 16 * 16], F32, es_r)
    M1 = p.tile("ff_m1", [128, 16], F32, es_r)
    M2 = p.tile("ff_m2", [128, 16], F32, es_r)
    WR = p.tile("ff_wr", [128, 16], F32, es_r)
    I2 = p.tile("ff_i2", [128, 16], F32, es_r)
    TMPB = p.tile("ff_tmpb", [128, 16], F32, es_r)
    CW = p.tile("ff_cw", [128, 16 * 16], F32, es_r)
    CWF = p.tile("ff_cwf", [16, T], F32, es_r)

    start = p.snap()

    @blk.sync
    def _(sync):
        p.wait_snap(sync, start)
        instr = sync.dma_start(
            out=gate_w[:].rearrange("p (k m) -> p k m", m=16),
            in_=dr["L0_ffn_gate_w"].rearrange("k p m -> p k m"))
        p.ds.inc(instr, 16)
        instr = sync.dma_start(out=gate_b[:], in_=dr["L0_ffn_gate_b"])
        p.ds.inc(instr, 16)

    ld_gw = ld_gb = p.ds.n

    with contextlib.ExitStack() as pse:
        ps_r = [pse.enter_context(p.nc.psum_tensor(f"ps_ffr{i}", [128, 512], F32))
                for i in range(2)]
        ps_t = pse.enter_context(p.nc.psum_tensor("ps_fft", [128, 512], F32))
        ps_t2 = pse.enter_context(p.nc.psum_tensor("ps_fft2", [128, 512], F32))

        act_r0 = p.act.n

        @blk.tensor
        def _(t):
            t.wait_ge(p.ds.h, ld_gw)
            p.wait_snap(t, start)
            for tg in range(NTG):
                if tg >= 2:
                    t.wait_ge(p.act.h, act_r0 + tg - 1)
                for k in range(NCH):
                    instr = t.matmul(
                        ps_r[tg % 2][0:16, 0:TG], gate_w[:, k * 16:k * 16 + 16],
                        XN[:, k * T + tg * TG:k * T + tg * TG + TG],
                        start=(k == 0), stop=(k == NCH - 1))
                p.pe.inc(instr)

        pe_r0 = p.pe.n - NTG

        @blk.scalar
        def _(s):
            s.wait_ge(p.ds.h, ld_gb)
            for tg in range(NTG):
                s.wait_ge(p.pe.h, pe_r0 + tg + 1)
                instr = s.activation(r_fm[:, tg * TG:tg * TG + TG],
                                     ps_r[tg % 2][0:16, 0:TG], AF.Sigmoid,
                                     bias=gate_b[:])
                p.act.inc(instr)

        act_r = p.act.n
        assert act_r == act_r0 + NTG

        d_id = p.dve.n

        @blk.tensor
        def _(t):
            t.wait_ge(p.act.h, act_r)
            t.wait_ge(p.dve.h, d_id)
            for tc in range(16):
                instr = t.matmul(ps_t[:, tc * 16:tc * 16 + 16],
                                 r_fm[:, tc * 128:tc * 128 + 128],
                                 identf[0:16, 0:16], is_transpose=True,
                                 start=True, stop=True)
            p.pe.inc(instr)

        pe_tr = p.pe.n

        @blk.vector
        def _(v):
            v.wait_ge(p.pe.h, pe_tr)
            instr = v.tensor_copy(r_tm[:], ps_t[:, 0:256])
            p.dve.inc(instr)

        d_rtm = p.dve.n

        if "dbg_rtm" in dr:
            @blk.sync
            def _(sync):
                sync.wait_ge(p.dve.h, d_rtm)
                p.ds.inc(sync.dma_start(out=dr["dbg_rfm"], in_=r_fm[:]), 16)
                p.ds.inc(sync.dma_start(out=dr["dbg_rtm"], in_=r_tm[:]), 16)

        @blk.vector
        def _(v):
            v.wait_ge(p.dve.h, d_rtm)
            zb = zeros[:, 0:1].broadcast_to([128, 16])

            def S(instr):
                p.dve.inc(instr)
                v.wait_ge(p.dve.h, p.dve.n)
            for tc in range(16):
                blkr = r_tm[:, tc * 16:tc * 16 + 16]
                cwb = CW[:, tc * 16:tc * 16 + 16]
                m1 = M1[:, tc:tc + 1]
                m2 = M2[:, tc:tc + 1]
                S(v.reduce_max(m1, blkr, AX.X))
                S(v.scalar_tensor_tensor(cwb, blkr, m1, zb, ALU.subtract, ALU.is_ge))
                S(v.scalar_tensor_tensor(I2[:], cwb, -1e30, blkr, ALU.mult, ALU.add))
                S(v.reduce_max(m2, I2[:], AX.X))
                S(v.tensor_tensor(WR[:, tc:tc + 1], m1, m2, ALU.add))
                S(v.scalar_tensor_tensor(I2[:], I2[:], m2, zb, ALU.subtract, ALU.is_ge))
                S(v.scalar_tensor_tensor(TMPB[:], I2[:], m2, zb, ALU.mult, ALU.add))
                S(v.scalar_tensor_tensor(cwb, cwb, m1, TMPB[:], ALU.mult, ALU.add))
            S(v.reciprocal(WR[:], WR[:]))
            for tc in range(16):
                S(v.scalar_tensor_tensor(
                    CW[:, tc * 16:tc * 16 + 16], CW[:, tc * 16:tc * 16 + 16],
                    WR[:, tc:tc + 1], zeros[:, 0:1].broadcast_to([128, 16]),
                    ALU.mult, ALU.add))

        d_cw = p.dve.n

        if "dbg_cw" in dr:
            @blk.sync
            def _(sync):
                sync.wait_ge(p.dve.h, d_cw)
                p.ds.inc(sync.dma_start(out=dr["dbg_cw"], in_=CW[:]), 16)

        d_cwf0 = p.dve.n

        @blk.tensor
        def _(t):
            t.wait_ge(p.dve.h, d_cw)
            for tc in range(16):
                bank = (ps_t, ps_t2)[tc % 2]
                if tc >= 2:
                    t.wait_ge(p.dve.h, d_cwf0 + tc - 1)
                instr = t.matmul(bank[0:16, 0:128],
                                 CW[:, tc * 16:tc * 16 + 16], identf[:],
                                 is_transpose=True, start=True, stop=True)
                p.pe.inc(instr)

        pe_cwt0 = p.pe.n - 16

        @blk.vector
        def _(v):
            for tc in range(16):
                v.wait_ge(p.pe.h, pe_cwt0 + tc + 1)
                bank = (ps_t, ps_t2)[tc % 2]
                instr = v.tensor_copy(CWF[:, tc * 128:tc * 128 + 128],
                                      bank[0:16, 0:128])
                p.dve.inc(instr)

        d_cwf = p.dve.n
        assert d_cwf == d_cwf0 + 16

    @blk.sync
    def _(sync):
        sync.wait_ge(p.dve.h, d_cwf)
        instr = sync.dma_start(out=dr["cwd"], in_=CWF[:])
        p.ds.inc(instr, 16)

    ds_cwd = p.ds.n
    es_r.close()
    if pre_e is not None:
        pre_e()
    es_e = contextlib.ExitStack()
    CWB = p.tile("ff_cwb", [128, 2 * T], F32, es_e)
    WGU = p.tile("ff_wgu", [128, 2 * 8 * 512], BF16, es_e)
    WGU2 = p.tile("ff_wgu2", [128, 2 * 8 * 512], BF16, es_e)
    SIL = p.tile("ff_sil", [128, 4 * TG], F32, es_e)
    HP = p.tile("ff_hp", [128, 2 * 4 * TG], BF16, es_e)

    def wgu_src(e):
        if e == 0:
            return [dr["L0_ffn_shared_wg"].rearrange("k p m -> p k m"),
                    dr["L0_ffn_shared_wu"].rearrange("k p m -> p k m")]
        return [dr["L0_ffn_experts_wg"][e - 1].rearrange("k p m -> p k m"),
                dr["L0_ffn_experts_wu"][e - 1].rearrange("k p m -> p k m")]

    fence_e = p.snap()

    @blk.gpsimd
    def _(g):
        p.wait_snap(g, fence_e)
        g.wait_ge(p.ds.h, ds_cwd)
        for i, srcap in enumerate(wgu_src(0)):
            instr = g.dma_start(
                out=WGU[:, i * 4096:i * 4096 + 4096].rearrange(
                    "p (k m) -> p k m", m=512),
                in_=srcap)
            p.dg.inc(instr, 16)

    @blk.sync
    def _(sync):
        p.wait_snap(sync, fence_e)
        instr = sync.dma_start(out=CWB[:, 0:T],
                               in_=dr["cwd"][0:1, :].broadcast_to([128, T]))
        p.ds.inc(instr, 16)

    hd_marks = []
    with contextlib.ExitStack() as pse:
        ps_h = [pse.enter_context(p.nc.psum_tensor(f"ps_ffh{i}", [128, 512], F32))
                for i in range(8)]

        for e in range(17):
            ld_done = p.dg.n
            cw_done = p.ds.n
            wbuf = (WGU, WGU2)[e % 2]
            nbuf = (WGU, WGU2)[(e + 1) % 2]

            if e < 16:
                @blk.gpsimd
                def _(g, e=e, nbuf=nbuf):
                    g.wait_ge(p.pe.h, p.pe.n)
                    for i, srcap in enumerate(wgu_src(e + 1)):
                        instr = g.dma_start(
                            out=nbuf[:, i * 4096:i * 4096 + 4096].rearrange(
                                "p (k m) -> p k m", m=512),
                            in_=srcap)
                        p.dg.inc(instr, 16)

                @blk.sync
                def _(sync, e=e, dve_now=None):
                    sync.wait_ge(p.dve.h, p.dve.n)
                    instr = sync.dma_start(
                        out=CWB[:, (e % 2) * T:(e % 2) * T + T],
                        in_=dr["cwd"][e:e + 1, :].broadcast_to([128, T]))
                    p.ds.inc(instr, 16)

            for tg in range(NTG):
                d_hp0 = p.dve.n

                @blk.tensor
                def _(t, e=e, tg=tg, wbuf=wbuf, ld_done=ld_done, d_hp0=d_hp0):
                    t.wait_ge(p.dg.h, ld_done)
                    t.wait_ge(p.dve.h, d_hp0)
                    t.wait_ge(p.act.h, p.act.n)
                    for half in range(2):
                        for m in range(4):
                            for k in range(NCH):
                                instr = t.matmul(
                                    ps_h[half * 4 + m][:, 0:TG],
                                    wbuf[:, half * 4096 + k * 512 + m * 128:
                                         half * 4096 + k * 512 + m * 128 + 128],
                                    XNBF[:, k * T + tg * TG:k * T + tg * TG + TG],
                                    start=(k == 0), stop=(k == NCH - 1))
                            if half == 1 and m == 3:
                                p.pe.inc(instr)

                pe_h = p.pe.n

                for m in range(4):
                    @blk.scalar
                    def _(s, m=m, pe_h=pe_h):
                        s.wait_ge(p.pe.h, pe_h)
                        instr = s.activation(SIL[:, m * TG:m * TG + TG],
                                             ps_h[m][:, 0:TG], AF.Silu)
                        p.act.inc(instr)

                    a_sil = p.act.n

                    @blk.vector
                    def _(v, e=e, tg=tg, m=m, a_sil=a_sil, cw_done=cw_done):
                        v.wait_ge(p.act.h, a_sil)
                        if m == 0 and len(hd_marks) >= 2:
                            v.wait_ge(p.ds.h, hd_marks[-2])
                        silm = SIL[:, m * TG:m * TG + TG]
                        hp = HP[:, (tg % 2) * 2048 + m * TG:
                                (tg % 2) * 2048 + m * TG + TG]
                        if e == 0:
                            instr = v.tensor_tensor(hp, silm, ps_h[4 + m][:, 0:TG],
                                                    ALU.mult)
                        else:
                            v.wait_ge(p.ds.h, cw_done)
                            v.tensor_tensor(silm, silm, ps_h[4 + m][:, 0:TG],
                                            ALU.mult)
                            instr = v.tensor_tensor(
                                hp, silm,
                                CWB[:, ((e - 1) % 2) * T + tg * TG:
                                    ((e - 1) % 2) * T + tg * TG + TG], ALU.mult)
                        p.dve.inc(instr)

                d_hp = p.dve.n

                @blk.sync
                def _(sync, e=e, tg=tg, d_hp=d_hp):
                    sync.wait_ge(p.dve.h, d_hp)
                    instr = sync.dma_start(
                        out=dr["hd"][4 * e:4 * e + 4, :, tg * TG:tg * TG + TG]
                        .rearrange("m p t -> p m t"),
                        in_=HP[:, (tg % 2) * 2048:(tg % 2) * 2048 + 2048]
                        .rearrange("p (m t) -> p m t", m=4))
                    p.ds.inc(instr, 16)
                hd_marks.append(p.ds.n)

    hd_done = p.ds.n
    es_e.close()
    if pre_d is not None:
        pre_d()

    def wd_src(kc, mh):
        e, sub = kc // 4, kc % 4
        base = (dr["L0_ffn_shared_wd"] if e == 0
                else dr["L0_ffn_experts_wd"][e - 1])
        return base[sub, :, mh * 512:mh * 512 + 512]

    es_d = contextlib.ExitStack()
    HB = p.tile("ff_hb", [128, 8 * TG], BF16, es_d)
    WD = p.tile("ff_wd", [128, NKC * 512], BF16, es_d)
    with contextlib.ExitStack() as pse:
        ps_y = [pse.enter_context(p.nc.psum_tensor(f"ps_ffy{i}", [128, 512], F32))
                for i in range(4)]
        for mh in range(2):
            @blk.gpsimd
            def _(g, mh=mh):
                g.wait_ge(p.pe.h, p.pe.n)
                g.wait_ge(p.dve.h, p.dve.n)
                g.wait_ge(p.act.h, p.act.n)
                g.wait_ge(p.ds.h, hd_done)
                for kc in range(NKC):
                    instr = g.dma_start(out=WD[:, kc * 512:kc * 512 + 512],
                                        in_=wd_src(kc, mh))
                    p.dg.inc(instr, 16)

            wd_ld = p.dg.n
            for tg in range(NTG):
                ds_h0 = p.ds.n

                pe_y0 = p.pe.n

                @blk.sync
                def _(sync, tg=tg, ds_h0=ds_h0, mh=mh, pe_y0=pe_y0):
                    sync.wait_ge(p.ds.h, hd_done)
                    sync.wait_ge(p.dve.h, p.dve.n)
                    for kc in range(NKC):
                        if kc >= 8:
                            sync.wait_ge(p.pe.h, pe_y0 + kc // 4 - 1)
                        instr = sync.dma_start(
                            out=HB[:, (kc % 8) * TG:(kc % 8) * TG + TG],
                            in_=dr["hd"][kc, :, tg * TG:tg * TG + TG])
                        p.ds.inc(instr, 16)

                @blk.tensor
                def _(t, tg=tg, mh=mh, wd_ld=wd_ld, ds_h0=ds_h0, pe_y0=pe_y0,
                      d_prev=None):
                    t.wait_ge(p.dg.h, wd_ld)
                    t.wait_ge(p.dve.h, p.dve.n)
                    for kc in range(NKC):
                        t.wait_ge(p.ds.h, ds_h0 + 16 * (kc + 1))
                        for m in range(4):
                            instr = t.matmul(
                                ps_y[m][:, 0:TG],
                                WD[:, kc * 512 + m * 128:kc * 512 + m * 128 + 128],
                                HB[:, (kc % 8) * TG:(kc % 8) * TG + TG],
                                start=(kc == 0), stop=(kc == NKC - 1))
                        if kc % 4 == 3 or kc == NKC - 1:
                            p.pe.inc(instr)

                pe_y = p.pe.n

                @blk.vector
                def _(v, tg=tg, mh=mh, pe_y=pe_y):
                    v.wait_ge(p.pe.h, pe_y)
                    XOUT = xo_get()
                    for m in range(4):
                        mc = mh * 4 + m
                        sl = slice(mc * T + tg * TG, mc * T + tg * TG + TG)
                        instr = v.tensor_tensor(XOUT[:, sl], XOUT[:, sl],
                                                ps_y[m][:, 0:TG], ALU.add)
                        p.dve.inc(instr)

    es_d.close()
    return dict(done=True)



def emit_l0attn_layer(p, INNER, XBF, lname, S, GBl, phases=None):
    PH = set(phases) if phases else {"scan", "oproj", "ln1", "ffn1",
                                     "ffn2", "ln2"}
    if "scan" in PH:
        PH |= {"qkv", "chunks", "c_at", "c_num", "c_den", "c_afm"}
    if "chunks" in PH:
        pass
    blk = p.blk
    C = p.C
    dr = p.dram
    U = p.uid()
    es_a = contextlib.ExitStack()
    PHI_Q = p.tile(f"la_q{U}", [128, 2 * T], BF16, es_a)
    PHI_K = p.tile(f"la_k{U}", [128, 2 * T], BF16, es_a)
    VV = p.tile(f"la_v{U}", [128, 2 * T], BF16, es_a)
    ATT = p.tile(f"la_att{U}", [128, NCH * T], BF16, es_a)
    WQKV = p.tile(f"la_wqkv{U}", [128, 2 * 3 * 1024], BF16, es_a)
    WO = p.tile(f"la_wo{U}", [128, 2 * 1024], BF16, es_a)
    BIAS = p.tile(f"la_bias{U}", [128, 4 * NCH], F32, es_a)
    KV_S = p.tile(f"la_kvs{U}", [128, 65], F32, es_a)
    KV_B = p.tile(f"la_kvb{U}", [128, 65], BF16, es_a)
    VA = p.tile(f"la_va{U}", [128, 130], BF16, es_a)
    KTM = p.tile(f"la_ktm{U}", [128, 128], BF16, es_a)
    AM = p.tile(f"la_am{U}", [128, 256], BF16, es_a)
    ATMP = p.tile(f"la_atmp{U}", [128, 128], BF16, es_a)
    DH_ = p.tile(f"la_den{U}", [128, 8], F32, es_a)
    T0 = WO[:, 0:TG]
    T1 = WO[:, TG:2 * TG]
    T2 = WO[:, 2 * TG:3 * TG]

    start = p.snap()
    zb = C["zeros"][:, 0:1]

    @blk.sync
    def _(sync):
        p.wait_snap(sync, start)
        for i, nmb in enumerate(("bq", "bk", "bv", "bo")):
            instr = sync.dma_start(out=BIAS[:, i * NCH:(i + 1) * NCH],
                                   in_=dr[f"{lname}_{nmb}"])
            p.ds.inc(instr, 16)

    ld_bias = p.ds.n

    @blk.vector
    def _(v):
        p.wait_snap(v, start)
        v.memset(VA[:, 64:65], 1.0)
        v.memset(VA[:, 129:130], 1.0)
        v.memset(KV_S[:], 0.0)
        instr = v.memset(KV_B[:], 0.0)
        p.dve.inc(instr)

    d_init = p.dve.n

    with contextlib.ExitStack() as pse:
        ps_a = [pse.enter_context(p.nc.psum_tensor(f"ps_laa{i}_{U}",
                                                   [128, 512], F32))
                for i in range(2)]
        ps_tr = pse.enter_context(p.nc.psum_tensor(f"ps_latr{U}",
                                                   [128, 512], F32))
        ps_at = pse.enter_context(p.nc.psum_tensor(f"ps_laat{U}",
                                                   [128, 512], F32))
        ps_num = pse.enter_context(p.nc.psum_tensor(f"ps_lanum{U}",
                                                    [128, 512], F32))
        ps_kvu = pse.enter_context(p.nc.psum_tensor(f"ps_lakvu{U}",
                                                    [128, 512], F32))
        ps_at2 = pse.enter_context(p.nc.psum_tensor(f"ps_laat2{U}",
                                                    [128, 512], F32))

        qkv_pe_prev = [p.pe.n]
        qkv_d_prev = [p.dve.n]
        qkv_a_prev = [p.act.n]
        t_aer = [p.act.n]
        t_mrg = [p.dve.n]
        for pp in (range(8) if ("scan" in PH or "qkv" in PH) else []):
            ld0 = p.dg.n

            @blk.gpsimd
            def _(g, pp=pp, pe_prev=qkv_pe_prev[0]):
                if pp == 0:
                    p.wait_snap(g, start)
                g.wait_ge(p.pe.h, pe_prev)
                for i, wn in enumerate(("wq", "wk", "wv")):
                    instr = g.dma_start(
                        out=WQKV[:, (pp % 2) * 3072 + i * 1024:
                               (pp % 2) * 3072 + (i + 1) * 1024]
                        .rearrange("p (k m) -> p k m", m=128),
                        in_=dr[f"{lname}_{wn}"].rearrange("k p m -> p k m")
                        [:, :, pp * 128:(pp + 1) * 128])
                    p.dg.inc(instr, 16)

            ld_w = p.dg.n
            d_ev0 = qkv_d_prev[0]
            a_ev0 = qkv_a_prev[0]
            pe_q0 = p.pe.n

            evac_marks = []
            for nn in range(12):
                i, tg = nn // NTG, nn % NTG
                tsl = slice((pp % 2) * T + tg * TG,
                            (pp % 2) * T + tg * TG + TG)

                @blk.tensor
                def _(t, pp=pp, i=i, tg=tg, nn=nn, ld_w=ld_w, d_ev0=d_ev0,
                      a_ev0=a_ev0):
                    if nn == 0:
                        t.wait_ge(p.dg.h, ld_w)
                        t.wait_ge(p.dve.h, d_ev0)
                        t.wait_ge(p.act.h, a_ev0)
                    if nn >= 2:
                        t.wait_ge(p.dve.h, evac_marks[nn - 2])
                    for k in range(NCH):
                        instr = t.matmul(
                            ps_a[nn % 2][:, 0:TG],
                            WQKV[:, (pp % 2) * 3072 + i * 1024 + k * 128:
                                 (pp % 2) * 3072 + i * 1024 + k * 128 + 128],
                            XBF[:, k * T + tg * TG:k * T + tg * TG + TG],
                            start=(k == 0), stop=(k == NCH - 1))
                    p.pe.inc(instr)

                pe_mm = p.pe.n
                if i < 2:
                    dst = (PHI_Q, PHI_K)[i]

                    @blk.vector
                    def _(v, pp=pp, i=i, nn=nn, pe_mm=pe_mm,
                          aer=t_aer[0]):
                        v.wait_ge(p.pe.h, pe_mm)
                        v.wait_ge(p.ds.h, ld_bias)
                        v.wait_ge(p.act.h, aer)
                        instr = v.scalar_tensor_tensor(
                            T0, ps_a[nn % 2][:, 0:TG],
                            BIAS[:, i * NCH + pp:i * NCH + pp + 1],
                            zb.broadcast_to([128, TG]), ALU.add, ALU.min)
                        p.dve.inc(instr)

                    d_t0 = p.dve.n

                    @blk.scalar
                    def _(s, nn=nn, d_t0=d_t0, pp=pp, i=i, mrg=t_mrg[0]):
                        s.wait_ge(p.dve.h, d_t0)
                        s.wait_ge(p.dve.h, mrg)
                        s.activation(T1, T0, AF.Exp)
                        instr = s.activation(
                            T2, ps_a[nn % 2][:, 0:TG], AF.Relu,
                            bias=BIAS[:, i * NCH + pp:i * NCH + pp + 1])
                        p.act.inc(instr)

                    a_er = p.act.n
                    t_aer[0] = a_er

                    @blk.vector
                    def _(v, dst=dst, tsl=tsl, nn=nn, a_er=a_er):
                        v.wait_ge(p.act.h, a_er)
                        instr = v.tensor_tensor(
                            dst[:, tsl], T2, T1, ALU.add)
                        p.dve.inc(instr)

                    t_mrg[0] = p.dve.n
                else:
                    @blk.vector
                    def _(v, pp=pp, nn=nn, tsl=tsl, pe_mm=pe_mm):
                        v.wait_ge(p.pe.h, pe_mm)
                        v.wait_ge(p.ds.h, ld_bias)
                        instr = v.scalar_tensor_tensor(
                            VV[:, tsl], ps_a[nn % 2][:, 0:TG],
                            BIAS[:, 2 * NCH + pp:2 * NCH + pp + 1],
                            zb.broadcast_to([128, TG]), ALU.add, ALU.add)
                        p.dve.inc(instr)
                evac_marks.append(p.dve.n)

            d_qkv = p.dve.n
            qkv_pe_prev[0] = p.pe.n
            qkv_d_prev[0] = p.dve.n
            qkv_a_prev[0] = p.act.n

            @blk.vector
            def _(v):
                v.wait_ge(p.pe.h, p.pe.n)
                v.memset(KV_S[:], 0.0)
                instr = v.memset(KV_B[:], 0.0)
                p.dve.inc(instr)

            for ni, n in enumerate(reversed(range(16)) if "chunks" in PH
                                   else []):
                csl = slice((pp % 2) * T + n * 128,
                            (pp % 2) * T + n * 128 + 128)
                d_pre = p.dve.n

                @blk.tensor
                def _(t, pp=pp, n=n, csl=csl, d_pre=d_pre):
                    t.wait_ge(p.dve.h, d_pre)
                    trv = ps_tr[:, 0:512].bitcast(BF16)
                    t.matmul(trv[:, 0:128], PHI_K[:, csl], C["ident"][:],
                             is_transpose=True, start=True, stop=True)
                    instr = t.matmul(trv[:, 128:256], VV[:, csl],
                                     C["ident"][:], is_transpose=True,
                                     start=True, stop=True)
                    p.pe.inc(instr)

                pe_tr = p.pe.n

                @blk.vector
                def _(v, pe_tr=pe_tr):
                    v.wait_ge(p.pe.h, pe_tr)
                    trv = ps_tr[:, 0:512].bitcast(BF16)
                    v.tensor_copy(KTM[:, 0:128], trv[:, 0:128])
                    v.tensor_copy(VA[:, 0:64], trv[:, 128:192])
                    instr = v.tensor_copy(VA[:, 65:129], trv[:, 192:256])
                    p.dve.inc(instr)

                d_tr = p.dve.n
                if "c_at" not in PH:
                    continue

                @blk.tensor
                def _(t, pp=pp, csl=csl, d_tr=d_tr):
                    t.wait_ge(p.dve.h, d_tr)
                    t.matmul(ps_at[:, 0:128], PHI_K[0:64, csl],
                             PHI_Q[0:64, csl], start=True, stop=True)
                    instr = t.matmul(ps_at2[:, 0:128], PHI_K[64:128, csl],
                                     PHI_Q[64:128, csl], start=True, stop=True)
                    p.pe.inc(instr)

                pe_at = p.pe.n

                @blk.vector
                def _(v, pe_at=pe_at):
                    v.wait_ge(p.pe.h, pe_at)
                    v.tensor_tensor(AM[:, 0:128], ps_at[:, 0:128],
                                    C["trilx4"][:, 0:128], ALU.mult)
                    instr = v.tensor_tensor(AM[:, 128:256], ps_at2[:, 0:128],
                                            C["trilx4"][:, 128:256], ALU.mult)
                    p.dve.inc(instr)

                d_am = p.dve.n
                if "c_num" not in PH:
                    continue

                @blk.tensor
                def _(t, pp=pp, csl=csl, d_am=d_am):
                    t.wait_ge(p.dve.h, d_am)
                    for par in range(2):
                        rows = slice(0, 64) if par == 0 else slice(64, 128)
                        t.matmul(ps_num[:, par * 65:par * 65 + 65],
                                 AM[:, par * 128:par * 128 + 128],
                                 VA[:, par * 65:par * 65 + 65],
                                 start=True, stop=False)
                        instr = t.matmul(ps_num[:, par * 65:par * 65 + 65],
                                         PHI_Q[rows, csl], KV_B[rows, 0:65],
                                         start=False, stop=True)
                    for par in range(2):
                        rows = slice(0, 64) if par == 0 else slice(64, 128)
                        instr2 = t.matmul(ps_kvu[rows, 0:65],
                                          KTM[:, par * 64:par * 64 + 64],
                                          VA[:, par * 65:par * 65 + 65],
                                          start=True, stop=True)
                    p.pe.inc(instr)
                    p.pe.inc(instr2)

                pe_num = p.pe.n
                if "c_den" not in PH:
                    continue

                @blk.vector
                def _(v, pe_num=pe_num):
                    v.wait_ge(p.pe.h, pe_num)
                    v.tensor_single_scalar(DH_[:, 0:1], ps_num[:, 64:65],
                                           ATT_EPS, ALU.add)
                    instr = v.tensor_single_scalar(DH_[:, 1:2],
                                                   ps_num[:, 129:130],
                                                   ATT_EPS, ALU.add)
                    p.dve.inc(instr)
                    v.wait_ge(p.dve.h, p.dve.n)
                    p.dve.inc(v.reciprocal(DH_[:, 2:4], DH_[:, 0:2]))
                    v.wait_ge(p.dve.h, p.dve.n)
                    for par in range(2):
                        v.scalar_tensor_tensor(
                            ATMP[:, par * 64:par * 64 + 64],
                            ps_num[:, par * 65:par * 65 + 64],
                            DH_[:, 2 + par:3 + par],
                            C["zeros"][:, 0:1].broadcast_to([128, 64]),
                            ALU.mult, ALU.add)
                    v.tensor_tensor(KV_S[:, 0:65], KV_S[:, 0:65],
                                    ps_kvu[:, 0:65], ALU.add)
                    instr = v.tensor_copy(KV_B[:, 0:65], KV_S[:, 0:65])
                    p.dve.inc(instr)

                d_scan = p.dve.n
                if "c_afm" not in PH:
                    continue

                @blk.tensor
                def _(t, pp=pp, n=n, d_scan=d_scan):
                    t.wait_ge(p.dve.h, d_scan)
                    instr = t.matmul(
                        ps_tr[:, 0:512].bitcast(BF16)[:, 256:384],
                        ATMP[:, 0:128], C["ident"][:, 0:128],
                        is_transpose=True, start=True, stop=True)
                    p.pe.inc(instr)

                pe_afm = p.pe.n

                @blk.vector
                def _(v, pp=pp, n=n, pe_afm=pe_afm):
                    v.wait_ge(p.pe.h, pe_afm)
                    instr = v.tensor_copy(
                        ATT[:, pp * T + n * 128:pp * T + n * 128 + 128],
                        ps_tr[:, 0:512].bitcast(BF16)[:, 256:384])
                    p.dve.inc(instr)

        if "oproj" not in PH:
            es_a.close()
            return
        d_att = p.dve.n
        if "dbg_latt" in dr:
            @blk.sync
            def _(sync):
                sync.wait_ge(p.dve.h, d_att)
                p.ds.inc(sync.dma_start(out=dr["dbg_latt"], in_=ATT[:]), 16)
        pe_o0 = p.pe.n
        ld_wo0 = p.dg.n

        @blk.gpsimd
        def _(g):
            g.wait_ge(p.dve.h, d_att)
            for m in range(NCH):
                if m >= 2:
                    g.wait_ge(p.pe.h, pe_o0 + 4 * (m - 1))
                instr = g.dma_start(
                    out=WO[:, (m % 2) * 1024:(m % 2) * 1024 + 1024]
                    .rearrange("p (k c) -> p k c", c=128),
                    in_=dr[f"{lname}_wo"].rearrange("k p m -> p k m")
                    [:, :, m * 128:(m + 1) * 128])
                p.dg.inc(instr, 16)

        @blk.tensor
        def _(t):
            t.wait_ge(p.dve.h, d_att)
            n = 0
            for m in range(NCH):
                t.wait_ge(p.dg.h, ld_wo0 + 16 * (m + 1))
                for tg in range(NTG):
                    if n >= 2:
                        t.wait_ge(p.dve.h, d_att + n - 1)
                    for k in range(NCH):
                        instr = t.matmul(
                            ps_a[n % 2][:, 0:TG],
                            WO[:, (m % 2) * 1024 + k * 128:
                               (m % 2) * 1024 + k * 128 + 128],
                            ATT[:, k * T + tg * TG:k * T + tg * TG + TG],
                            start=(k == 0), stop=(k == NCH - 1))
                    p.pe.inc(instr)
                    n += 1

        @blk.vector
        def _(v):
            n = 0
            for m in range(NCH):
                for tg in range(NTG):
                    v.wait_ge(p.pe.h, pe_o0 + n + 1)
                    sl = slice(m * T + tg * TG, m * T + tg * TG + TG)
                    instr = v.scalar_tensor_tensor(
                        INNER[:, sl], ps_a[n % 2][:, 0:TG],
                        BIAS[:, 3 * NCH + m:3 * NCH + m + 1],
                        INNER[:, sl], ALU.add, ALU.add)
                    p.dve.inc(instr)
                    n += 1

    es_a.close()
    if "dbg_xa" in dr:
        emit_spill(p, INNER, "dbg_xa")

    if "ln1" not in PH:
        return
    emit_ln_params(p, S, INNER, INNER, XBF, f"{lname}_g1", f"{lname}_be1", GBl)

    if "ffn1" not in PH:
        return
    es_f = contextlib.ExitStack()
    W1B = p.tile(f"la_w1{U}", [128, 2 * 8 * 128], BF16, es_f)
    B1C = p.tile(f"la_b1{U}", [128, 32], F32, es_f)
    HROW = p.tile(f"la_hrow{U}", [128, 2 * T], BF16, es_f)

    startf = p.snap()

    @blk.sync
    def _(sync):
        p.wait_snap(sync, startf)
        instr = sync.dma_start(out=B1C[:], in_=dr[f"{lname}_b1"])
        p.ds.inc(instr, 16)

    ld_b1 = p.ds.n
    ld_w10 = p.dg.n
    hd_marks = []
    pe_f0 = p.pe.n

    @blk.gpsimd
    def _(g):
        p.wait_snap(g, startf)
        for m in range(32):
            if m >= 2:
                g.wait_ge(p.pe.h, pe_f0 + 4 * (m - 1))
            instr = g.dma_start(
                out=W1B[:, (m % 2) * 1024:(m % 2) * 1024 + 1024]
                .rearrange("p (k c) -> p k c", c=128),
                in_=dr[f"{lname}_w1"].rearrange("k p m -> p k m")
                [:, :, m * 128:(m + 1) * 128])
            p.dg.inc(instr, 16)
    with contextlib.ExitStack() as pse:
        ps_f = [pse.enter_context(p.nc.psum_tensor(f"ps_laf{i}_{U}",
                                                   [128, 512], F32))
                for i in range(2)]
        a_f0 = p.act.n

        @blk.tensor
        def _(t):
            n = 0
            for m in range(32):
                t.wait_ge(p.dg.h, ld_w10 + 16 * (m + 1))
                for tg in range(NTG):
                    if n >= 2:
                        t.wait_ge(p.act.h, a_f0 + n - 1)
                    for k in range(NCH):
                        instr = t.matmul(
                            ps_f[n % 2][:, 0:TG],
                            W1B[:, (m % 2) * 1024 + k * 128:
                                (m % 2) * 1024 + k * 128 + 128],
                            XBF[:, k * T + tg * TG:k * T + tg * TG + TG],
                            start=(k == 0), stop=(k == NCH - 1))
                    p.pe.inc(instr)
                    n += 1

        pe_fq0 = pe_f0

        @blk.scalar
        def _(s):
            n = 0
            for m in range(32):
                for tg in range(NTG):
                    s.wait_ge(p.pe.h, pe_fq0 + n + 1)
                    s.wait_ge(p.ds.h, ld_b1)
                    if tg == 0 and len(hd_marks) >= 2:
                        s.wait_ge(p.ds.h, hd_marks[-2])
                    instr = s.activation(
                        HROW[:, (m % 2) * T + tg * TG:(m % 2) * T + tg * TG + TG],
                        ps_f[n % 2][:, 0:TG], AF.Gelu,
                        bias=B1C[:, m:m + 1])
                    p.act.inc(instr)
                    n += 1

        for m in range(32):
            a_need = a_f0 + 4 * (m + 1)

            @blk.sync
            def _(sync, m=m, a_need=a_need):
                sync.wait_ge(p.act.h, a_need)
                instr = sync.dma_start(
                    out=dr["hd"][m, :, :],
                    in_=HROW[:, (m % 2) * T:(m % 2) * T + T])
                p.ds.inc(instr, 16)
            hd_marks.append(p.ds.n)

    hd_done = p.ds.n
    es_f.close()

    if "ffn2" not in PH:
        return
    es_d = contextlib.ExitStack()
    HB = p.tile(f"la_hb{U}", [128, 8 * TG], BF16, es_d)
    WD = p.tile(f"la_wd{U}", [128, 32 * 512], BF16, es_d)
    B2C = p.tile(f"la_b2{U}", [128, NCH], F32, es_d)

    @blk.sync
    def _(sync):
        instr = sync.dma_start(out=B2C[:], in_=dr[f"{lname}_b2"])
        p.ds.inc(instr, 16)

    ld_b2 = p.ds.n

    with contextlib.ExitStack() as pse:
        ps_y = [pse.enter_context(p.nc.psum_tensor(f"ps_lay{i}_{U}",
                                                   [128, 512], F32))
                for i in range(4)]
        pe_dp0 = p.pe.n
        gl = [0]
        for mh in range(2):
            @blk.gpsimd
            def _(g, mh=mh):
                g.wait_ge(p.pe.h, p.pe.n)
                g.wait_ge(p.dve.h, p.dve.n)
                g.wait_ge(p.act.h, p.act.n)
                g.wait_ge(p.ds.h, hd_done)
                for kc in range(32):
                    instr = g.dma_start(
                        out=WD[:, kc * 512:kc * 512 + 512],
                        in_=dr[f"{lname}_w2"][kc, :, mh * 512:mh * 512 + 512])
                    p.dg.inc(instr, 16)

            wd_ld = p.dg.n
            for tg in range(NTG):
                @blk.sync
                def _(sync, tg=tg, mh=mh):
                    sync.wait_ge(p.ds.h, hd_done)
                    for kc in range(32):
                        L = gl[0]
                        if L >= 8:
                            sync.wait_ge(p.pe.h, pe_dp0 + L // 4 - 1)
                        instr = sync.dma_start(
                            out=HB[:, (kc % 8) * TG:(kc % 8) * TG + TG],
                            in_=dr["hd"][kc, :, tg * TG:tg * TG + TG])
                        p.ds.inc(instr, 16)
                        gl[0] += 1

                ds_h0 = p.ds.n - 16 * 32

                @blk.tensor
                def _(t, tg=tg, mh=mh, wd_ld=wd_ld, ds_h0=ds_h0):
                    t.wait_ge(p.dg.h, wd_ld)
                    t.wait_ge(p.dve.h, p.dve.n)
                    for kc in range(32):
                        t.wait_ge(p.ds.h, ds_h0 + 16 * (kc + 1))
                        for m in range(4):
                            instr = t.matmul(
                                ps_y[m][:, 0:TG],
                                WD[:, kc * 512 + m * 128:kc * 512 + m * 128 + 128],
                                HB[:, (kc % 8) * TG:(kc % 8) * TG + TG],
                                start=(kc == 0), stop=(kc == 31))
                        if kc % 4 == 3:
                            p.pe.inc(instr)

                pe_y = p.pe.n

                @blk.vector
                def _(v, tg=tg, mh=mh, pe_y=pe_y):
                    v.wait_ge(p.pe.h, pe_y)
                    v.wait_ge(p.ds.h, ld_b2)
                    for m in range(4):
                        mc = mh * 4 + m
                        sl = slice(mc * T + tg * TG, mc * T + tg * TG + TG)
                        instr = v.scalar_tensor_tensor(
                            INNER[:, sl], ps_y[m][:, 0:TG],
                            B2C[:, mc:mc + 1], INNER[:, sl],
                            ALU.add, ALU.add)
                        p.dve.inc(instr)
    es_d.close()

    if "ln2" not in PH:
        return
    emit_ln_params(p, S, INNER, INNER, XBF, f"{lname}_g2", f"{lname}_be2", GBl)


def emit_l0attn(p, INNER, XBF, GB):
    for l in range(6):
        with contextlib.ExitStack() as es_s:
            S = LnScratch(p, f"l0a{l}", es_s)
            emit_l0attn_layer(p, INNER, XBF, f"A{l}", S, GB)
    with contextlib.ExitStack() as es_s:
        S = LnScratch(p, "l0af", es_s)
        emit_ln_params(p, S, INNER, INNER, None, "A_gf", "A_bf", GB)






def emit_l1moe(p, XN, XNBF, XOUT, between=None):
    blk = p.blk
    C = p.C
    dr = p.dram
    NE = 8
    NKC = 32
    identf = C["identf"]
    zeros = C["zeros"]
    es_r = contextlib.ExitStack()
    gate_w = p.tile("mo_gatew", [128, 8 * 8], F32, es_r)
    r_fm = p.tile("mo_rfm", [8, T], F32, es_r)
    r_tm = p.tile("mo_rtm", [128, 16 * 8], F32, es_r)
    M1 = p.tile("mo_m1", [128, 16], F32, es_r)
    M2 = p.tile("mo_m2", [128, 16], F32, es_r)
    P1 = p.tile("mo_p1", [128, 16], F32, es_r)
    P2 = p.tile("mo_p2", [128, 16], F32, es_r)
    I2 = p.tile("mo_i2", [128, 8], F32, es_r)
    TMPB = p.tile("mo_tmpb", [128, 8], F32, es_r)
    CW = p.tile("mo_cw", [128, 16 * 8], F32, es_r)
    CWF = p.tile("mo_cwf", [8, T], F32, es_r)

    start = p.snap()

    @blk.sync
    def _(sync):
        p.wait_snap(sync, start)
        instr = sync.dma_start(
            out=gate_w[:].rearrange("p (k m) -> p k m", m=8),
            in_=dr["L1_ffn_Wg"].rearrange("k p m -> p k m"))
        p.ds.inc(instr, 16)

    ld_gw = p.ds.n

    with contextlib.ExitStack() as pse:
        ps_r = [pse.enter_context(p.nc.psum_tensor(f"ps_mor{i}", [128, 512], F32))
                for i in range(2)]
        ps_t = pse.enter_context(p.nc.psum_tensor("ps_mot", [128, 512], F32))
        ps_t2 = pse.enter_context(p.nc.psum_tensor("ps_mot2", [128, 512], F32))

        d_ev0 = p.dve.n

        @blk.tensor
        def _(t):
            t.wait_ge(p.ds.h, ld_gw)
            p.wait_snap(t, start)
            for tg in range(NTG):
                if tg >= 2:
                    t.wait_ge(p.dve.h, d_ev0 + tg - 1)
                for k in range(NCH):
                    instr = t.matmul(
                        ps_r[tg % 2][0:8, 0:TG], gate_w[:, k * 8:k * 8 + 8],
                        XN[:, k * T + tg * TG:k * T + tg * TG + TG],
                        start=(k == 0), stop=(k == NCH - 1))
                p.pe.inc(instr)

        pe_r0 = p.pe.n - NTG

        @blk.vector
        def _(v):
            for tg in range(NTG):
                v.wait_ge(p.pe.h, pe_r0 + tg + 1)
                instr = v.tensor_copy(r_fm[:, tg * TG:tg * TG + TG],
                                      ps_r[tg % 2][0:8, 0:TG])
                p.dve.inc(instr)

        d_r = p.dve.n

        @blk.tensor
        def _(t):
            t.wait_ge(p.dve.h, d_r)
            for tc in range(16):
                instr = t.matmul(ps_t[:, tc * 8:tc * 8 + 8],
                                 r_fm[:, tc * 128:tc * 128 + 128],
                                 identf[0:8, 0:8], is_transpose=True,
                                 start=True, stop=True)
            p.pe.inc(instr)

        pe_tr = p.pe.n

        @blk.vector
        def _(v):
            v.wait_ge(p.pe.h, pe_tr)
            instr = v.tensor_copy(r_tm[:], ps_t[:, 0:128])
            p.dve.inc(instr)

        d_rtm = p.dve.n

        @blk.vector
        def _(v):
            v.wait_ge(p.dve.h, d_rtm)
            zb = zeros[:, 0:1].broadcast_to([128, 8])

            def S(instr):
                p.dve.inc(instr)
                v.wait_ge(p.dve.h, p.dve.n)
            for tc in range(16):
                blkr = r_tm[:, tc * 8:tc * 8 + 8]
                cwb = CW[:, tc * 8:tc * 8 + 8]
                m1 = M1[:, tc:tc + 1]
                m2 = M2[:, tc:tc + 1]
                S(v.reduce_max(m1, blkr, AX.X))
                S(v.scalar_tensor_tensor(cwb, blkr, m1, zb, ALU.subtract,
                                         ALU.is_ge))
                S(v.scalar_tensor_tensor(I2[:], cwb, -1e30, blkr, ALU.mult,
                                         ALU.add))
                S(v.reduce_max(m2, I2[:], AX.X))
                S(v.scalar_tensor_tensor(I2[:], I2[:], m2, zb, ALU.subtract,
                                         ALU.is_ge))
                S(v.tensor_copy(TMPB[:], I2[:]))
                S(v.scalar_tensor_tensor(CW[:, tc * 8:tc * 8 + 8], cwb, 1.0,
                                         TMPB[:], ALU.mult, ALU.bypass)
                  if False else v.engine_nop())
                S(v.scalar_tensor_tensor(cwb, TMPB[:], 2.0, cwb, ALU.mult,
                                         ALU.add))
            instr = v.tensor_tensor(P2[:], M1[:], M2[:], ALU.subtract)
            p.dve.inc(instr)

        d_dif = p.dve.n

        @blk.scalar
        def _(s):
            s.wait_ge(p.dve.h, d_dif)
            instr = s.activation(P1[:], P2[:], AF.Sigmoid)
            p.act.inc(instr)

        a_sig = p.act.n

        @blk.vector
        def _(v):
            v.wait_ge(p.act.h, a_sig)

            def S(instr):
                p.dve.inc(instr)
                v.wait_ge(p.dve.h, p.dve.n)
            zb = zeros[:, 0:1].broadcast_to([128, 8])
            S(v.tensor_single_scalar(P2[:], P1[:], -1.0, ALU.mult))
            S(v.tensor_single_scalar(P2[:], P2[:], 1.0, ALU.add))
            for tc in range(16):
                cwb = CW[:, tc * 8:tc * 8 + 8]
                S(v.tensor_single_scalar(I2[:], cwb, 2.0, ALU.is_equal))
                S(v.tensor_single_scalar(TMPB[:], cwb, 1.0, ALU.is_equal))
                S(v.scalar_tensor_tensor(I2[:], I2[:], P2[:, tc:tc + 1], zb,
                                         ALU.mult, ALU.add))
                S(v.scalar_tensor_tensor(cwb, TMPB[:], P1[:, tc:tc + 1],
                                         I2[:], ALU.mult, ALU.add))

        d_cw = p.dve.n
        d_cwf0 = p.dve.n

        @blk.tensor
        def _(t):
            t.wait_ge(p.dve.h, d_cw)
            for tc in range(16):
                bank = (ps_t, ps_t2)[tc % 2]
                if tc >= 2:
                    t.wait_ge(p.dve.h, d_cwf0 + tc - 1)
                instr = t.matmul(bank[0:8, 0:128],
                                 CW[:, tc * 8:tc * 8 + 8], identf[:],
                                 is_transpose=True, start=True, stop=True)
                p.pe.inc(instr)

        pe_cwt0 = p.pe.n - 16

        @blk.vector
        def _(v):
            for tc in range(16):
                v.wait_ge(p.pe.h, pe_cwt0 + tc + 1)
                bank = (ps_t, ps_t2)[tc % 2]
                instr = v.tensor_copy(CWF[:, tc * 128:tc * 128 + 128],
                                      bank[0:8, 0:128])
                p.dve.inc(instr)

        d_cwf = p.dve.n

    @blk.sync
    def _(sync):
        sync.wait_ge(p.dve.h, d_cwf)
        instr = sync.dma_start(out=dr["cwd"][0:8, :], in_=CWF[:])
        p.ds.inc(instr, 16)

    ds_cwd = p.ds.n
    es_r.close()
    if between is not None:
        between()

    es_e = contextlib.ExitStack()
    CWB = p.tile("mo_cwb", [128, 2 * T], F32, es_e)
    W1B = p.tile("mo_w1b", [128, 2 * 8 * 512], BF16, es_e)
    HP = p.tile("mo_hp", [128, 2 * 4 * TG], BF16, es_e)

    def w1_src(e):
        return dr["L1_ffn_W1"][e].rearrange("k p m -> p k m")

    fence = p.snap()

    @blk.gpsimd
    def _(g):
        p.wait_snap(g, fence)
        g.wait_ge(p.ds.h, ds_cwd)
        instr = g.dma_start(
            out=W1B[:, 0:4096].rearrange("p (k m) -> p k m", m=512),
            in_=w1_src(0))
        p.dg.inc(instr, 16)

    @blk.sync
    def _(sync):
        p.wait_snap(sync, fence)
        instr = sync.dma_start(out=CWB[:, 0:T],
                               in_=dr["cwd"][0:1, :].broadcast_to([128, T]))
        p.ds.inc(instr, 16)

    hd_marks = []
    with contextlib.ExitStack() as pse:
        ps_h = [pse.enter_context(p.nc.psum_tensor(f"ps_moh{i}", [128, 512], F32))
                for i in range(8)]
        for e in range(NE):
            ld_done = p.dg.n
            cw_done = p.ds.n
            wbuf = slice((e % 2) * 4096, (e % 2) * 4096 + 4096)
            if e < NE - 1:
                @blk.gpsimd
                def _(g, e=e):
                    g.wait_ge(p.pe.h, p.pe.n)
                    instr = g.dma_start(
                        out=W1B[:, ((e + 1) % 2) * 4096:
                               ((e + 1) % 2) * 4096 + 4096]
                        .rearrange("p (k m) -> p k m", m=512),
                        in_=w1_src(e + 1))
                    p.dg.inc(instr, 16)

                @blk.sync
                def _(sync, e=e):
                    sync.wait_ge(p.dve.h, p.dve.n)
                    instr = sync.dma_start(
                        out=CWB[:, ((e + 1) % 2) * T:((e + 1) % 2) * T + T],
                        in_=dr["cwd"][e + 1:e + 2, :].broadcast_to([128, T]))
                    p.ds.inc(instr, 16)

            for tg in range(NTG):
                d_hp0 = p.dve.n

                @blk.tensor
                def _(t, e=e, tg=tg, wbuf=wbuf, ld_done=ld_done, d_hp0=d_hp0,
                      a_now=None):
                    t.wait_ge(p.dg.h, ld_done)
                    t.wait_ge(p.dve.h, d_hp0)
                    t.wait_ge(p.act.h, p.act.n)
                    for m in range(4):
                        for k in range(NCH):
                            instr = t.matmul(
                                ps_h[(tg % 2) * 4 + m][:, 0:TG],
                                W1B[:, wbuf.start + k * 512 + m * 128:
                                   wbuf.start + k * 512 + m * 128 + 128],
                                XNBF[:, k * T + tg * TG:k * T + tg * TG + TG],
                                start=(k == 0), stop=(k == NCH - 1))
                        if m == 3:
                            p.pe.inc(instr)

                pe_h = p.pe.n

                for m in range(4):
                    @blk.scalar
                    def _(s, tg=tg, m=m, pe_h=pe_h):
                        s.wait_ge(p.pe.h, pe_h)
                        if len(hd_marks) >= 2 and m == 0:
                            s.wait_ge(p.ds.h, hd_marks[-2])
                        instr = s.activation(
                            HP[:, (tg % 2) * 2048 + m * TG:
                               (tg % 2) * 2048 + m * TG + TG],
                            ps_h[(tg % 2) * 4 + m][:, 0:TG], AF.Relu)
                        p.act.inc(instr)

                a_relu = p.act.n

                @blk.vector
                def _(v, e=e, tg=tg, a_relu=a_relu, cw_done=cw_done):
                    v.wait_ge(p.act.h, a_relu)
                    v.wait_ge(p.ds.h, cw_done)
                    for m in range(4):
                        hp = HP[:, (tg % 2) * 2048 + m * TG:
                                (tg % 2) * 2048 + m * TG + TG]
                        instr = v.tensor_tensor(
                            hp, hp,
                            CWB[:, (e % 2) * T + tg * TG:
                                (e % 2) * T + tg * TG + TG], ALU.mult)
                        p.dve.inc(instr)

                d_hp = p.dve.n

                @blk.sync
                def _(sync, e=e, tg=tg, d_hp=d_hp):
                    sync.wait_ge(p.dve.h, d_hp)
                    instr = sync.dma_start(
                        out=dr["hd"][4 * e:4 * e + 4, :, tg * TG:tg * TG + TG]
                        .rearrange("m p t -> p m t"),
                        in_=HP[:, (tg % 2) * 2048:(tg % 2) * 2048 + 2048]
                        .rearrange("p (m t) -> p m t", m=4))
                    p.ds.inc(instr, 16)
                hd_marks.append(p.ds.n)

    hd_done = p.ds.n
    es_e.close()

    def w2_src(kc, mh):
        e, sub = kc // 4, kc % 4
        return dr["L1_ffn_W2"][e][sub, :, mh * 512:mh * 512 + 512]

    es_d = contextlib.ExitStack()
    HB = p.tile("mo_hb", [128, 8 * TG], BF16, es_d)
    WD = p.tile("mo_wd", [128, NKC * 512], BF16, es_d)
    with contextlib.ExitStack() as pse:
        ps_y = [pse.enter_context(p.nc.psum_tensor(f"ps_moy{i}", [128, 512], F32))
                for i in range(4)]
        pe_dp0 = p.pe.n
        gl = [0]
        for mh in range(2):
            @blk.gpsimd
            def _(g, mh=mh):
                g.wait_ge(p.pe.h, p.pe.n)
                g.wait_ge(p.dve.h, p.dve.n)
                g.wait_ge(p.act.h, p.act.n)
                g.wait_ge(p.ds.h, hd_done)
                for kc in range(NKC):
                    instr = g.dma_start(out=WD[:, kc * 512:kc * 512 + 512],
                                        in_=w2_src(kc, mh))
                    p.dg.inc(instr, 16)

            wd_ld = p.dg.n
            for tg in range(NTG):
                pe_y0 = p.pe.n

                @blk.sync
                def _(sync, tg=tg, mh=mh, pe_y0=pe_y0):
                    sync.wait_ge(p.ds.h, hd_done)
                    for kc in range(NKC):
                        L = gl[0]
                        if L >= 8:
                            sync.wait_ge(p.pe.h, pe_dp0 + L // 4 - 1)
                        instr = sync.dma_start(
                            out=HB[:, (kc % 8) * TG:(kc % 8) * TG + TG],
                            in_=dr["hd"][kc, :, tg * TG:tg * TG + TG])
                        p.ds.inc(instr, 16)
                        gl[0] += 1

                ds_h0 = p.ds.n - 16 * NKC

                @blk.tensor
                def _(t, tg=tg, mh=mh, wd_ld=wd_ld, ds_h0=ds_h0):
                    t.wait_ge(p.dg.h, wd_ld)
                    t.wait_ge(p.dve.h, p.dve.n)
                    for kc in range(NKC):
                        t.wait_ge(p.ds.h, ds_h0 + 16 * (kc + 1))
                        for m in range(4):
                            instr = t.matmul(
                                ps_y[m][:, 0:TG],
                                WD[:, kc * 512 + m * 128:kc * 512 + m * 128 + 128],
                                HB[:, (kc % 8) * TG:(kc % 8) * TG + TG],
                                start=(kc == 0), stop=(kc == NKC - 1))
                        if kc % 4 == 3 or kc == NKC - 1:
                            p.pe.inc(instr)

                pe_y = p.pe.n

                @blk.vector
                def _(v, tg=tg, mh=mh, pe_y=pe_y):
                    v.wait_ge(p.pe.h, pe_y)
                    for m in range(4):
                        mc = mh * 4 + m
                        sl = slice(mc * T + tg * TG, mc * T + tg * TG + TG)
                        instr = v.tensor_tensor(XOUT[:, sl], XOUT[:, sl],
                                                ps_y[m][:, 0:TG], ALU.add)
                        p.dve.inc(instr)
    es_d.close()
    return dict(done=p.dve.n)



def emit_mha(p, XNBF, XOUT, wq, wk, wv, wo, bq, bk, bv, bo, scale, alibi,
             tag=""):
    blk = p.blk
    C = p.C
    dr = p.dram
    NH = 16
    DH = 64
    NQC = 16
    NKB = 4

    es_m = contextlib.ExitStack()
    QT = [p.tile(f"mh_qt{i}{tag}", [128, T], BF16, es_m) for i in range(2)]
    KT = [p.tile(f"mh_kt{i}{tag}", [128, T], BF16, es_m) for i in range(2)]
    VF = p.tile(f"mh_vf{tag}", [128, T], BF16, es_m)
    VTM = p.tile(f"mh_vtm{tag}", [128, 2 * 16 * DH], BF16, es_m)
    PSB = p.tile(f"mh_psb{tag}", [128, 4 * T], BF16, es_m)
    PTS = p.tile(f"mh_pts{tag}", [128, 16 * 512], BF16, es_m)
    ATT = p.tile(f"mh_att{tag}", [128, NCH * T], BF16, es_m)
    WQ = p.tile(f"mh_wq{tag}", [128, 3 * 1024], BF16, es_m)
    WO = p.tile(f"mh_wo{tag}", [128, 2 * 1024], BF16, es_m)
    BIAS = p.tile(f"mh_bias{tag}", [128, 4 * NCH], F32, es_m)
    SLO = p.tile(f"mh_slo{tag}", [128, 16], F32, es_m)

    DIAG = p.tile(f"mh_diag{tag}", [128, 2 * 128], BF16, es_m)
    DEN = p.tile(f"mh_den{tag}", [128, 24], F32, es_m)
    scol = p.tile(f"mh_scol{tag}", [128, 1], F32, es_m)

    start = p.snap()

    @blk.vector
    def _(v):
        p.wait_snap(v, start)
        v.memset(scol[:], scale)
        v.memset(QT[0][64:128, :], 0.0)
        v.memset(QT[0][64:65, :], 1.0)
        v.memset(QT[1][0:64, :], 0.0)
        v.memset(QT[1][0:1, :], 1.0)
        v.memset(KT[0][64:128, :], 0.0)
        instr = v.memset(KT[1][0:64, :], 0.0)
        p.dve.inc(instr)

    d_const = p.dve.n
    pl_iota = p.pool.n

    @blk.sync
    def _(sync):
        sync.wait_ge(p.dve.h, d_const)
        instr = sync.dma_start(out=QT[0][65:66, :], in_=QT[0][64:65, :])
        p.ds.inc(instr, 16)
        instr = sync.dma_start(out=QT[1][1:2, :], in_=QT[1][0:1, :])
        p.ds.inc(instr, 16)
    ds_qaug = p.ds.n

    @blk.sync
    def _(sync):
        p.wait_snap(sync, start)
        for i, nmb in enumerate((bq, bk, bv, bo)):
            if nmb is not None:
                instr = sync.dma_start(out=BIAS[:, i * NCH:(i + 1) * NCH],
                                       in_=dr[nmb])
                p.ds.inc(instr, 16)
        if alibi is not None:
            instr = sync.dma_start(out=SLO[0:1, :],
                                   in_=dr[alibi].rearrange("h one -> one h"))
            p.ds.inc(instr, 16)
            instr = sync.dma_start(out=SLO[64:65, :],
                                   in_=dr[alibi].rearrange("h one -> one h"))
            p.ds.inc(instr, 16)

    ld_bias = p.ds.n

    zb512 = C["zeros"][:, 0:1].broadcast_to([128, 512])

    with contextlib.ExitStack() as pse:
        ps_qkv = [pse.enter_context(p.nc.psum_tensor(f"ps_mhq0{tag}",
                                                     [128, 512], F32))] * 2
        ps_s = pse.enter_context(p.nc.psum_tensor(f"ps_mhs{tag}",
                                                  [128, 2048], F32))
        ps_pt = pse.enter_context(p.nc.psum_tensor(f"ps_mhpt{tag}",
                                                   [128, 512], F32))
        ps_pt2 = pse.enter_context(p.nc.psum_tensor(f"ps_mhpt2{tag}",
                                                    [128, 512], F32))
        ps_ctx = pse.enter_context(p.nc.psum_tensor(f"ps_mhctx{tag}",
                                                    [128, 512], F32))

        QCN = [0]
        qmeta = {}
        smark = {}
        nm = {}
        em = {}
        emA = {}
        dgm = {}
        trb = {}
        trslot = {}
        slot_ev = {}
        cxm = {}

        for pp in range(8):
            ld0 = p.dg.n

            @blk.gpsimd
            def _(g, pp=pp, fstart=start):
                if pp == 0:
                    p.wait_snap(g, fstart)
                g.wait_ge(p.pe.h, p.pe.n)
                for i, wn in enumerate((wq, wk, wv)):
                    instr = g.dma_start(
                        out=WQ[:, i * 1024:(i + 1) * 1024]
                        .rearrange("p (k m) -> p k m", m=128),
                        in_=dr[wn].rearrange("k p m -> p k m")
                        [:, :, pp * 128:(pp + 1) * 128])
                    p.dg.inc(instr, 16)

            ld_w = p.dg.n

            d_ev0 = p.dve.n
            pe_q0 = p.pe.n

            @blk.tensor
            def _(t, pp=pp, ld_w=ld_w, d_ev0=d_ev0):
                t.wait_ge(p.dg.h, ld_w)
                p.wait_snap(t, start)
                t.wait_ge(p.dve.h, d_ev0)
                n = 0
                for i in range(3):
                    for tg in range(NTG):
                        if n >= 1:
                            t.wait_ge(p.dve.h, d_ev0 + 2 * n)
                        for k in range(NCH):
                            instr = t.matmul(
                                ps_qkv[n % 2][:, 0:TG],
                                WQ[:, i * 1024 + k * 128:
                                   i * 1024 + k * 128 + 128],
                                XNBF[:, k * T + tg * TG:k * T + tg * TG + TG],
                                start=(k == 0), stop=(k == NCH - 1))
                        p.pe.inc(instr)
                        n += 1

            @blk.vector
            def _(v, pp=pp, pe_q0=pe_q0):
                v.wait_ge(p.ds.h, ld_bias)
                n = 0
                for i in range(3):
                    for tg in range(NTG):
                        v.wait_ge(p.pe.h, pe_q0 + n + 1)
                        ps = ps_qkv[n % 2]
                        tsl = slice(tg * TG, tg * TG + TG)
                        for par in range(2):
                            rows = slice(0, 64) if par == 0 else slice(64, 128)
                            pr = ps[rows, 0:TG]
                            if i == 0:
                                if bq is not None:
                                    instr = v.scalar_tensor_tensor(
                                        QT[par][rows, tsl], pr,
                                        BIAS[rows, 0 * NCH + pp:0 * NCH + pp + 1],
                                        scol[rows, 0:1].broadcast_to([64, TG]),
                                        ALU.add, ALU.mult)
                                else:
                                    instr = v.scalar_tensor_tensor(
                                        QT[par][rows, tsl], pr, float(scale),
                                        zb512[rows, :], ALU.mult, ALU.add)
                            elif i == 1:
                                if bk is not None:
                                    instr = v.scalar_tensor_tensor(
                                        KT[par][rows, tsl], pr,
                                        BIAS[rows, 1 * NCH + pp:1 * NCH + pp + 1],
                                        zb512[rows, :], ALU.add, ALU.add)
                                else:
                                    instr = v.tensor_copy(KT[par][rows, tsl], pr)
                            else:
                                if bv is not None:
                                    instr = v.scalar_tensor_tensor(
                                        VF[rows, tsl], pr,
                                        BIAS[rows, 2 * NCH + pp:2 * NCH + pp + 1],
                                        zb512[rows, :], ALU.add, ALU.add)
                                else:
                                    instr = v.tensor_copy(VF[rows, tsl], pr)
                            p.dve.inc(instr)
                        n += 1

            d_qkv = p.dve.n

            if alibi is not None:
                d_qkv2 = p.dve.n

                @blk.gpsimd
                def _(g, pp=pp, d_qkv2=d_qkv2):
                    g.wait_ge(p.dve.h, d_qkv2)
                    scr = PSB[:].bitcast(F32)
                    g.iota(scr[0:1, T:2 * T], [[1, T]], channel_multiplier=0,
                           allow_small_or_imprecise_dtypes=True)
                    instr = g.iota(scr[64:65, T:2 * T], [[1, T]],
                                   channel_multiplier=0,
                                   allow_small_or_imprecise_dtypes=True)
                    p.pool.inc(instr)

                pl_io = p.pool.n

                @blk.vector
                def _(v, pp=pp, pl_io=pl_io):
                    v.wait_ge(p.ds.h, ld_bias)
                    v.wait_ge(p.pool.h, pl_io)
                    v.wait_ge(p.dve.h, d_qkv)
                    scr = PSB[:].bitcast(F32)
                    for par, rr in ((0, slice(64, 65)), (1, slice(0, 1))):
                        h = 2 * pp + par
                        kt = KT[par]
                        hi_row = kt[64:65, :] if par == 0 else kt[0:1, :]
                        v.scalar_tensor_tensor(
                            scr[rr, 0:T], scr[rr, T:2 * T],
                            SLO[rr, h:h + 1],
                            C["zeros"][rr, 0:1].broadcast_to([1, T]),
                            ALU.mult, ALU.add)
                        v.tensor_copy(hi_row, scr[rr, 0:T])
                        instr = v.scalar_tensor_tensor(
                            VTM[rr, 0:T], hi_row, -1.0, scr[rr, 0:T],
                            ALU.mult, ALU.add)
                        p.dve.inc(instr)

                d_srow_v = p.dve.n

                @blk.sync
                def _(sync, pp=pp, d_srow_v=0):
                    sync.wait_ge(p.dve.h, p.dve.n)
                    instr = sync.dma_start(out=KT[0][65:66, :],
                                           in_=VTM[64:65, 0:T])
                    p.ds.inc(instr, 16)
                    instr = sync.dma_start(out=KT[1][1:2, :],
                                           in_=VTM[0:1, 0:T])
                    p.ds.inc(instr, 16)

            d_srow = p.dve.n
            ds_srow = p.ds.n

            pe_vt0 = p.pe.n
            d_vt0 = p.dve.n

            @blk.tensor
            def _(t, pp=pp, d_qkv=d_qkv, d_vt0=d_vt0):
                t.wait_ge(p.dve.h, d_qkv)
                for half in range(2):
                    if half == 1:
                        t.wait_ge(p.dve.h, d_vt0 + 1)
                    for j in range(8):
                        kc = half * 8 + j
                        instr = t.matmul(
                            ps_pt[:, 0:512].bitcast(BF16)
                            [:, j * 128:j * 128 + 128],
                            VF[:, kc * 128:kc * 128 + 128],
                            C["ident"][:],
                            is_transpose=True, start=True, stop=True)
                    p.pe.inc(instr)

            @blk.vector
            def _(v, pp=pp, pe_vt0=pe_vt0):
                v.wait_ge(p.ds.h, p.ds.n)
                for half in range(2):
                    v.wait_ge(p.pe.h, pe_vt0 + half + 1)
                    instr = v.tensor_copy(
                        VTM[:, half * 1024:half * 1024 + 1024],
                        ps_pt[:, 0:512].bitcast(BF16))
                    p.dve.inc(instr)

            d_vtm = p.dve.n
            slot_ev[0] = ("dve", d_vtm)

            def emit_S(qc, first_of_pair, d_vtm=d_vtm, ds_srow=ds_srow,
                       ds_qaug=ds_qaug):
                hh_, qg_, qi_, pp_, gg_, ql_ = qmeta[qc]
                rows_ap = slice(0, 66) if hh_ == 0 else slice(0, 128)
                pe_s0 = p.pe.n
                warA = emA.get(qc - 1, 0)
                warB = em.get(qc - 1, 0)

                @blk.tensor
                def _(t, qc=qc, hh_=hh_, ql_=ql_, rows_ap=rows_ap, warA=warA,
                      fop=first_of_pair):
                    if fop:
                        t.wait_ge(p.dve.h, d_vtm)
                        t.wait_ge(p.ds.h, ds_srow)
                        t.wait_ge(p.ds.h, ds_qaug)
                    if warA:
                        t.wait_ge(p.act.h, warA)
                    for kb in range(2):
                        instr = t.matmul(
                            ps_s[:, kb * 512:kb * 512 + 512],
                            QT[hh_][rows_ap, ql_ * 128:ql_ * 128 + 128],
                            KT[hh_][rows_ap, kb * 512:kb * 512 + 512],
                            start=True, stop=True)
                        p.pe.inc(instr)

                @blk.tensor
                def _(t, qc=qc, hh_=hh_, ql_=ql_, rows_ap=rows_ap, warB=warB):
                    if warB:
                        t.wait_ge(p.act.h, warB)
                    for kb in range(2, NKB):
                        instr = t.matmul(
                            ps_s[:, kb * 512:kb * 512 + 512],
                            QT[hh_][rows_ap, ql_ * 128:ql_ * 128 + 128],
                            KT[hh_][rows_ap, kb * 512:kb * 512 + 512],
                            start=True, stop=True)
                        p.pe.inc(instr)

                smark[qc] = [pe_s0 + kb + 1 for kb in range(NKB)]

            def emit_max(qc):
                par = qc % 2
                base = par * 12

                @blk.vector
                def _(v, qc=qc, base=base, sm=smark[qc]):
                    for kb in range(NKB):
                        v.wait_ge(p.pe.h, sm[kb])
                        p.dve.inc(v.reduce_max(DEN[:, base + kb:base + kb + 1],
                                               ps_s[:, kb * 512:kb * 512 + 512],
                                               AX.X))
                    v.wait_ge(p.dve.h, p.dve.n)
                    p.dve.inc(v.reduce_max(DEN[:, base + 4:base + 5],
                                           DEN[:, base + 0:base + 4], AX.X))
                    v.wait_ge(p.dve.h, p.dve.n)
                    instr = v.tensor_single_scalar(
                        DEN[:, base + 5:base + 6], DEN[:, base + 4:base + 5],
                        -1.0, ALU.mult)
                    p.dve.inc(instr)

                nm[qc] = p.dve.n

            def emit_exp(qc):
                hh_, qg_, qi_, pp_, gg_, ql_ = qmeta[qc]
                par = qc % 2
                base = par * 12
                slot_war = trslot.get(qi_, 0)
                a0 = p.act.n

                @blk.scalar
                def _(s, qc=qc, qi_=qi_, base=base, slot_war=slot_war):
                    s.wait_ge(p.dve.h, nm[qc])
                    if slot_war:
                        s.wait_ge(p.pe.h, slot_war)
                    for kb in range(NKB):
                        instr = s.activation(
                            PSB[:, qi_ * T + kb * 512:qi_ * T + kb * 512 + 512],
                            ps_s[:, kb * 512:kb * 512 + 512],
                            AF.Exp, bias=DEN[:, base + 5:base + 6],
                            accum_out=DEN[:, base + 6 + kb:base + 7 + kb])
                        p.act.inc(instr)

                emA[qc] = a0 + 2
                em[qc] = p.act.n

            def emit_tail(qc):
                par = qc % 2
                base = par * 12
                war_pe = trb.get(qc - 2, [0])[-1]

                @blk.vector
                def _(v, qc=qc, par=par, base=base, war_pe=war_pe):
                    v.wait_ge(p.act.h, em[qc])
                    if war_pe:
                        v.wait_ge(p.pe.h, war_pe)
                    p.dve.inc(v.reduce_sum(DEN[:, base + 10:base + 11],
                                           DEN[:, base + 6:base + 10], AX.X))
                    v.wait_ge(p.dve.h, p.dve.n)
                    p.dve.inc(v.reciprocal(DEN[:, base + 11:base + 12],
                                           DEN[:, base + 10:base + 11]))
                    v.wait_ge(p.dve.h, p.dve.n)
                    instr = v.scalar_tensor_tensor(
                        DIAG[:, par * 128:par * 128 + 128],
                        C["ident"][:, 0:128], DEN[:, base + 11:base + 12],
                        C["zeros"][:, 0:1].broadcast_to([128, 128]),
                        ALU.mult, ALU.add)
                    p.dve.inc(instr)

                dgm[qc] = p.dve.n

            def emit_tr(qc):
                hh_, qg_, qi_, pp_, gg_, ql_ = qmeta[qc]
                par = qc % 2
                marks = []
                ctx_war = cxm.get(gg_ - 1, 0) if qi_ == 0 else 0
                for bb in range(4):
                    bank = (ps_pt, ps_pt2)[bb % 2]
                    ev_eng, ev_cnt = slot_ev.get(bb % 2, ("dve", 0))

                    @blk.tensor
                    def _(t, qc=qc, bb=bb, par=par, qi_=qi_, bank=bank,
                          ev_eng=ev_eng, ev_cnt=ev_cnt):
                        if bb == 0:
                            t.wait_ge(p.dve.h, dgm[qc])
                        if ev_cnt:
                            t.wait_ge((p.dve if ev_eng == "dve" else p.act).h,
                                      ev_cnt)
                        for j in range(4):
                            kc = bb * 4 + j
                            instr = t.matmul(
                                bank[:, j * 128:j * 128 + 128],
                                PSB[:, qi_ * T + kc * 128:
                                    qi_ * T + kc * 128 + 128],
                                DIAG[:, par * 128:par * 128 + 128],
                                start=True, stop=True)
                        p.pe.inc(instr)

                    marks.append(p.pe.n)
                    pe_need = p.pe.n
                    dst3 = PTS[:].rearrange("p (kc q) -> p kc q", q=512)
                    src3 = bank[:, 0:512].rearrange("p (j q) -> p j q", q=128)

                    @blk.vector
                    def _(v, bb=bb, qi_=qi_, pe_need=pe_need,
                          ctx_war=ctx_war, dst3=dst3, src3=src3):
                        if ctx_war:
                            v.wait_ge(p.pe.h, ctx_war)
                        v.wait_ge(p.pe.h, pe_need)
                        instr = v.tensor_copy(
                            dst3[:, bb * 4:bb * 4 + 4,
                                 qi_ * 128:qi_ * 128 + 128], src3)
                        p.dve.inc(instr)

                    slot_ev[bb % 2] = ("dve", p.dve.n)
                trb[qc] = marks
                trslot[qi_] = marks[-1]

            def emit_ctx(gg, hh_, pp_, qg_):
                d_need = p.dve.n
                a_need = p.act.n

                @blk.tensor
                def _(t, hh_=hh_, d_need=d_need, a_need=a_need):
                    t.wait_ge(p.dve.h, d_need)
                    t.wait_ge(p.act.h, a_need)
                    base = slice(0, 64) if hh_ == 0 else slice(64, 128)
                    for kc in range(16):
                        instr = t.matmul(
                            ps_ctx[base, 0:512],
                            VTM[:, kc * 128 + hh_ * 64:
                                kc * 128 + hh_ * 64 + 64],
                            PTS[:, kc * 512:kc * 512 + 512],
                            start=(kc == 0), stop=(kc == 15))
                    p.pe.inc(instr)

                cxm[gg] = p.pe.n
                pe_c = p.pe.n

                @blk.vector
                def _(v, hh_=hh_, pp_=pp_, qg_=qg_, pe_c=pe_c):
                    v.wait_ge(p.pe.h, pe_c)
                    base = slice(0, 64) if hh_ == 0 else slice(64, 128)
                    instr = v.tensor_copy(
                        ATT[base, pp_ * T + qg_ * 512:pp_ * T + qg_ * 512 + 512],
                        ps_ctx[base, 0:512])
                    p.dve.inc(instr)

            for hh in range(2):
                for qg in range(4):
                    gg = pp * 8 + hh * 4 + qg
                    for qi in range(4):
                        ql = qg * 4 + qi
                        qc = QCN[0]
                        QCN[0] += 1
                        qmeta[qc] = (hh, qg, qi, pp, gg, ql)
                        emit_S(qc, first_of_pair=(qc % 32 == 0))
                        emit_max(qc)
                        emit_exp(qc)
                        if qi > 0:
                            emit_tail(qc - 1)
                            emit_tr(qc - 1)
                    emit_tail(qc)
                    emit_tr(qc)
                    emit_ctx(gg, hh, pp, qg)

        d_att = p.dve.n
        a_att = p.act.n
        if "dbg_att" in dr:
            @blk.sync
            def _(sync):
                sync.wait_ge(p.dve.h, d_att)
                sync.wait_ge(p.act.h, a_att)
                p.ds.inc(sync.dma_start(out=dr["dbg_att"], in_=ATT[:]), 16)
                p.ds.inc(sync.dma_start(out=dr["dbg_qt0"], in_=QT[0][:]), 16)
                p.ds.inc(sync.dma_start(out=dr["dbg_kt0"], in_=KT[0][:]), 16)
                p.ds.inc(sync.dma_start(out=dr["dbg_qt1"], in_=QT[1][:]), 16)
                p.ds.inc(sync.dma_start(out=dr["dbg_kt1"], in_=KT[1][:]), 16)
                p.ds.inc(sync.dma_start(out=dr["dbg_vf"], in_=VF[:]), 16)
                p.ds.inc(sync.dma_start(out=dr["dbg_vtm"], in_=VTM[:]), 16)
                p.ds.inc(sync.dma_start(out=dr["dbg_psb"], in_=PSB[:]), 16)
        pe_o0 = p.pe.n
        ld_wo0 = p.dg.n

        @blk.gpsimd
        def _(g):
            g.wait_ge(p.dve.h, d_att)
            g.wait_ge(p.act.h, a_att)
            for m in range(NCH):
                if m >= 2:
                    g.wait_ge(p.pe.h, pe_o0 + 4 * (m - 1))
                instr = g.dma_start(
                    out=WO[:, (m % 2) * 1024:(m % 2) * 1024 + 1024]
                    .rearrange("p (k c) -> p k c", c=128),
                    in_=dr[wo].rearrange("k p m -> p k m")
                    [:, :, m * 128:(m + 1) * 128])
                p.dg.inc(instr, 16)

        @blk.tensor
        def _(t):
            t.wait_ge(p.dve.h, d_att)
            t.wait_ge(p.act.h, a_att)
            n = 0
            for m in range(NCH):
                t.wait_ge(p.dg.h, ld_wo0 + 16 * (m + 1))
                for tg in range(NTG):
                    if n >= 1:
                        t.wait_ge(p.dve.h, d_att + n)
                    for k in range(NCH):
                        instr = t.matmul(
                            ps_qkv[n % 2][:, 0:TG],
                            WO[:, (m % 2) * 1024 + k * 128:
                               (m % 2) * 1024 + k * 128 + 128],
                            ATT[:, k * T + tg * TG:k * T + tg * TG + TG],
                            start=(k == 0), stop=(k == NCH - 1))
                    p.pe.inc(instr)
                    n += 1

        @blk.vector
        def _(v):
            n = 0
            for m in range(NCH):
                for tg in range(NTG):
                    v.wait_ge(p.pe.h, pe_o0 + n + 1)
                    sl = slice(m * T + tg * TG, m * T + tg * TG + TG)
                    if bo is not None:
                        instr = v.scalar_tensor_tensor(
                            XOUT[:, sl], ps_qkv[n % 2][:, 0:TG],
                            BIAS[:, 3 * NCH + m:3 * NCH + m + 1],
                            XOUT[:, sl], ALU.add, ALU.add)
                    else:
                        instr = v.tensor_tensor(
                            XOUT[:, sl], XOUT[:, sl], ps_qkv[n % 2][:, 0:TG],
                            ALU.add)
                    p.dve.inc(instr)
                    n += 1

    es_m.close()
    return dict(done=p.dve.n)






def build_ln_only():
    nc = bass.Bass("TRN2", target_bir_lowering=False)
    es = contextlib.ExitStack()
    with es:
        xd = nc.dram_tensor("x", [D, T], F32, kind="ExternalInput")
        od = nc.dram_tensor("out", [D, T], F32, kind="ExternalOutput")
        blk = es.enter_context(nc.Block())
        p = P(nc, es, blk)
        p.dram["x"] = xd.ap()
        p.dram["out"] = od.ap()
        X = p.sbuf("X", [128, NCH * T], F32)
        XN = X
        XNBF = p.sbuf("XNBF", [128, NCH * T], BF16)
        emit_consts(p)
        emit_stream_load(p, "x", X[:])
        n = p.ds.n

        @blk.vector
        def _(v):
            v.wait_ge(p.ds.h, n)

        @blk.scalar
        def _(s):
            s.wait_ge(p.ds.h, n)

        @blk.tensor
        def _(t):
            t.wait_ge(p.ds.h, n)

        S = LnScratch(p)
        emit_ln(p, S, X[:], XN[:], XNBF[:])
        emit_stream_store(p, XN[:], "out", waits=p.snap())
        fin = p.ds.n

        @blk.sync
        def _(sync):
            sync.wait_ge(p.ds.h, fin)
    return nc


def build_l0ffn_dev():
    nc = bass.Bass("TRN2", target_bir_lowering=False)
    es = contextlib.ExitStack()
    with es:
        nm = dict(
            x=([D, T], F32, "ExternalInput"),
            out=([D, T], F32, "ExternalOutput"),
            L0_ffn_gate_w=([8, 128, 16], F32, "ExternalInput"),
            L0_ffn_gate_b=([16, 1], F32, "ExternalInput"),
            L0_ffn_shared_wg=([8, 128, 512], F32, "ExternalInput"),
            L0_ffn_shared_wu=([8, 128, 512], F32, "ExternalInput"),
            L0_ffn_shared_wd=([4, 128, 1024], F32, "ExternalInput"),
            L0_ffn_experts_wg=([16, 8, 128, 512], F32, "ExternalInput"),
            L0_ffn_experts_wu=([16, 8, 128, 512], F32, "ExternalInput"),
            L0_ffn_experts_wd=([16, 4, 128, 1024], F32, "ExternalInput"),
            cwd=([16, T], F32, "ExternalOutput"),
            dbg_rfm=([16, T], F32, "ExternalOutput"),
            dbg_rtm=([128, 256], F32, "ExternalOutput"),
            dbg_cw=([128, 256], F32, "ExternalOutput"),
            hd=([68, 128, T], BF16, "ExternalOutput"),
        )
        blk = es.enter_context(nc.Block())
        p = P(nc, es, blk)
        for k, (shp, dt, kind) in nm.items():
            p.dram[k] = nc.dram_tensor(k, shp, dt, kind=kind).ap()
        X = p.sbuf("X", [128, NCH * T], F32)
        XNBF = p.sbuf("XNBF", [128, NCH * T], BF16)
        emit_consts(p)
        emit_stream_load(p, "x", X[:])
        n = p.ds.n

        @blk.vector
        def _(v):
            v.wait_ge(p.ds.h, n)
            for c in range(NCH):
                instr = v.tensor_copy(XNBF[:, c * T:(c + 1) * T],
                                      X[:, c * T:(c + 1) * T])
                p.dve.inc(instr)

        emit_l0ffn(p, X[:], XNBF[:], lambda: X[:])
        emit_stream_store(p, X[:], "out", waits=p.snap())
        fin = p.ds.n

        @blk.sync
        def _(sync):
            sync.wait_ge(p.ds.h, fin)
    return nc


def build_mha_dev(alibi=True):
    nc = bass.Bass("TRN2", target_bir_lowering=False)
    es = contextlib.ExitStack()
    with es:
        nm = dict(
            x=([D, T], F32, "ExternalInput"),
            out=([D, T], F32, "ExternalOutput"),
            wq=([8, 128, 1024], F32, "ExternalInput"),
            wk=([8, 128, 1024], F32, "ExternalInput"),
            wv=([8, 128, 1024], F32, "ExternalInput"),
            wo=([8, 128, 1024], F32, "ExternalInput"),
            bq=([128, 8], F32, "ExternalInput"),
            bk=([128, 8], F32, "ExternalInput"),
            bv=([128, 8], F32, "ExternalInput"),
            bo=([128, 8], F32, "ExternalInput"),
            slopes=([16, 1], F32, "ExternalInput"),
            dbg_att=([128, NCH * T], BF16, "ExternalOutput"),
            dbg_qt0=([128, T], BF16, "ExternalOutput"),
            dbg_kt0=([128, T], BF16, "ExternalOutput"),
            dbg_qt1=([128, T], BF16, "ExternalOutput"),
            dbg_kt1=([128, T], BF16, "ExternalOutput"),
            dbg_vf=([128, T], BF16, "ExternalOutput"),
            dbg_vtm=([128, 2048], BF16, "ExternalOutput"),
            dbg_psb=([128, 4 * T], BF16, "ExternalOutput"),
        )
        blk = es.enter_context(nc.Block())
        p = P(nc, es, blk)
        for k, (shp, dt, kind) in nm.items():
            p.dram[k] = nc.dram_tensor(k, shp, dt, kind=kind).ap()
        X = p.sbuf("X", [128, NCH * T], F32)
        XNBF = p.sbuf("XNBF", [128, NCH * T], BF16)
        emit_consts(p)
        emit_stream_load(p, "x", X[:])
        n = p.ds.n

        @blk.vector
        def _(v):
            v.wait_ge(p.ds.h, n)
            for c in range(NCH):
                instr = v.tensor_copy(XNBF[:, c * T:(c + 1) * T],
                                      X[:, c * T:(c + 1) * T])
                p.dve.inc(instr)
            for c in range(NCH):
                instr = v.memset(X[:, c * T:(c + 1) * T], 0.0)
                p.dve.inc(instr)

        emit_mha(p, XNBF[:], X[:], "wq", "wk", "wv", "wo",
                 "bq", "bk", "bv", "bo", 0.125,
                 "slopes" if alibi else None)
        emit_stream_store(p, X[:], "out", waits=p.snap())
        fin = p.ds.n

        @blk.sync
        def _(sync):
            sync.wait_ge(p.ds.h, fin)
    return nc


def emit_stream_add(p, DST, SRC):
    blk = p.blk
    start = p.snap()

    @blk.vector
    def _(v):
        p.wait_snap(v, start)
        for c in range(NCH):
            instr = v.tensor_tensor(DST[:, c * T:(c + 1) * T],
                                    DST[:, c * T:(c + 1) * T],
                                    SRC[:, c * T:(c + 1) * T], ALU.add)
            p.dve.inc(instr)
    return p.dve.n


def emit_spill(p, SRC, dram_name, load=False):
    blk = p.blk
    start = p.snap()

    @blk.sync
    def _(sync):
        p.wait_snap(sync, start)
        if load:
            instr = sync.dma_start(out=SRC, in_=p.dram[dram_name])
        else:
            instr = sync.dma_start(out=p.dram[dram_name], in_=SRC)
        p.ds.inc(instr, 16)
    return p.ds.n


def emit_ln_params(p, S, SRC, DST, DST_BF, gname, bname, gb_tile):
    blk = p.blk
    start = p.snap()

    @blk.sync
    def _(sync):
        p.wait_snap(sync, start)
        instr = sync.dma_start(out=gb_tile[:, 0:NCH], in_=p.dram[gname])
        p.ds.inc(instr, 16)
        instr = sync.dma_start(out=gb_tile[:, NCH:2 * NCH], in_=p.dram[bname])
        p.ds.inc(instr, 16)

    emit_ln(p, S, SRC, DST, DST_BF, gcol=gb_tile[:, 0:NCH],
            becol=gb_tile[:, NCH:2 * NCH])


def emit_block1(p, XOB, XNBF, GB):
    inn = StreamBuf(p, "b1_inner")
    INNER = inn.alloc()
    with contextlib.ExitStack() as es_s:
        S = LnScratch(p, "b1", es_s)
        emit_ln(p, S, XOB.ap, INNER, XNBF)
    inn.spill("xnspill")
    emit_mha(p, XNBF, XOB.ap, "L1_attn_Wq", "L1_attn_Wk", "L1_attn_Wv",
             "L1_attn_Wo", "L1_attn_bq", "L1_attn_bk", "L1_attn_bv",
             "L1_attn_bo", 0.125, "L1_attn_slopes", tag="a1")
    XOB.spill("xospill")
    INNER = inn.reload("xnspill")
    with contextlib.ExitStack() as es_s:
        S = LnScratch(p, "b1b", es_s)
        emit_ln_params(p, S, INNER, None, XNBF, "L1_ffn_g1", "L1_ffn_b1", GB)
    emit_mha(p, XNBF, INNER, "L1_ffn_Wq", "L1_ffn_Wk", "L1_ffn_Wv",
             "L1_ffn_Wo", None, None, None, None, 0.125, None, tag="f1")
    if "dbg_x1" in p.dram:
        emit_spill(p, INNER, "dbg_x1")
    XO = XOB.reload("xospill")
    emit_stream_add(p, XO, INNER)
    with contextlib.ExitStack() as es_s:
        S = LnScratch(p, "b1c", es_s)
        emit_ln_params(p, S, INNER, INNER, XNBF, "L1_ffn_g2", "L1_ffn_b2", GB)
    if "dbg_xn2" in p.dram:
        emit_spill(p, INNER, "dbg_xn2")
    emit_l1moe(p, INNER, XNBF, XO, between=inn.drop)


def build_block1_dev():
    nc = bass.Bass("TRN2", target_bir_lowering=False)
    es = contextlib.ExitStack()
    with es:
        nm = dict(
            x=([D, T], F32, "ExternalInput"),
            out=([D, T], F32, "ExternalOutput"),
            L1_attn_Wq=([8, 128, 1024], F32, "ExternalInput"),
            L1_attn_Wk=([8, 128, 1024], F32, "ExternalInput"),
            L1_attn_Wv=([8, 128, 1024], F32, "ExternalInput"),
            L1_attn_Wo=([8, 128, 1024], F32, "ExternalInput"),
            L1_attn_bq=([128, 8], F32, "ExternalInput"),
            L1_attn_bk=([128, 8], F32, "ExternalInput"),
            L1_attn_bv=([128, 8], F32, "ExternalInput"),
            L1_attn_bo=([128, 8], F32, "ExternalInput"),
            L1_attn_slopes=([16, 1], F32, "ExternalInput"),
            L1_ffn_Wq=([8, 128, 1024], F32, "ExternalInput"),
            L1_ffn_Wk=([8, 128, 1024], F32, "ExternalInput"),
            L1_ffn_Wv=([8, 128, 1024], F32, "ExternalInput"),
            L1_ffn_Wo=([8, 128, 1024], F32, "ExternalInput"),
            L1_ffn_g1=([128, 8], F32, "ExternalInput"),
            L1_ffn_b1=([128, 8], F32, "ExternalInput"),
            L1_ffn_g2=([128, 8], F32, "ExternalInput"),
            L1_ffn_b2=([128, 8], F32, "ExternalInput"),
            L1_ffn_Wg=([8, 128, 8], F32, "ExternalInput"),
            L1_ffn_W1=([8, 8, 128, 512], F32, "ExternalInput"),
            L1_ffn_W2=([8, 4, 128, 1024], F32, "ExternalInput"),
            xnspill=([128, NCH * T], F32, "ExternalOutput"),
            xospill=([128, NCH * T], F32, "ExternalOutput"),
            dbg_x1=([128, NCH * T], F32, "ExternalOutput"),
            dbg_xn2=([128, NCH * T], F32, "ExternalOutput"),
            cwd=([16, T], F32, "ExternalOutput"),
            hd=([68, 128, T], BF16, "ExternalOutput"),
        )
        blk = es.enter_context(nc.Block())
        p = P(nc, es, blk)
        for k, (shp, dt, kind) in nm.items():
            p.dram[k] = nc.dram_tensor(k, shp, dt, kind=kind).ap()
        XNBF = p.sbuf("XNBF", [128, NCH * T], BF16)
        GB = p.sbuf("GB", [128, 2 * NCH], F32)
        p.init_arena(44050)
        XOB = StreamBuf(p, "XO")
        XOB.alloc()
        emit_consts(p)
        emit_stream_load(p, "x", XOB.ap)
        emit_block1(p, XOB, XNBF[:], GB)
        emit_stream_store(p, XOB.ap, "out", waits=p.snap())
        fin = p.ds.n

        @blk.sync
        def _(sync):
            sync.wait_ge(p.ds.h, fin)
    return nc


def build_l0a_dev(nlayers=1):
    nc = bass.Bass("TRN2", target_bir_lowering=False)
    es = contextlib.ExitStack()
    with es:
        nm = dict(
            x=([D, T], F32, "ExternalInput"),
            out=([D, T], F32, "ExternalOutput"),
            hd=([68, 128, T], BF16, "Internal"),
            dbg_xa=([128, NCH * T], F32, "ExternalOutput"),
            dbg_latt=([128, NCH * T], BF16, "ExternalOutput"),
        )
        for l in range(nlayers):
            nm.update({
                f"A{l}_wq": ([8, 128, 1024], F32, "ExternalInput"),
                f"A{l}_wk": ([8, 128, 1024], F32, "ExternalInput"),
                f"A{l}_wv": ([8, 128, 1024], F32, "ExternalInput"),
                f"A{l}_wo": ([8, 128, 1024], F32, "ExternalInput"),
                f"A{l}_bq": ([128, 8], F32, "ExternalInput"),
                f"A{l}_bk": ([128, 8], F32, "ExternalInput"),
                f"A{l}_bv": ([128, 8], F32, "ExternalInput"),
                f"A{l}_bo": ([128, 8], F32, "ExternalInput"),
                f"A{l}_g1": ([128, 8], F32, "ExternalInput"),
                f"A{l}_be1": ([128, 8], F32, "ExternalInput"),
                f"A{l}_g2": ([128, 8], F32, "ExternalInput"),
                f"A{l}_be2": ([128, 8], F32, "ExternalInput"),
                f"A{l}_w1": ([8, 128, 4096], F32, "ExternalInput"),
                f"A{l}_b1": ([128, 32], F32, "ExternalInput"),
                f"A{l}_w2": ([32, 128, 1024], F32, "ExternalInput"),
                f"A{l}_b2": ([128, 8], F32, "ExternalInput"),
            })
        blk = es.enter_context(nc.Block())
        p = P(nc, es, blk)
        for k, (shp, dt, kind) in nm.items():
            p.dram[k] = nc.dram_tensor(k, shp, dt, kind=kind).ap()
        XBF = p.sbuf("XBF", [128, NCH * T], BF16)
        GB = p.sbuf("GB", [128, 2 * NCH], F32)
        p.init_arena(44050)
        inn = StreamBuf(p, "INNER")
        INNER = inn.alloc()
        emit_consts(p)
        emit_stream_load(p, "x", INNER)
        n = p.ds.n

        @blk.vector
        def _(v):
            v.wait_ge(p.ds.h, n)
            for c in range(NCH):
                instr = v.tensor_copy(XBF[:, c * T:(c + 1) * T],
                                      INNER[:, c * T:(c + 1) * T])
                p.dve.inc(instr)

        import os as _os
        _ph = _os.environ.get("L0A_PH")
        phset = set(_ph.split(",")) if _ph else None
        for l in range(nlayers):
            with contextlib.ExitStack() as es_s:
                S = LnScratch(p, f"l0a{l}", es_s)
                emit_l0attn_layer(p, INNER, XBF[:], f"A{l}", S, GB, phset)
        emit_stream_store(p, INNER, "out", waits=p.snap())
        fin = p.ds.n

        @blk.sync
        def _(sync):
            sync.wait_ge(p.ds.h, fin)
    return nc


def emit_block0(p, XOB, XNBF, GB):
    inn = StreamBuf(p, "b0_inner")
    INNER = inn.alloc()
    with contextlib.ExitStack() as es_s:
        S = LnScratch(p, "b0", es_s)
        emit_ln(p, S, XOB.ap, INNER, XNBF)
    emit_stream_add(p, XOB.ap, INNER)

    def pre_e():
        XOB.spill("xospill")
        inn.spill("xnspill")

    emit_l0ffn(p, INNER, XNBF, xo_get=lambda: XOB.ap,
               pre_e=pre_e,
               pre_d=lambda: XOB.reload("xospill"))
    XOB.spill("xospill")
    INNER = inn.reload("xnspill")
    emit_l0attn(p, INNER, XNBF, GB)
    XOB.reload("xospill")
    emit_stream_add(p, XOB.ap, INNER)
    inn.drop()


def _dram_specs(part):
    nm = dict(
        x=([D, T], F32, "ExternalInput"),
        out=([D, T], F32, "ExternalOutput"),
        xnspill=([128, NCH * T], F32, "Internal"),
        xospill=([128, NCH * T], F32, "Internal"),
        cwd=([16, T], F32, "Internal"),
        hd=([68, 128, T], BF16, "Internal"),
    )
    _dbg = os.environ.get("MK_DBG", "")
    if _dbg in ("1", "x1"):
        nm["dbg_x1"] = ([128, NCH * T], F32, "Internal")
    if _dbg in ("1", "xn2"):
        nm["dbg_xn2"] = ([128, NCH * T], F32, "Internal")
    if part == "b0":
        nm["x"] = ([D, T], F32, "ExternalInput")
    if part in ("all", "b0"):
        for l in range(6):
            nm.update({
                f"A{l}_wq": ([8, 128, 1024], F32, "ExternalInput"),
                f"A{l}_wk": ([8, 128, 1024], F32, "ExternalInput"),
                f"A{l}_wv": ([8, 128, 1024], F32, "ExternalInput"),
                f"A{l}_wo": ([8, 128, 1024], F32, "ExternalInput"),
                f"A{l}_bq": ([128, 8], F32, "ExternalInput"),
                f"A{l}_bk": ([128, 8], F32, "ExternalInput"),
                f"A{l}_bv": ([128, 8], F32, "ExternalInput"),
                f"A{l}_bo": ([128, 8], F32, "ExternalInput"),
                f"A{l}_g1": ([128, 8], F32, "ExternalInput"),
                f"A{l}_be1": ([128, 8], F32, "ExternalInput"),
                f"A{l}_g2": ([128, 8], F32, "ExternalInput"),
                f"A{l}_be2": ([128, 8], F32, "ExternalInput"),
                f"A{l}_w1": ([8, 128, 4096], F32, "ExternalInput"),
                f"A{l}_b1": ([128, 32], F32, "ExternalInput"),
                f"A{l}_w2": ([32, 128, 1024], F32, "ExternalInput"),
                f"A{l}_b2": ([128, 8], F32, "ExternalInput"),
            })
        nm.update({
            "A_gf": ([128, 8], F32, "ExternalInput"),
            "A_bf": ([128, 8], F32, "ExternalInput"),
            "L0_ffn_gate_w": ([8, 128, 16], F32, "ExternalInput"),
            "L0_ffn_gate_b": ([16, 1], F32, "ExternalInput"),
            "L0_ffn_shared_wg": ([8, 128, 512], F32, "ExternalInput"),
            "L0_ffn_shared_wu": ([8, 128, 512], F32, "ExternalInput"),
            "L0_ffn_shared_wd": ([4, 128, 1024], F32, "ExternalInput"),
            "L0_ffn_experts_wg": ([16, 8, 128, 512], F32, "ExternalInput"),
            "L0_ffn_experts_wu": ([16, 8, 128, 512], F32, "ExternalInput"),
            "L0_ffn_experts_wd": ([16, 4, 128, 1024], F32, "ExternalInput"),
        })
    if part in ("all", "b1"):
        nm.update({
            "L1_attn_Wq": ([8, 128, 1024], F32, "ExternalInput"),
            "L1_attn_Wk": ([8, 128, 1024], F32, "ExternalInput"),
            "L1_attn_Wv": ([8, 128, 1024], F32, "ExternalInput"),
            "L1_attn_Wo": ([8, 128, 1024], F32, "ExternalInput"),
            "L1_attn_bq": ([128, 8], F32, "ExternalInput"),
            "L1_attn_bk": ([128, 8], F32, "ExternalInput"),
            "L1_attn_bv": ([128, 8], F32, "ExternalInput"),
            "L1_attn_bo": ([128, 8], F32, "ExternalInput"),
            "L1_attn_slopes": ([16, 1], F32, "ExternalInput"),
            "L1_ffn_Wq": ([8, 128, 1024], F32, "ExternalInput"),
            "L1_ffn_Wk": ([8, 128, 1024], F32, "ExternalInput"),
            "L1_ffn_Wv": ([8, 128, 1024], F32, "ExternalInput"),
            "L1_ffn_Wo": ([8, 128, 1024], F32, "ExternalInput"),
            "L1_ffn_g1": ([128, 8], F32, "ExternalInput"),
            "L1_ffn_b1": ([128, 8], F32, "ExternalInput"),
            "L1_ffn_g2": ([128, 8], F32, "ExternalInput"),
            "L1_ffn_b2": ([128, 8], F32, "ExternalInput"),
            "L1_ffn_Wg": ([8, 128, 8], F32, "ExternalInput"),
            "L1_ffn_W1": ([8, 8, 128, 512], F32, "ExternalInput"),
            "L1_ffn_W2": ([8, 4, 128, 1024], F32, "ExternalInput"),
        })
    return nm


def build_full(part="all"):
    nc = bass.Bass("TRN2", target_bir_lowering=False)
    es = contextlib.ExitStack()
    with es:
        blk = es.enter_context(nc.Block())
        p = P(nc, es, blk)
        for k, (shp, dt, kind) in _dram_specs(part).items():
            p.dram[k] = nc.dram_tensor(k, shp, dt, kind=kind).ap()
        XNBF = p.sbuf("XNBF", [128, NCH * T], BF16)
        GB = p.sbuf("GB", [128, 2 * NCH], F32)
        p.init_arena(44050)
        XOB = StreamBuf(p, "XO")
        XOB.alloc()
        emit_consts(p)
        emit_stream_load(p, "x", XOB.ap)
        if part in ("all", "b0"):
            emit_block0(p, XOB, XNBF[:], GB)
        if part in ("all", "b1"):
            emit_block1(p, XOB, XNBF[:], GB)
        emit_stream_store(p, XOB.ap, "out", waits=p.snap())
        fin = p.ds.n

        @blk.sync
        def _(sync):
            sync.wait_ge(p.ds.h, fin)
    return nc


def _prep_weights(inputs):
    I = inputs
    w = {}
    for l in range(6):
        w[f"A{l}_wq"] = _wchunk(I["L0_attn_Wq"][l])
        w[f"A{l}_wk"] = _wchunk(I["L0_attn_Wk"][l])
        w[f"A{l}_wv"] = _wchunk(I["L0_attn_Wv"][l])
        w[f"A{l}_wo"] = _wchunk(I["L0_attn_Wo"][l])
        w[f"A{l}_bq"] = _colvec(I["L0_attn_bq"][l])
        w[f"A{l}_bk"] = _colvec(I["L0_attn_bk"][l])
        w[f"A{l}_bv"] = _colvec(I["L0_attn_bv"][l])
        w[f"A{l}_bo"] = _colvec(I["L0_attn_bo"][l])
        w[f"A{l}_g1"] = _colvec(I["L0_attn_g1"][l])
        w[f"A{l}_be1"] = _colvec(I["L0_attn_be1"][l])
        w[f"A{l}_g2"] = _colvec(I["L0_attn_g2"][l])
        w[f"A{l}_be2"] = _colvec(I["L0_attn_be2"][l])
        w[f"A{l}_w1"] = _wchunk(I["L0_attn_W1"][l])
        w[f"A{l}_b1"] = _colvec(I["L0_attn_b1"][l])
        w[f"A{l}_w2"] = _wchunk(I["L0_attn_W2"][l])
        w[f"A{l}_b2"] = _colvec(I["L0_attn_b2"][l])
    w["A_gf"] = _colvec(I["L0_attn_gf"])
    w["A_bf"] = _colvec(I["L0_attn_bf"])
    w["L0_ffn_gate_w"] = _wchunk(I["L0_ffn_gate_w"])
    w["L0_ffn_gate_b"] = np.ascontiguousarray(
        np.asarray(I["L0_ffn_gate_b"], np.float32)[:, None])
    w["L0_ffn_shared_wg"] = _wchunk(I["L0_ffn_shared_wg"])
    w["L0_ffn_shared_wu"] = _wchunk(I["L0_ffn_shared_wu"])
    w["L0_ffn_shared_wd"] = _wchunk(I["L0_ffn_shared_wd"])
    for nm2 in ("wg", "wu", "wd"):
        w[f"L0_ffn_experts_{nm2}"] = np.stack(
            [_wchunk(np.asarray(I[f"L0_ffn_experts_{nm2}"][e]))
             for e in range(16)])
    w["L1_attn_Wq"] = _wchunk(np.asarray(I["L1_attn_Wq"], np.float32).T)
    w["L1_attn_Wk"] = _wchunk(np.asarray(I["L1_attn_Wk"], np.float32).T)
    w["L1_attn_Wv"] = _wchunk(np.asarray(I["L1_attn_Wv"], np.float32).T)
    w["L1_attn_Wo"] = _wchunk(np.asarray(I["L1_attn_Wo"], np.float32).T)
    w["L1_attn_bq"] = _colvec(I["L1_attn_bq"])
    w["L1_attn_bk"] = _colvec(I["L1_attn_bk"])
    w["L1_attn_bv"] = _colvec(I["L1_attn_bv"])
    w["L1_attn_bo"] = _colvec(I["L1_attn_bo"])
    w["L1_attn_slopes"] = np.ascontiguousarray(
        np.asarray(I["L1_attn_slopes"], np.float32)[:, None])
    for nm2 in ("Wq", "Wk", "Wv", "Wo"):
        w[f"L1_ffn_{nm2}"] = _wchunk(I[f"L1_ffn_{nm2}"])
    w["L1_ffn_g1"] = _colvec(I["L1_ffn_g1"])
    w["L1_ffn_b1"] = _colvec(I["L1_ffn_b1"])
    w["L1_ffn_g2"] = _colvec(I["L1_ffn_g2"])
    w["L1_ffn_b2"] = _colvec(I["L1_ffn_b2"])
    w["L1_ffn_Wg"] = _wchunk(I["L1_ffn_Wg"])
    w["L1_ffn_W1"] = np.stack(
        [_wchunk(np.asarray(I["L1_ffn_W1"][e])) for e in range(8)])
    w["L1_ffn_W2"] = np.stack(
        [_wchunk(np.asarray(I["L1_ffn_W2"][e])) for e in range(8)])
    return w


def kernel(**inputs):
    from concourse.bass_utils import run_bass_kernel_spmd as _run
    w = _prep_weights(inputs)
    x = np.asarray(inputs["L0_attn_x"], np.float32)
    xcores = [np.ascontiguousarray(x[i].T) for i in range(8)]
    mode = int(os.environ.get("MK_LAUNCHES", "1"))
    if mode == 1:
        nc = build_full("all")
        in_maps = [dict(w, x=xcores[i]) for i in range(8)]
        res = _run(nc, in_maps, core_ids=list(range(8)))
        return np.stack([np.asarray(r["out"]).T for r in res.results])
    else:
        nc0 = build_full("b0")
        keys0 = {k for k, (s, d, kd) in _dram_specs("b0").items()
                 if kd == "ExternalInput"}
        in_maps = [{k: (xcores[i] if k == "x" else w[k]) for k in keys0}
                   for i in range(8)]
        res0 = _run(nc0, in_maps, core_ids=list(range(8)))
        mids = [np.asarray(r["out"]) for r in res0.results]
        nc1 = build_full("b1")
        keys1 = {k for k, (s, d, kd) in _dram_specs("b1").items()
                 if kd == "ExternalInput"}
        in_maps = [{k: (mids[i] if k == "x" else w[k]) for k in keys1}
                   for i in range(8)]
        res1 = _run(nc1, in_maps, core_ids=list(range(8)))
        return np.stack([np.asarray(r["out"]).T for r in res1.results])


# version 11 (final)
# speedup vs baseline: 1.2844x; 1.0132x over previous
import os
import contextlib
import numpy as np
import ml_dtypes
import concourse.bass as bass
import concourse.mybir as mybir
from concourse.bass_utils import run_bass_kernel_spmd

F32 = mybir.dt.float32
BF16 = mybir.dt.bfloat16
AF = mybir.ActivationFunctionType
ALU = mybir.AluOpType
AX = mybir.AxisListType

D = 1024
T = 2048
NCH = 8
NTG = 4
TG = 512
LN_EPS = 1e-5
ATT_EPS = 1e-6

BF = ml_dtypes.bfloat16



def _wchunk(w):
    w = np.asarray(w, np.float32)
    k, m = w.shape
    return np.ascontiguousarray(w.reshape(k // 128, 128, m))


def _colvec(b):
    b = np.asarray(b, np.float32)
    return np.ascontiguousarray(b.reshape(-1, 128).T)



class Sem:
    def __init__(self, es, nc, name):
        self.h = es.enter_context(nc.semaphore(name))
        self.n = 0

    def inc(self, instr, by=1):
        instr.then_inc(self.h, by)
        self.n += by
        return self.n


class P:

    def __init__(self, nc, es, blk):
        self.nc = nc
        self.es = es
        self.blk = blk
        self.pe = Sem(es, nc, "s_pe")
        self.dve = Sem(es, nc, "s_dve")
        self.act = Sem(es, nc, "s_act")
        self.pool = Sem(es, nc, "s_pool")
        self.ds = Sem(es, nc, "d_sync")
        self.dg = Sem(es, nc, "d_gps")
        self.da = Sem(es, nc, "d_act")
        self.all_sems = [self.pe, self.dve, self.act, self.pool,
                         self.ds, self.dg, self.da]
        self.C = {}
        self.dram = {}
        self._uid = 0

    def uid(self):
        self._uid += 1
        return self._uid

    def sbuf(self, name, shape, dtype, es=None):
        return (es or self.es).enter_context(
            self.nc.sbuf_tensor(name, shape, dtype))

    def init_arena(self, words):
        self._arena = self.es.enter_context(
            self.nc.sbuf_tensor("ARENA", [128, words], F32))
        self._free = [(0, words)]
        self._ar_words = words

    def _alloc(self, name, n):
        for i, (o, l) in enumerate(self._free):
            if l >= n:
                if l == n:
                    self._free.pop(i)
                else:
                    self._free[i] = (o + n, l - n)
                return o
        raise AssertionError(
            f"arena overflow: {name} needs {n}; free={self._free}")

    def _release(self, off, n):
        self._free.append((off, n))
        self._free.sort()
        merged = []
        for o, l in self._free:
            if merged and merged[-1][0] + merged[-1][1] == o:
                merged[-1] = (merged[-1][0], merged[-1][1] + l)
            else:
                merged.append((o, l))
        self._free = merged

    def tile(self, name, shape, dtype, es=None):
        assert shape[0] <= 128
        nbytes = shape[1] * mybir.dt.size(dtype)
        nwords = (nbytes + 3) // 4
        off = self._alloc(name, nwords)
        if es is not None:
            es.callback(lambda o=off, n=nwords: self._release(o, n))
        if dtype == F32:
            v = self._arena[0:shape[0], off:off + shape[1]]
        elif dtype == BF16:
            v = self._arena[:, :].bitcast(BF16)[0:shape[0],
                                                2 * off:2 * off + shape[1]]
        else:
            raise ValueError(dtype)
        return _TileView(v)


    def psum(self, name, shape=(128, 512), dtype=F32):
        return self.es.enter_context(
            self.nc.psum_tensor(name, list(shape), dtype))

    def snap(self):
        return [(s, s.n) for s in self.all_sems]

    def wait_snap(self, eng, snap):
        for s, n in snap:
            if n:
                eng.wait_ge(s.h, n)


class StreamBuf:

    def __init__(self, p, name):
        self.p = p
        self.name = name
        self.es = None
        self.ap = None

    def alloc(self):
        assert self.es is None
        self.es = contextlib.ExitStack()
        self.ap = self.p.tile(self.name, [128, NCH * T], F32, self.es)[:]
        return self.ap

    def drop(self):
        self.es.close()
        self.es = None
        self.ap = None

    def spill(self, dram_name):
        emit_spill(self.p, self.ap, dram_name)
        self.drop()

    def reload(self, dram_name):
        self.alloc()
        emit_spill(self.p, self.ap, dram_name, load=True)
        return self.ap


class _TileView:
    def __init__(self, ap):
        self._ap = ap

    def __getitem__(self, idx):
        return self._ap[idx]

    def ap(self):
        return self._ap


def emit_consts(p):
    blk = p.blk
    C = p.C
    C["ones"] = p.sbuf("c_ones", [128, 128], BF16)
    C["onesf"] = p.sbuf("c_onesf", [1, 128], F32)
    C["ident"] = p.sbuf("c_ident", [128, 128], BF16)
    C["trilx4"] = p.sbuf("c_trilx4", [128, 512], F32)
    C["identf"] = p.sbuf("c_identf", [128, 128], F32)
    C["identb64"] = p.sbuf("c_identb64", [128, 64], BF16)
    C["zeros"] = p.sbuf("c_zeros", [128, 1], F32)

    @blk.vector
    def _(v):
        v.memset(C["ones"][:], 1.0)
        v.memset(C["onesf"][:], 1.0)
        v.memset(C["zeros"][:], 0.0)
        instr = v.memset(C["trilx4"][:], 1.0)
        p.dve.inc(instr)

    dn = p.dve.n

    @blk.gpsimd
    def _(g):
        g.wait_ge(p.dve.h, dn)
        g.affine_select(C["ident"][:], C["ones"][:], [[-1, 128]],
                        ALU.is_equal, 0.0, channel_multiplier=1)
        instr = g.affine_select(C["identf"][:], C["trilx4"][:, 0:128],
                                [[-1, 128]], ALU.is_equal, 0.0,
                                channel_multiplier=1)
        p.pool.inc(instr)
        g.affine_select(C["identb64"][:], C["ones"][:, 0:64], [[-1, 64]],
                        ALU.is_equal, 0.0, channel_multiplier=1, base=-64)
        g.wait_ge(p.pool.h, p.pool.n)
        instr = g.affine_select(C["trilx4"][:], C["trilx4"][:],
                                [[0, 4], [-1, 128]], ALU.is_ge, 0.0,
                                channel_multiplier=1)
        p.pool.inc(instr)



class LnScratch:
    def __init__(self, p, tag="", es=None):
        self.xbf = p.tile(f"lnS_xbf{tag}", [128, 2 * TG], BF16, es)
        self.xsq = p.tile(f"lnS_xsq{tag}", [128, 2 * TG], BF16, es)
        self.R = p.tile(f"lnS_R{tag}", [1, 3 * TG], F32, es)
        self.AB = p.tile(f"lnS_AB{tag}", [1, 2 * TG], F32, es)
        self.HL = p.tile(f"lnS_HL{tag}", [1, 2 * T], BF16, es)
        self.LL = p.tile(f"lnS_LL{tag}", [1, 2 * T], BF16, es)
        self.t1 = p.tile(f"lnS_t1{tag}", [128, 2 * TG], F32, es)


def emit_ln(p, S, SRC, DST, DST_BF, gcol=None, becol=None):
    blk = p.blk
    C = p.C
    ds_entry = p.ds.n
    U = p.uid()
    with contextlib.ExitStack() as ps_es:
        ps_stat = [ps_es.enter_context(p.nc.psum_tensor(f"ps_lnst{i}_{U}", [128, 512], F32))
                   for i in range(8)]

        start = p.snap()
        dve0 = p.dve.n
        act0 = p.act.n
        pe0 = p.pe.n

        @blk.vector
        def _(v):
            p.wait_snap(v, start)
            for c in range(NCH):
                for tg in range(NTG):
                    idx = c * NTG + tg
                    if idx >= 2:
                        v.wait_ge(p.pe.h, pe0 + idx - 1)
                    instr = v.tensor_copy(
                        S.xbf[:, (idx % 2) * TG:(idx % 2) * TG + TG],
                        SRC[:, c * T + tg * TG:c * T + tg * TG + TG])
                    p.dve.inc(instr)

        @blk.scalar
        def _(s):
            p.wait_snap(s, start)
            for idx in range(NCH * NTG):
                s.wait_ge(p.dve.h, dve0 + idx + 1)
                instr = s.activation(
                    S.xsq[:, (idx % 2) * TG:(idx % 2) * TG + TG],
                    S.xbf[:, (idx % 2) * TG:(idx % 2) * TG + TG], AF.Square)
                p.act.inc(instr)

        @blk.tensor
        def _(t):
            for c in range(NCH):
                for tg in range(NTG):
                    idx = c * NTG + tg
                    t.wait_ge(p.act.h, act0 + idx + 1)
                    for si in range(2):
                        srcx = (S.xbf, S.xsq)[si]
                        instr = t.matmul(
                            ps_stat[si * 4 + tg][0:1, 0:TG],
                            C["ones"][:, 0:1],
                            srcx[:, (idx % 2) * TG:(idx % 2) * TG + TG],
                            start=(c == 0), stop=(c == NCH - 1))
                    p.pe.inc(instr)

        pe_stats = p.pe.n
        mu = S.R[0:1, 0:TG]
        var = S.R[0:1, TG:2 * TG]
        tmp = S.R[0:1, 2 * TG:3 * TG]
        arow = S.AB[0:1, 0:TG]
        brow = S.AB[0:1, TG:2 * TG]

        for tg in range(NTG):
            a_hi = S.HL[0:1, tg * TG:tg * TG + TG]
            b_hi = S.HL[0:1, T + tg * TG:T + tg * TG + TG]
            a_lo = S.LL[0:1, tg * TG:tg * TG + TG]
            b_lo = S.LL[0:1, T + tg * TG:T + tg * TG + TG]

            @blk.vector
            def _(v, tg=tg):
                v.wait_ge(p.pe.h, pe_stats)
                v.tensor_single_scalar(mu, ps_stat[tg][0:1, 0:TG],
                                       1.0 / D, ALU.mult)
                v.tensor_single_scalar(tmp, ps_stat[4 + tg][0:1, 0:TG],
                                       1.0 / D, ALU.mult)
                v.tensor_tensor(var, mu, mu, ALU.mult)
                v.scalar_tensor_tensor(var, var, -1.0, tmp, ALU.mult, ALU.add)
                instr = v.tensor_single_scalar(var, var, LN_EPS, ALU.add)
                p.dve.inc(instr)

            d_var = p.dve.n

            @blk.scalar
            def _(s):
                s.wait_ge(p.dve.h, d_var)
                s.activation(tmp, var, AF.Ln)
                instr = s.activation(arow, tmp, AF.Exp, scale=-0.5)
                p.act.inc(instr)

            a_rstd = p.act.n

            @blk.vector
            def _(v, a_hi=a_hi, b_hi=b_hi, a_lo=a_lo, b_lo=b_lo):
                v.wait_ge(p.act.h, a_rstd)
                v.scalar_tensor_tensor(brow, mu, -1.0, arow, ALU.mult, ALU.mult)
                v.tensor_copy(a_hi, arow)
                v.tensor_copy(b_hi, brow)
                v.tensor_copy(tmp, a_hi)
                v.tensor_tensor(a_lo, arow, tmp, ALU.subtract)
                v.tensor_copy(tmp, b_hi)
                instr = v.tensor_tensor(b_lo, brow, tmp, ALU.subtract)
                p.dve.inc(instr)

        d_rows = p.dve.n

    with contextlib.ExitStack() as ps_es:
        ps_ab = [ps_es.enter_context(p.nc.psum_tensor(f"ps_lnab{i}_{U}", [128, 512], F32))
                 for i in range(4)]
        act_marks = []
        dve_tg_marks = []
        for tg in range(NTG):
            a_hi = S.HL[0:1, tg * TG:tg * TG + TG]
            b_hi = S.HL[0:1, T + tg * TG:T + tg * TG + TG]
            a_lo = S.LL[0:1, tg * TG:tg * TG + TG]
            b_lo = S.LL[0:1, T + tg * TG:T + tg * TG + TG]
            bank_a = ps_ab[2 * (tg % 2)]
            bank_b = ps_ab[2 * (tg % 2) + 1]

            @blk.tensor
            def _(t, tg=tg, bank_a=bank_a, bank_b=bank_b,
                  a_hi=a_hi, b_hi=b_hi, a_lo=a_lo, b_lo=b_lo):
                t.wait_ge(p.dve.h, d_rows)
                if tg >= 2:
                    t.wait_ge(p.dve.h, dve_tg_marks[tg - 2])
                for bank, hi, lo in ((bank_a, a_hi, a_lo), (bank_b, b_hi, b_lo)):
                    for half, rt in ((0, hi), (1, lo)):
                        instr = t.matmul(bank[:, 0:TG], C["ones"][0:1, :], rt,
                                         start=(half == 0), stop=(half == 1))
                        if half == 1:
                            p.pe.inc(instr)

            pe_bc = p.pe.n

            for c in range(NCH):
                @blk.vector
                def _(v, c=c, tg=tg, bank_a=bank_a, bank_b=bank_b):
                    v.wait_ge(p.pe.h, pe_bc)
                    t1c = S.t1[:, (c % 2) * TG:(c % 2) * TG + TG]
                    if len(act_marks) >= 2:
                        v.wait_ge(p.act.h, act_marks[-2])
                    v.tensor_tensor(t1c, SRC[:, c * T + tg * TG:
                                             c * T + tg * TG + TG],
                                    bank_a[:, 0:TG], ALU.mult)
                    instr = v.tensor_tensor(t1c, t1c, bank_b[:, 0:TG], ALU.add)
                    p.dve.inc(instr)

                d_t = p.dve.n

                @blk.scalar
                def _(s, c=c, tg=tg):
                    s.wait_ge(p.dve.h, d_t)
                    if gcol is not None:
                        s.wait_ge(p.ds.h, ds_entry)
                    t1c = S.t1[:, (c % 2) * TG:(c % 2) * TG + TG]
                    dsl = slice(c * T + tg * TG, c * T + tg * TG + TG)
                    for dst in (DST, DST_BF):
                        if dst is None:
                            continue
                        if gcol is None:
                            instr = s.activation(dst[:, dsl], t1c, AF.Identity)
                        else:
                            instr = s.activation(dst[:, dsl], t1c, AF.Identity,
                                                 bias=becol[:, c:c + 1],
                                                 scale=gcol[:, c:c + 1])
                        p.act.inc(instr)
                act_marks.append(p.act.n)
            dve_tg_marks.append(p.dve.n)

    return dict(act_done=p.act.n, dve_done=p.dve.n)



def emit_stream_load(p, dram_name, DSTAP, f32=True):
    blk = p.blk
    x = p.dram[dram_name]

    @blk.sync
    def _(sync):
        src = x.rearrange("(c p) t -> p c t", p=128)
        dst = DSTAP.rearrange("p (c t) -> p c t", t=T)
        instr = sync.dma_start(out=dst, in_=src)
        p.ds.inc(instr, 16)
    return p.ds.n


def emit_stream_store(p, SRCAP, dram_name, waits=None):
    blk = p.blk
    y = p.dram[dram_name]

    @blk.sync
    def _(sync):
        if waits:
            p.wait_snap(sync, waits)
        dst = y.rearrange("(c p) t -> p c t", p=128)
        srcv = SRCAP.rearrange("p (c t) -> p c t", t=T)
        instr = sync.dma_start(out=dst, in_=srcv)
        p.ds.inc(instr, 16)
    return p.ds.n



def emit_castload(p, eng_sections_done, pairs):
    blk = p.blk

    @blk.gpsimd
    def _(g):
        for dst, srcap in pairs:
            instr = g.dma_start(out=dst, in_=srcap)
            p.dg.inc(instr, 16)
    return p.dg.n



def emit_l0ffn(p, XN, XNBF, xo_get, pre_e=None, pre_d=None):
    blk = p.blk
    C = p.C
    dr = p.dram
    NE = 16
    NKC = 68

    identf = C["identf"]
    zeros = C["zeros"]
    es_r = contextlib.ExitStack()
    gate_w = p.tile("ff_gatew", [128, 8 * 16], F32, es_r)
    gate_b = p.tile("ff_gateb", [16, 1], F32, es_r)
    r_fm = p.tile("ff_rfm", [16, T], F32, es_r)
    r_tm = p.tile("ff_rtm", [128, 16 * 16], F32, es_r)
    M1 = p.tile("ff_m1", [128, 16], F32, es_r)
    M2 = p.tile("ff_m2", [128, 16], F32, es_r)
    WR = p.tile("ff_wr", [128, 16], F32, es_r)
    I2 = p.tile("ff_i2", [128, 16], F32, es_r)
    TMPB = p.tile("ff_tmpb", [128, 16], F32, es_r)
    CW = p.tile("ff_cw", [128, 16 * 16], F32, es_r)
    CWF = p.tile("ff_cwf", [16, T], F32, es_r)

    start = p.snap()

    @blk.sync
    def _(sync):
        p.wait_snap(sync, start)
        instr = sync.dma_start(
            out=gate_w[:].rearrange("p (k m) -> p k m", m=16),
            in_=dr["L0_ffn_gate_w"].rearrange("k p m -> p k m"))
        p.ds.inc(instr, 16)
        instr = sync.dma_start(out=gate_b[:], in_=dr["L0_ffn_gate_b"])
        p.ds.inc(instr, 16)

    ld_gw = ld_gb = p.ds.n

    with contextlib.ExitStack() as pse:
        ps_r = [pse.enter_context(p.nc.psum_tensor(f"ps_ffr{i}", [128, 512], F32))
                for i in range(2)]
        ps_t = pse.enter_context(p.nc.psum_tensor("ps_fft", [128, 512], F32))
        ps_t2 = pse.enter_context(p.nc.psum_tensor("ps_fft2", [128, 512], F32))

        act_r0 = p.act.n

        @blk.tensor
        def _(t):
            t.wait_ge(p.ds.h, ld_gw)
            p.wait_snap(t, start)
            for tg in range(NTG):
                if tg >= 2:
                    t.wait_ge(p.act.h, act_r0 + tg - 1)
                for k in range(NCH):
                    instr = t.matmul(
                        ps_r[tg % 2][0:16, 0:TG], gate_w[:, k * 16:k * 16 + 16],
                        XN[:, k * T + tg * TG:k * T + tg * TG + TG],
                        start=(k == 0), stop=(k == NCH - 1))
                p.pe.inc(instr)

        pe_r0 = p.pe.n - NTG

        @blk.scalar
        def _(s):
            s.wait_ge(p.ds.h, ld_gb)
            for tg in range(NTG):
                s.wait_ge(p.pe.h, pe_r0 + tg + 1)
                instr = s.activation(r_fm[:, tg * TG:tg * TG + TG],
                                     ps_r[tg % 2][0:16, 0:TG], AF.Sigmoid,
                                     bias=gate_b[:])
                p.act.inc(instr)

        act_r = p.act.n
        assert act_r == act_r0 + NTG

        d_id = p.dve.n

        @blk.tensor
        def _(t):
            t.wait_ge(p.act.h, act_r)
            t.wait_ge(p.dve.h, d_id)
            for tc in range(16):
                instr = t.matmul(ps_t[:, tc * 16:tc * 16 + 16],
                                 r_fm[:, tc * 128:tc * 128 + 128],
                                 identf[0:16, 0:16], is_transpose=True,
                                 start=True, stop=True)
            p.pe.inc(instr)

        pe_tr = p.pe.n

        @blk.vector
        def _(v):
            v.wait_ge(p.pe.h, pe_tr)
            instr = v.tensor_copy(r_tm[:], ps_t[:, 0:256])
            p.dve.inc(instr)

        d_rtm = p.dve.n

        if "dbg_rtm" in dr:
            @blk.sync
            def _(sync):
                sync.wait_ge(p.dve.h, d_rtm)
                p.ds.inc(sync.dma_start(out=dr["dbg_rfm"], in_=r_fm[:]), 16)
                p.ds.inc(sync.dma_start(out=dr["dbg_rtm"], in_=r_tm[:]), 16)

        @blk.vector
        def _(v):
            v.wait_ge(p.dve.h, d_rtm)
            zb = zeros[:, 0:1].broadcast_to([128, 16])

            def S(instr):
                p.dve.inc(instr)
                v.wait_ge(p.dve.h, p.dve.n)
            for tc in range(16):
                blkr = r_tm[:, tc * 16:tc * 16 + 16]
                cwb = CW[:, tc * 16:tc * 16 + 16]
                m1 = M1[:, tc:tc + 1]
                m2 = M2[:, tc:tc + 1]
                S(v.reduce_max(m1, blkr, AX.X))
                S(v.scalar_tensor_tensor(cwb, blkr, m1, zb, ALU.subtract, ALU.is_ge))
                S(v.scalar_tensor_tensor(I2[:], cwb, -1e30, blkr, ALU.mult, ALU.add))
                S(v.reduce_max(m2, I2[:], AX.X))
                S(v.tensor_tensor(WR[:, tc:tc + 1], m1, m2, ALU.add))
                S(v.scalar_tensor_tensor(I2[:], I2[:], m2, zb, ALU.subtract, ALU.is_ge))
                S(v.scalar_tensor_tensor(TMPB[:], I2[:], m2, zb, ALU.mult, ALU.add))
                S(v.scalar_tensor_tensor(cwb, cwb, m1, TMPB[:], ALU.mult, ALU.add))
            S(v.reciprocal(WR[:], WR[:]))
            for tc in range(16):
                S(v.scalar_tensor_tensor(
                    CW[:, tc * 16:tc * 16 + 16], CW[:, tc * 16:tc * 16 + 16],
                    WR[:, tc:tc + 1], zeros[:, 0:1].broadcast_to([128, 16]),
                    ALU.mult, ALU.add))

        d_cw = p.dve.n

        if "dbg_cw" in dr:
            @blk.sync
            def _(sync):
                sync.wait_ge(p.dve.h, d_cw)
                p.ds.inc(sync.dma_start(out=dr["dbg_cw"], in_=CW[:]), 16)

        d_cwf0 = p.dve.n

        @blk.tensor
        def _(t):
            t.wait_ge(p.dve.h, d_cw)
            for tc in range(16):
                bank = (ps_t, ps_t2)[tc % 2]
                if tc >= 2:
                    t.wait_ge(p.dve.h, d_cwf0 + tc - 1)
                instr = t.matmul(bank[0:16, 0:128],
                                 CW[:, tc * 16:tc * 16 + 16], identf[:],
                                 is_transpose=True, start=True, stop=True)
                p.pe.inc(instr)

        pe_cwt0 = p.pe.n - 16

        @blk.vector
        def _(v):
            for tc in range(16):
                v.wait_ge(p.pe.h, pe_cwt0 + tc + 1)
                bank = (ps_t, ps_t2)[tc % 2]
                instr = v.tensor_copy(CWF[:, tc * 128:tc * 128 + 128],
                                      bank[0:16, 0:128])
                p.dve.inc(instr)

        d_cwf = p.dve.n
        assert d_cwf == d_cwf0 + 16

    @blk.sync
    def _(sync):
        sync.wait_ge(p.dve.h, d_cwf)
        instr = sync.dma_start(out=dr["cwd"], in_=CWF[:])
        p.ds.inc(instr, 16)

    ds_cwd = p.ds.n
    es_r.close()
    if pre_e is not None:
        pre_e()
    es_e = contextlib.ExitStack()
    CWB = p.tile("ff_cwb", [128, 2 * T], F32, es_e)
    WGU = p.tile("ff_wgu", [128, 2 * 8 * 512], BF16, es_e)
    WGU2 = p.tile("ff_wgu2", [128, 2 * 8 * 512], BF16, es_e)
    SIL = p.tile("ff_sil", [128, 4 * TG], F32, es_e)
    HP = p.tile("ff_hp", [128, 2 * 4 * TG], BF16, es_e)

    def wgu_src(e):
        if e == 0:
            return [dr["L0_ffn_shared_wg"].rearrange("k p m -> p k m"),
                    dr["L0_ffn_shared_wu"].rearrange("k p m -> p k m")]
        return [dr["L0_ffn_experts_wg"][e - 1].rearrange("k p m -> p k m"),
                dr["L0_ffn_experts_wu"][e - 1].rearrange("k p m -> p k m")]

    fence_e = p.snap()

    @blk.gpsimd
    def _(g):
        p.wait_snap(g, fence_e)
        g.wait_ge(p.ds.h, ds_cwd)
        for i, srcap in enumerate(wgu_src(0)):
            instr = g.dma_start(
                out=WGU[:, i * 4096:i * 4096 + 4096].rearrange(
                    "p (k m) -> p k m", m=512),
                in_=srcap)
            p.dg.inc(instr, 16)

    @blk.sync
    def _(sync):
        p.wait_snap(sync, fence_e)
        instr = sync.dma_start(out=CWB[:, 0:T],
                               in_=dr["cwd"][0:1, :].broadcast_to([128, T]))
        p.ds.inc(instr, 16)

    hd_marks = []
    with contextlib.ExitStack() as pse:
        ps_h = [pse.enter_context(p.nc.psum_tensor(f"ps_ffh{i}", [128, 512], F32))
                for i in range(8)]

        for e in range(17):
            ld_done = p.dg.n
            cw_done = p.ds.n
            wbuf = (WGU, WGU2)[e % 2]
            nbuf = (WGU, WGU2)[(e + 1) % 2]

            if e < 16:
                @blk.gpsimd
                def _(g, e=e, nbuf=nbuf):
                    g.wait_ge(p.pe.h, p.pe.n)
                    for i, srcap in enumerate(wgu_src(e + 1)):
                        instr = g.dma_start(
                            out=nbuf[:, i * 4096:i * 4096 + 4096].rearrange(
                                "p (k m) -> p k m", m=512),
                            in_=srcap)
                        p.dg.inc(instr, 16)

                @blk.sync
                def _(sync, e=e, dve_now=None):
                    sync.wait_ge(p.dve.h, p.dve.n)
                    instr = sync.dma_start(
                        out=CWB[:, (e % 2) * T:(e % 2) * T + T],
                        in_=dr["cwd"][e:e + 1, :].broadcast_to([128, T]))
                    p.ds.inc(instr, 16)

            for tg in range(NTG):
                d_hp0 = p.dve.n

                @blk.tensor
                def _(t, e=e, tg=tg, wbuf=wbuf, ld_done=ld_done, d_hp0=d_hp0):
                    t.wait_ge(p.dg.h, ld_done)
                    t.wait_ge(p.dve.h, d_hp0)
                    t.wait_ge(p.act.h, p.act.n)
                    for half in range(2):
                        for m in range(4):
                            for k in range(NCH):
                                instr = t.matmul(
                                    ps_h[half * 4 + m][:, 0:TG],
                                    wbuf[:, half * 4096 + k * 512 + m * 128:
                                         half * 4096 + k * 512 + m * 128 + 128],
                                    XNBF[:, k * T + tg * TG:k * T + tg * TG + TG],
                                    start=(k == 0), stop=(k == NCH - 1))
                            if half == 1 and m == 3:
                                p.pe.inc(instr)

                pe_h = p.pe.n

                for m in range(4):
                    @blk.scalar
                    def _(s, m=m, pe_h=pe_h):
                        s.wait_ge(p.pe.h, pe_h)
                        instr = s.activation(SIL[:, m * TG:m * TG + TG],
                                             ps_h[m][:, 0:TG], AF.Silu)
                        p.act.inc(instr)

                    a_sil = p.act.n

                    @blk.vector
                    def _(v, e=e, tg=tg, m=m, a_sil=a_sil, cw_done=cw_done):
                        v.wait_ge(p.act.h, a_sil)
                        if m == 0 and len(hd_marks) >= 2:
                            v.wait_ge(p.ds.h, hd_marks[-2])
                        silm = SIL[:, m * TG:m * TG + TG]
                        hp = HP[:, (tg % 2) * 2048 + m * TG:
                                (tg % 2) * 2048 + m * TG + TG]
                        if e == 0:
                            instr = v.tensor_tensor(hp, silm, ps_h[4 + m][:, 0:TG],
                                                    ALU.mult)
                        else:
                            v.wait_ge(p.ds.h, cw_done)
                            v.tensor_tensor(silm, silm, ps_h[4 + m][:, 0:TG],
                                            ALU.mult)
                            instr = v.tensor_tensor(
                                hp, silm,
                                CWB[:, ((e - 1) % 2) * T + tg * TG:
                                    ((e - 1) % 2) * T + tg * TG + TG], ALU.mult)
                        p.dve.inc(instr)

                d_hp = p.dve.n

                @blk.sync
                def _(sync, e=e, tg=tg, d_hp=d_hp):
                    sync.wait_ge(p.dve.h, d_hp)
                    instr = sync.dma_start(
                        out=dr["hd"][4 * e:4 * e + 4, :, tg * TG:tg * TG + TG]
                        .rearrange("m p t -> p m t"),
                        in_=HP[:, (tg % 2) * 2048:(tg % 2) * 2048 + 2048]
                        .rearrange("p (m t) -> p m t", m=4))
                    p.ds.inc(instr, 16)
                hd_marks.append(p.ds.n)

    hd_done = p.ds.n
    es_e.close()
    if pre_d is not None:
        pre_d()

    def wd_src(kc, mh):
        e, sub = kc // 4, kc % 4
        base = (dr["L0_ffn_shared_wd"] if e == 0
                else dr["L0_ffn_experts_wd"][e - 1])
        return base[sub, :, mh * 512:mh * 512 + 512]

    es_d = contextlib.ExitStack()
    HB = p.tile("ff_hb", [128, 8 * TG], BF16, es_d)
    WD = p.tile("ff_wd", [128, NKC * 512], BF16, es_d)
    with contextlib.ExitStack() as pse:
        ps_y = [pse.enter_context(p.nc.psum_tensor(f"ps_ffy{i}", [128, 512], F32))
                for i in range(4)]
        for mh in range(2):
            @blk.gpsimd
            def _(g, mh=mh):
                g.wait_ge(p.pe.h, p.pe.n)
                g.wait_ge(p.dve.h, p.dve.n)
                g.wait_ge(p.act.h, p.act.n)
                g.wait_ge(p.ds.h, hd_done)
                for kc in range(NKC):
                    instr = g.dma_start(out=WD[:, kc * 512:kc * 512 + 512],
                                        in_=wd_src(kc, mh))
                    p.dg.inc(instr, 16)

            wd_ld = p.dg.n
            for tg in range(NTG):
                ds_h0 = p.ds.n

                pe_y0 = p.pe.n

                @blk.sync
                def _(sync, tg=tg, ds_h0=ds_h0, mh=mh, pe_y0=pe_y0):
                    sync.wait_ge(p.ds.h, hd_done)
                    sync.wait_ge(p.dve.h, p.dve.n)
                    for kc in range(NKC):
                        if kc >= 8:
                            sync.wait_ge(p.pe.h, pe_y0 + kc // 4 - 1)
                        instr = sync.dma_start(
                            out=HB[:, (kc % 8) * TG:(kc % 8) * TG + TG],
                            in_=dr["hd"][kc, :, tg * TG:tg * TG + TG])
                        p.ds.inc(instr, 16)

                @blk.tensor
                def _(t, tg=tg, mh=mh, wd_ld=wd_ld, ds_h0=ds_h0, pe_y0=pe_y0,
                      d_prev=None):
                    t.wait_ge(p.dg.h, wd_ld)
                    t.wait_ge(p.dve.h, p.dve.n)
                    for kc in range(NKC):
                        t.wait_ge(p.ds.h, ds_h0 + 16 * (kc + 1))
                        for m in range(4):
                            instr = t.matmul(
                                ps_y[m][:, 0:TG],
                                WD[:, kc * 512 + m * 128:kc * 512 + m * 128 + 128],
                                HB[:, (kc % 8) * TG:(kc % 8) * TG + TG],
                                start=(kc == 0), stop=(kc == NKC - 1))
                        if kc % 4 == 3 or kc == NKC - 1:
                            p.pe.inc(instr)

                pe_y = p.pe.n

                @blk.vector
                def _(v, tg=tg, mh=mh, pe_y=pe_y):
                    v.wait_ge(p.pe.h, pe_y)
                    XOUT = xo_get()
                    for m in range(4):
                        mc = mh * 4 + m
                        sl = slice(mc * T + tg * TG, mc * T + tg * TG + TG)
                        instr = v.tensor_tensor(XOUT[:, sl], XOUT[:, sl],
                                                ps_y[m][:, 0:TG], ALU.add)
                        p.dve.inc(instr)

    es_d.close()
    return dict(done=True)



def emit_l0attn_layer(p, INNER, XBF, lname, S, GBl, phases=None):
    PH = set(phases) if phases else {"scan", "oproj", "ln1", "ffn1",
                                     "ffn2", "ln2"}
    if "scan" in PH:
        PH |= {"qkv", "chunks", "c_at", "c_num", "c_den", "c_afm"}
    if "chunks" in PH:
        pass
    blk = p.blk
    C = p.C
    dr = p.dram
    U = p.uid()
    es_a = contextlib.ExitStack()
    PHI_Q = p.tile(f"la_q{U}", [128, 2 * T], BF16, es_a)
    PHI_K = p.tile(f"la_k{U}", [128, 2 * T], BF16, es_a)
    VV = p.tile(f"la_v{U}", [128, 2 * T], BF16, es_a)
    ATT = p.tile(f"la_att{U}", [128, NCH * T], BF16, es_a)
    WQKV = p.tile(f"la_wqkv{U}", [128, 2 * 3 * 1024], BF16, es_a)
    WO = p.tile(f"la_wo{U}", [128, 2 * 1024], BF16, es_a)
    BIAS = p.tile(f"la_bias{U}", [128, 4 * NCH], F32, es_a)
    KV_S = p.tile(f"la_kvs{U}", [128, 65], F32, es_a)
    KV_B = p.tile(f"la_kvb{U}", [128, 65], BF16, es_a)
    VA = p.tile(f"la_va{U}", [128, 130], BF16, es_a)
    KTM = p.tile(f"la_ktm{U}", [128, 128], BF16, es_a)
    AM = p.tile(f"la_am{U}", [128, 256], BF16, es_a)
    ATMP = p.tile(f"la_atmp{U}", [128, 128], BF16, es_a)
    DH_ = p.tile(f"la_den{U}", [128, 8], F32, es_a)
    T0 = WO[:, 0:TG]
    T1 = WO[:, TG:2 * TG]
    T2 = WO[:, 2 * TG:3 * TG]

    start = p.snap()
    zb = C["zeros"][:, 0:1]

    @blk.sync
    def _(sync):
        p.wait_snap(sync, start)
        for i, nmb in enumerate(("bq", "bk", "bv", "bo")):
            instr = sync.dma_start(out=BIAS[:, i * NCH:(i + 1) * NCH],
                                   in_=dr[f"{lname}_{nmb}"])
            p.ds.inc(instr, 16)

    ld_bias = p.ds.n

    @blk.vector
    def _(v):
        p.wait_snap(v, start)
        v.memset(VA[:, 64:65], 1.0)
        v.memset(VA[:, 129:130], 1.0)
        v.memset(KV_S[:], 0.0)
        instr = v.memset(KV_B[:], 0.0)
        p.dve.inc(instr)

    d_init = p.dve.n

    with contextlib.ExitStack() as pse:
        ps_a = [pse.enter_context(p.nc.psum_tensor(f"ps_laa{i}_{U}",
                                                   [128, 512], F32))
                for i in range(2)]
        ps_tr = pse.enter_context(p.nc.psum_tensor(f"ps_latr{U}",
                                                   [128, 512], F32))
        ps_at = pse.enter_context(p.nc.psum_tensor(f"ps_laat{U}",
                                                   [128, 512], F32))
        ps_num = pse.enter_context(p.nc.psum_tensor(f"ps_lanum{U}",
                                                    [128, 512], F32))
        ps_kvu = pse.enter_context(p.nc.psum_tensor(f"ps_lakvu{U}",
                                                    [128, 512], F32))
        ps_at2 = pse.enter_context(p.nc.psum_tensor(f"ps_laat2{U}",
                                                    [128, 512], F32))

        qkv_pe_prev = [p.pe.n]
        qkv_d_prev = [p.dve.n]
        qkv_a_prev = [p.act.n]
        t_aer = [p.act.n]
        t_mrg = [p.dve.n]
        for pp in (range(8) if ("scan" in PH or "qkv" in PH) else []):
            ld0 = p.dg.n

            @blk.gpsimd
            def _(g, pp=pp, pe_prev=qkv_pe_prev[0]):
                if pp == 0:
                    p.wait_snap(g, start)
                g.wait_ge(p.pe.h, pe_prev)
                for i, wn in enumerate(("wq", "wk", "wv")):
                    instr = g.dma_start(
                        out=WQKV[:, (pp % 2) * 3072 + i * 1024:
                               (pp % 2) * 3072 + (i + 1) * 1024]
                        .rearrange("p (k m) -> p k m", m=128),
                        in_=dr[f"{lname}_{wn}"].rearrange("k p m -> p k m")
                        [:, :, pp * 128:(pp + 1) * 128])
                    p.dg.inc(instr, 16)

            ld_w = p.dg.n
            d_ev0 = qkv_d_prev[0]
            a_ev0 = qkv_a_prev[0]
            pe_q0 = p.pe.n

            evac_marks = []
            for nn in range(12):
                i, tg = nn // NTG, nn % NTG
                tsl = slice((pp % 2) * T + tg * TG,
                            (pp % 2) * T + tg * TG + TG)

                @blk.tensor
                def _(t, pp=pp, i=i, tg=tg, nn=nn, ld_w=ld_w, d_ev0=d_ev0,
                      a_ev0=a_ev0):
                    if nn == 0:
                        t.wait_ge(p.dg.h, ld_w)
                        t.wait_ge(p.dve.h, d_ev0)
                        t.wait_ge(p.act.h, a_ev0)
                    if nn >= 2:
                        t.wait_ge(p.dve.h, evac_marks[nn - 2])
                    for k in range(NCH):
                        instr = t.matmul(
                            ps_a[nn % 2][:, 0:TG],
                            WQKV[:, (pp % 2) * 3072 + i * 1024 + k * 128:
                                 (pp % 2) * 3072 + i * 1024 + k * 128 + 128],
                            XBF[:, k * T + tg * TG:k * T + tg * TG + TG],
                            start=(k == 0), stop=(k == NCH - 1))
                    p.pe.inc(instr)

                pe_mm = p.pe.n
                if i < 2:
                    dst = (PHI_Q, PHI_K)[i]

                    @blk.vector
                    def _(v, pp=pp, i=i, nn=nn, pe_mm=pe_mm,
                          aer=t_aer[0]):
                        v.wait_ge(p.pe.h, pe_mm)
                        v.wait_ge(p.ds.h, ld_bias)
                        v.wait_ge(p.act.h, aer)
                        instr = v.scalar_tensor_tensor(
                            T0, ps_a[nn % 2][:, 0:TG],
                            BIAS[:, i * NCH + pp:i * NCH + pp + 1],
                            zb.broadcast_to([128, TG]), ALU.add, ALU.min)
                        p.dve.inc(instr)

                    d_t0 = p.dve.n

                    @blk.scalar
                    def _(s, nn=nn, d_t0=d_t0, pp=pp, i=i, mrg=t_mrg[0]):
                        s.wait_ge(p.dve.h, d_t0)
                        s.wait_ge(p.dve.h, mrg)
                        s.activation(T1, T0, AF.Exp)
                        instr = s.activation(
                            T2, ps_a[nn % 2][:, 0:TG], AF.Relu,
                            bias=BIAS[:, i * NCH + pp:i * NCH + pp + 1])
                        p.act.inc(instr)

                    a_er = p.act.n
                    t_aer[0] = a_er

                    @blk.vector
                    def _(v, dst=dst, tsl=tsl, nn=nn, a_er=a_er):
                        v.wait_ge(p.act.h, a_er)
                        instr = v.tensor_tensor(
                            dst[:, tsl], T2, T1, ALU.add)
                        p.dve.inc(instr)

                    t_mrg[0] = p.dve.n
                else:
                    @blk.vector
                    def _(v, pp=pp, nn=nn, tsl=tsl, pe_mm=pe_mm):
                        v.wait_ge(p.pe.h, pe_mm)
                        v.wait_ge(p.ds.h, ld_bias)
                        instr = v.scalar_tensor_tensor(
                            VV[:, tsl], ps_a[nn % 2][:, 0:TG],
                            BIAS[:, 2 * NCH + pp:2 * NCH + pp + 1],
                            zb.broadcast_to([128, TG]), ALU.add, ALU.add)
                        p.dve.inc(instr)
                evac_marks.append(p.dve.n)

            d_qkv = p.dve.n
            qkv_pe_prev[0] = p.pe.n
            qkv_d_prev[0] = p.dve.n
            qkv_a_prev[0] = p.act.n

            @blk.vector
            def _(v):
                v.wait_ge(p.pe.h, p.pe.n)
                v.memset(KV_S[:], 0.0)
                instr = v.memset(KV_B[:], 0.0)
                p.dve.inc(instr)

            for ni, n in enumerate(reversed(range(16)) if "chunks" in PH
                                   else []):
                csl = slice((pp % 2) * T + n * 128,
                            (pp % 2) * T + n * 128 + 128)
                d_pre = p.dve.n

                @blk.tensor
                def _(t, pp=pp, n=n, csl=csl, d_pre=d_pre):
                    t.wait_ge(p.dve.h, d_pre)
                    trv = ps_tr[:, 0:512].bitcast(BF16)
                    t.matmul(trv[:, 0:128], PHI_K[:, csl], C["ident"][:],
                             is_transpose=True, start=True, stop=True)
                    instr = t.matmul(trv[:, 128:256], VV[:, csl],
                                     C["ident"][:], is_transpose=True,
                                     start=True, stop=True)
                    p.pe.inc(instr)

                pe_tr = p.pe.n

                @blk.vector
                def _(v, pe_tr=pe_tr):
                    v.wait_ge(p.pe.h, pe_tr)
                    trv = ps_tr[:, 0:512].bitcast(BF16)
                    v.tensor_copy(KTM[:, 0:128], trv[:, 0:128])
                    v.tensor_copy(VA[:, 0:64], trv[:, 128:192])
                    instr = v.tensor_copy(VA[:, 65:129], trv[:, 192:256])
                    p.dve.inc(instr)

                d_tr = p.dve.n
                if "c_at" not in PH:
                    continue

                @blk.tensor
                def _(t, pp=pp, csl=csl, d_tr=d_tr):
                    t.wait_ge(p.dve.h, d_tr)
                    t.matmul(ps_at[:, 0:128], PHI_K[0:64, csl],
                             PHI_Q[0:64, csl], start=True, stop=True)
                    instr = t.matmul(ps_at2[:, 0:128], PHI_K[64:128, csl],
                                     PHI_Q[64:128, csl], start=True, stop=True)
                    p.pe.inc(instr)

                pe_at = p.pe.n

                @blk.vector
                def _(v, pe_at=pe_at):
                    v.wait_ge(p.pe.h, pe_at)
                    v.tensor_tensor(AM[:, 0:128], ps_at[:, 0:128],
                                    C["trilx4"][:, 0:128], ALU.mult)
                    instr = v.tensor_tensor(AM[:, 128:256], ps_at2[:, 0:128],
                                            C["trilx4"][:, 128:256], ALU.mult)
                    p.dve.inc(instr)

                d_am = p.dve.n
                if "c_num" not in PH:
                    continue

                @blk.tensor
                def _(t, pp=pp, csl=csl, d_am=d_am):
                    t.wait_ge(p.dve.h, d_am)
                    for par in range(2):
                        rows = slice(0, 64) if par == 0 else slice(64, 128)
                        t.matmul(ps_num[:, par * 65:par * 65 + 65],
                                 AM[:, par * 128:par * 128 + 128],
                                 VA[:, par * 65:par * 65 + 65],
                                 start=True, stop=False)
                        instr = t.matmul(ps_num[:, par * 65:par * 65 + 65],
                                         PHI_Q[rows, csl], KV_B[rows, 0:65],
                                         start=False, stop=True)
                    for par in range(2):
                        rows = slice(0, 64) if par == 0 else slice(64, 128)
                        instr2 = t.matmul(ps_kvu[rows, 0:65],
                                          KTM[:, par * 64:par * 64 + 64],
                                          VA[:, par * 65:par * 65 + 65],
                                          start=True, stop=True)
                    p.pe.inc(instr)
                    p.pe.inc(instr2)

                pe_num = p.pe.n
                if "c_den" not in PH:
                    continue

                @blk.vector
                def _(v, pe_num=pe_num):
                    v.wait_ge(p.pe.h, pe_num)
                    v.tensor_single_scalar(DH_[:, 0:1], ps_num[:, 64:65],
                                           ATT_EPS, ALU.add)
                    instr = v.tensor_single_scalar(DH_[:, 1:2],
                                                   ps_num[:, 129:130],
                                                   ATT_EPS, ALU.add)
                    p.dve.inc(instr)
                    v.wait_ge(p.dve.h, p.dve.n)
                    p.dve.inc(v.reciprocal(DH_[:, 2:4], DH_[:, 0:2]))
                    v.wait_ge(p.dve.h, p.dve.n)
                    for par in range(2):
                        v.scalar_tensor_tensor(
                            ATMP[:, par * 64:par * 64 + 64],
                            ps_num[:, par * 65:par * 65 + 64],
                            DH_[:, 2 + par:3 + par],
                            C["zeros"][:, 0:1].broadcast_to([128, 64]),
                            ALU.mult, ALU.add)
                    v.tensor_tensor(KV_S[:, 0:65], KV_S[:, 0:65],
                                    ps_kvu[:, 0:65], ALU.add)
                    instr = v.tensor_copy(KV_B[:, 0:65], KV_S[:, 0:65])
                    p.dve.inc(instr)

                d_scan = p.dve.n
                if "c_afm" not in PH:
                    continue

                @blk.tensor
                def _(t, pp=pp, n=n, d_scan=d_scan):
                    t.wait_ge(p.dve.h, d_scan)
                    instr = t.matmul(
                        ps_tr[:, 0:512].bitcast(BF16)[:, 256:384],
                        ATMP[:, 0:128], C["ident"][:, 0:128],
                        is_transpose=True, start=True, stop=True)
                    p.pe.inc(instr)

                pe_afm = p.pe.n

                @blk.vector
                def _(v, pp=pp, n=n, pe_afm=pe_afm):
                    v.wait_ge(p.pe.h, pe_afm)
                    instr = v.tensor_copy(
                        ATT[:, pp * T + n * 128:pp * T + n * 128 + 128],
                        ps_tr[:, 0:512].bitcast(BF16)[:, 256:384])
                    p.dve.inc(instr)

        if "oproj" not in PH:
            es_a.close()
            return
        d_att = p.dve.n
        if "dbg_latt" in dr:
            @blk.sync
            def _(sync):
                sync.wait_ge(p.dve.h, d_att)
                p.ds.inc(sync.dma_start(out=dr["dbg_latt"], in_=ATT[:]), 16)
        pe_o0 = p.pe.n
        ld_wo0 = p.dg.n

        @blk.gpsimd
        def _(g):
            g.wait_ge(p.dve.h, d_att)
            for m in range(NCH):
                if m >= 2:
                    g.wait_ge(p.pe.h, pe_o0 + 4 * (m - 1))
                instr = g.dma_start(
                    out=WO[:, (m % 2) * 1024:(m % 2) * 1024 + 1024]
                    .rearrange("p (k c) -> p k c", c=128),
                    in_=dr[f"{lname}_wo"].rearrange("k p m -> p k m")
                    [:, :, m * 128:(m + 1) * 128])
                p.dg.inc(instr, 16)

        @blk.tensor
        def _(t):
            t.wait_ge(p.dve.h, d_att)
            n = 0
            for m in range(NCH):
                t.wait_ge(p.dg.h, ld_wo0 + 16 * (m + 1))
                for tg in range(NTG):
                    if n >= 2:
                        t.wait_ge(p.dve.h, d_att + n - 1)
                    for k in range(NCH):
                        instr = t.matmul(
                            ps_a[n % 2][:, 0:TG],
                            WO[:, (m % 2) * 1024 + k * 128:
                               (m % 2) * 1024 + k * 128 + 128],
                            ATT[:, k * T + tg * TG:k * T + tg * TG + TG],
                            start=(k == 0), stop=(k == NCH - 1))
                    p.pe.inc(instr)
                    n += 1

        @blk.vector
        def _(v):
            n = 0
            for m in range(NCH):
                for tg in range(NTG):
                    v.wait_ge(p.pe.h, pe_o0 + n + 1)
                    sl = slice(m * T + tg * TG, m * T + tg * TG + TG)
                    instr = v.scalar_tensor_tensor(
                        INNER[:, sl], ps_a[n % 2][:, 0:TG],
                        BIAS[:, 3 * NCH + m:3 * NCH + m + 1],
                        INNER[:, sl], ALU.add, ALU.add)
                    p.dve.inc(instr)
                    n += 1

    es_a.close()
    if "dbg_xa" in dr:
        emit_spill(p, INNER, "dbg_xa")

    if "ln1" not in PH:
        return
    emit_ln_params(p, S, INNER, INNER, XBF, f"{lname}_g1", f"{lname}_be1", GBl)

    if "ffn1" not in PH:
        return
    es_f = contextlib.ExitStack()
    W1B = p.tile(f"la_w1{U}", [128, 2 * 8 * 128], BF16, es_f)
    B1C = p.tile(f"la_b1{U}", [128, 32], F32, es_f)
    HROW = p.tile(f"la_hrow{U}", [128, 2 * T], BF16, es_f)

    startf = p.snap()

    @blk.sync
    def _(sync):
        p.wait_snap(sync, startf)
        instr = sync.dma_start(out=B1C[:], in_=dr[f"{lname}_b1"])
        p.ds.inc(instr, 16)

    ld_b1 = p.ds.n
    ld_w10 = p.dg.n
    hd_marks = []
    pe_f0 = p.pe.n

    @blk.gpsimd
    def _(g):
        p.wait_snap(g, startf)
        for m in range(32):
            if m >= 2:
                g.wait_ge(p.pe.h, pe_f0 + 4 * (m - 1))
            instr = g.dma_start(
                out=W1B[:, (m % 2) * 1024:(m % 2) * 1024 + 1024]
                .rearrange("p (k c) -> p k c", c=128),
                in_=dr[f"{lname}_w1"].rearrange("k p m -> p k m")
                [:, :, m * 128:(m + 1) * 128])
            p.dg.inc(instr, 16)
    with contextlib.ExitStack() as pse:
        ps_f = [pse.enter_context(p.nc.psum_tensor(f"ps_laf{i}_{U}",
                                                   [128, 512], F32))
                for i in range(2)]
        a_f0 = p.act.n

        @blk.tensor
        def _(t):
            n = 0
            for m in range(32):
                t.wait_ge(p.dg.h, ld_w10 + 16 * (m + 1))
                for tg in range(NTG):
                    if n >= 2:
                        t.wait_ge(p.act.h, a_f0 + n - 1)
                    for k in range(NCH):
                        instr = t.matmul(
                            ps_f[n % 2][:, 0:TG],
                            W1B[:, (m % 2) * 1024 + k * 128:
                                (m % 2) * 1024 + k * 128 + 128],
                            XBF[:, k * T + tg * TG:k * T + tg * TG + TG],
                            start=(k == 0), stop=(k == NCH - 1))
                    p.pe.inc(instr)
                    n += 1

        pe_fq0 = pe_f0

        @blk.scalar
        def _(s):
            n = 0
            for m in range(32):
                for tg in range(NTG):
                    s.wait_ge(p.pe.h, pe_fq0 + n + 1)
                    s.wait_ge(p.ds.h, ld_b1)
                    if tg == 0 and len(hd_marks) >= 2:
                        s.wait_ge(p.ds.h, hd_marks[-2])
                    instr = s.activation(
                        HROW[:, (m % 2) * T + tg * TG:(m % 2) * T + tg * TG + TG],
                        ps_f[n % 2][:, 0:TG], AF.Gelu,
                        bias=B1C[:, m:m + 1])
                    p.act.inc(instr)
                    n += 1

        for m in range(32):
            a_need = a_f0 + 4 * (m + 1)

            @blk.sync
            def _(sync, m=m, a_need=a_need):
                sync.wait_ge(p.act.h, a_need)
                instr = sync.dma_start(
                    out=dr["hd"][m, :, :],
                    in_=HROW[:, (m % 2) * T:(m % 2) * T + T])
                p.ds.inc(instr, 16)
            hd_marks.append(p.ds.n)

    hd_done = p.ds.n
    es_f.close()

    if "ffn2" not in PH:
        return
    es_d = contextlib.ExitStack()
    HB = p.tile(f"la_hb{U}", [128, 8 * TG], BF16, es_d)
    WD = p.tile(f"la_wd{U}", [128, 32 * 512], BF16, es_d)
    B2C = p.tile(f"la_b2{U}", [128, NCH], F32, es_d)

    @blk.sync
    def _(sync):
        instr = sync.dma_start(out=B2C[:], in_=dr[f"{lname}_b2"])
        p.ds.inc(instr, 16)

    ld_b2 = p.ds.n

    with contextlib.ExitStack() as pse:
        ps_y = [pse.enter_context(p.nc.psum_tensor(f"ps_lay{i}_{U}",
                                                   [128, 512], F32))
                for i in range(4)]
        pe_dp0 = p.pe.n
        gl = [0]
        for mh in range(2):
            @blk.gpsimd
            def _(g, mh=mh):
                g.wait_ge(p.pe.h, p.pe.n)
                g.wait_ge(p.dve.h, p.dve.n)
                g.wait_ge(p.act.h, p.act.n)
                g.wait_ge(p.ds.h, hd_done)
                for kc in range(32):
                    instr = g.dma_start(
                        out=WD[:, kc * 512:kc * 512 + 512],
                        in_=dr[f"{lname}_w2"][kc, :, mh * 512:mh * 512 + 512])
                    p.dg.inc(instr, 16)

            wd_ld = p.dg.n
            for tg in range(NTG):
                @blk.sync
                def _(sync, tg=tg, mh=mh):
                    sync.wait_ge(p.ds.h, hd_done)
                    for kc in range(32):
                        L = gl[0]
                        if L >= 8:
                            sync.wait_ge(p.pe.h, pe_dp0 + L // 4 - 1)
                        instr = sync.dma_start(
                            out=HB[:, (kc % 8) * TG:(kc % 8) * TG + TG],
                            in_=dr["hd"][kc, :, tg * TG:tg * TG + TG])
                        p.ds.inc(instr, 16)
                        gl[0] += 1

                ds_h0 = p.ds.n - 16 * 32

                @blk.tensor
                def _(t, tg=tg, mh=mh, wd_ld=wd_ld, ds_h0=ds_h0):
                    t.wait_ge(p.dg.h, wd_ld)
                    t.wait_ge(p.dve.h, p.dve.n)
                    for kc in range(32):
                        t.wait_ge(p.ds.h, ds_h0 + 16 * (kc + 1))
                        for m in range(4):
                            instr = t.matmul(
                                ps_y[m][:, 0:TG],
                                WD[:, kc * 512 + m * 128:kc * 512 + m * 128 + 128],
                                HB[:, (kc % 8) * TG:(kc % 8) * TG + TG],
                                start=(kc == 0), stop=(kc == 31))
                        if kc % 4 == 3:
                            p.pe.inc(instr)

                pe_y = p.pe.n

                @blk.vector
                def _(v, tg=tg, mh=mh, pe_y=pe_y):
                    v.wait_ge(p.pe.h, pe_y)
                    v.wait_ge(p.ds.h, ld_b2)
                    for m in range(4):
                        mc = mh * 4 + m
                        sl = slice(mc * T + tg * TG, mc * T + tg * TG + TG)
                        instr = v.scalar_tensor_tensor(
                            INNER[:, sl], ps_y[m][:, 0:TG],
                            B2C[:, mc:mc + 1], INNER[:, sl],
                            ALU.add, ALU.add)
                        p.dve.inc(instr)
    es_d.close()

    if "ln2" not in PH:
        return
    emit_ln_params(p, S, INNER, INNER, XBF, f"{lname}_g2", f"{lname}_be2", GBl)


def emit_l0attn(p, INNER, XBF, GB):
    for l in range(6):
        with contextlib.ExitStack() as es_s:
            S = LnScratch(p, f"l0a{l}", es_s)
            emit_l0attn_layer(p, INNER, XBF, f"A{l}", S, GB)
    with contextlib.ExitStack() as es_s:
        S = LnScratch(p, "l0af", es_s)
        emit_ln_params(p, S, INNER, INNER, None, "A_gf", "A_bf", GB)






def emit_l1moe(p, XN, XNBF, XOUT, between=None):
    blk = p.blk
    C = p.C
    dr = p.dram
    NE = 8
    NKC = 32
    identf = C["identf"]
    zeros = C["zeros"]
    es_r = contextlib.ExitStack()
    gate_w = p.tile("mo_gatew", [128, 8 * 8], F32, es_r)
    r_fm = p.tile("mo_rfm", [8, T], F32, es_r)
    r_tm = p.tile("mo_rtm", [128, 16 * 8], F32, es_r)
    M1 = p.tile("mo_m1", [128, 16], F32, es_r)
    M2 = p.tile("mo_m2", [128, 16], F32, es_r)
    P1 = p.tile("mo_p1", [128, 16], F32, es_r)
    P2 = p.tile("mo_p2", [128, 16], F32, es_r)
    I2 = p.tile("mo_i2", [128, 8], F32, es_r)
    TMPB = p.tile("mo_tmpb", [128, 8], F32, es_r)
    CW = p.tile("mo_cw", [128, 16 * 8], F32, es_r)
    CWF = p.tile("mo_cwf", [8, T], F32, es_r)

    start = p.snap()

    @blk.sync
    def _(sync):
        p.wait_snap(sync, start)
        instr = sync.dma_start(
            out=gate_w[:].rearrange("p (k m) -> p k m", m=8),
            in_=dr["L1_ffn_Wg"].rearrange("k p m -> p k m"))
        p.ds.inc(instr, 16)

    ld_gw = p.ds.n

    with contextlib.ExitStack() as pse:
        ps_r = [pse.enter_context(p.nc.psum_tensor(f"ps_mor{i}", [128, 512], F32))
                for i in range(2)]
        ps_t = pse.enter_context(p.nc.psum_tensor("ps_mot", [128, 512], F32))
        ps_t2 = pse.enter_context(p.nc.psum_tensor("ps_mot2", [128, 512], F32))

        d_ev0 = p.dve.n

        @blk.tensor
        def _(t):
            t.wait_ge(p.ds.h, ld_gw)
            p.wait_snap(t, start)
            for tg in range(NTG):
                if tg >= 2:
                    t.wait_ge(p.dve.h, d_ev0 + tg - 1)
                for k in range(NCH):
                    instr = t.matmul(
                        ps_r[tg % 2][0:8, 0:TG], gate_w[:, k * 8:k * 8 + 8],
                        XN[:, k * T + tg * TG:k * T + tg * TG + TG],
                        start=(k == 0), stop=(k == NCH - 1))
                p.pe.inc(instr)

        pe_r0 = p.pe.n - NTG

        @blk.vector
        def _(v):
            for tg in range(NTG):
                v.wait_ge(p.pe.h, pe_r0 + tg + 1)
                instr = v.tensor_copy(r_fm[:, tg * TG:tg * TG + TG],
                                      ps_r[tg % 2][0:8, 0:TG])
                p.dve.inc(instr)

        d_r = p.dve.n

        @blk.tensor
        def _(t):
            t.wait_ge(p.dve.h, d_r)
            for tc in range(16):
                instr = t.matmul(ps_t[:, tc * 8:tc * 8 + 8],
                                 r_fm[:, tc * 128:tc * 128 + 128],
                                 identf[0:8, 0:8], is_transpose=True,
                                 start=True, stop=True)
            p.pe.inc(instr)

        pe_tr = p.pe.n

        @blk.vector
        def _(v):
            v.wait_ge(p.pe.h, pe_tr)
            instr = v.tensor_copy(r_tm[:], ps_t[:, 0:128])
            p.dve.inc(instr)

        d_rtm = p.dve.n

        @blk.vector
        def _(v):
            v.wait_ge(p.dve.h, d_rtm)
            zb = zeros[:, 0:1].broadcast_to([128, 8])

            def S(instr):
                p.dve.inc(instr)
                v.wait_ge(p.dve.h, p.dve.n)
            for tc in range(16):
                blkr = r_tm[:, tc * 8:tc * 8 + 8]
                cwb = CW[:, tc * 8:tc * 8 + 8]
                m1 = M1[:, tc:tc + 1]
                m2 = M2[:, tc:tc + 1]
                S(v.reduce_max(m1, blkr, AX.X))
                S(v.scalar_tensor_tensor(cwb, blkr, m1, zb, ALU.subtract,
                                         ALU.is_ge))
                S(v.scalar_tensor_tensor(I2[:], cwb, -1e30, blkr, ALU.mult,
                                         ALU.add))
                S(v.reduce_max(m2, I2[:], AX.X))
                S(v.scalar_tensor_tensor(I2[:], I2[:], m2, zb, ALU.subtract,
                                         ALU.is_ge))
                S(v.tensor_copy(TMPB[:], I2[:]))
                S(v.scalar_tensor_tensor(CW[:, tc * 8:tc * 8 + 8], cwb, 1.0,
                                         TMPB[:], ALU.mult, ALU.bypass)
                  if False else v.engine_nop())
                S(v.scalar_tensor_tensor(cwb, TMPB[:], 2.0, cwb, ALU.mult,
                                         ALU.add))
            instr = v.tensor_tensor(P2[:], M1[:], M2[:], ALU.subtract)
            p.dve.inc(instr)

        d_dif = p.dve.n

        @blk.scalar
        def _(s):
            s.wait_ge(p.dve.h, d_dif)
            instr = s.activation(P1[:], P2[:], AF.Sigmoid)
            p.act.inc(instr)

        a_sig = p.act.n

        @blk.vector
        def _(v):
            v.wait_ge(p.act.h, a_sig)

            def S(instr):
                p.dve.inc(instr)
                v.wait_ge(p.dve.h, p.dve.n)
            zb = zeros[:, 0:1].broadcast_to([128, 8])
            S(v.tensor_single_scalar(P2[:], P1[:], -1.0, ALU.mult))
            S(v.tensor_single_scalar(P2[:], P2[:], 1.0, ALU.add))
            for tc in range(16):
                cwb = CW[:, tc * 8:tc * 8 + 8]
                S(v.tensor_single_scalar(I2[:], cwb, 2.0, ALU.is_equal))
                S(v.tensor_single_scalar(TMPB[:], cwb, 1.0, ALU.is_equal))
                S(v.scalar_tensor_tensor(I2[:], I2[:], P2[:, tc:tc + 1], zb,
                                         ALU.mult, ALU.add))
                S(v.scalar_tensor_tensor(cwb, TMPB[:], P1[:, tc:tc + 1],
                                         I2[:], ALU.mult, ALU.add))

        d_cw = p.dve.n
        d_cwf0 = p.dve.n

        @blk.tensor
        def _(t):
            t.wait_ge(p.dve.h, d_cw)
            for tc in range(16):
                bank = (ps_t, ps_t2)[tc % 2]
                if tc >= 2:
                    t.wait_ge(p.dve.h, d_cwf0 + tc - 1)
                instr = t.matmul(bank[0:8, 0:128],
                                 CW[:, tc * 8:tc * 8 + 8], identf[:],
                                 is_transpose=True, start=True, stop=True)
                p.pe.inc(instr)

        pe_cwt0 = p.pe.n - 16

        @blk.vector
        def _(v):
            for tc in range(16):
                v.wait_ge(p.pe.h, pe_cwt0 + tc + 1)
                bank = (ps_t, ps_t2)[tc % 2]
                instr = v.tensor_copy(CWF[:, tc * 128:tc * 128 + 128],
                                      bank[0:8, 0:128])
                p.dve.inc(instr)

        d_cwf = p.dve.n

    @blk.sync
    def _(sync):
        sync.wait_ge(p.dve.h, d_cwf)
        instr = sync.dma_start(out=dr["cwd"][0:8, :], in_=CWF[:])
        p.ds.inc(instr, 16)

    ds_cwd = p.ds.n
    es_r.close()
    if between is not None:
        between()

    es_e = contextlib.ExitStack()
    CWB = p.tile("mo_cwb", [128, 2 * T], F32, es_e)
    W1B = p.tile("mo_w1b", [128, 2 * 8 * 512], BF16, es_e)
    HP = p.tile("mo_hp", [128, 2 * 4 * TG], BF16, es_e)

    def w1_src(e):
        return dr["L1_ffn_W1"][e].rearrange("k p m -> p k m")

    fence = p.snap()

    @blk.gpsimd
    def _(g):
        p.wait_snap(g, fence)
        g.wait_ge(p.ds.h, ds_cwd)
        instr = g.dma_start(
            out=W1B[:, 0:4096].rearrange("p (k m) -> p k m", m=512),
            in_=w1_src(0))
        p.dg.inc(instr, 16)

    @blk.sync
    def _(sync):
        p.wait_snap(sync, fence)
        instr = sync.dma_start(out=CWB[:, 0:T],
                               in_=dr["cwd"][0:1, :].broadcast_to([128, T]))
        p.ds.inc(instr, 16)

    hd_marks = []
    with contextlib.ExitStack() as pse:
        ps_h = [pse.enter_context(p.nc.psum_tensor(f"ps_moh{i}", [128, 512], F32))
                for i in range(8)]
        for e in range(NE):
            ld_done = p.dg.n
            cw_done = p.ds.n
            wbuf = slice((e % 2) * 4096, (e % 2) * 4096 + 4096)
            if e < NE - 1:
                @blk.gpsimd
                def _(g, e=e):
                    g.wait_ge(p.pe.h, p.pe.n)
                    instr = g.dma_start(
                        out=W1B[:, ((e + 1) % 2) * 4096:
                               ((e + 1) % 2) * 4096 + 4096]
                        .rearrange("p (k m) -> p k m", m=512),
                        in_=w1_src(e + 1))
                    p.dg.inc(instr, 16)

                @blk.sync
                def _(sync, e=e):
                    sync.wait_ge(p.dve.h, p.dve.n)
                    instr = sync.dma_start(
                        out=CWB[:, ((e + 1) % 2) * T:((e + 1) % 2) * T + T],
                        in_=dr["cwd"][e + 1:e + 2, :].broadcast_to([128, T]))
                    p.ds.inc(instr, 16)

            for tg in range(NTG):
                d_hp0 = p.dve.n

                @blk.tensor
                def _(t, e=e, tg=tg, wbuf=wbuf, ld_done=ld_done, d_hp0=d_hp0,
                      a_now=None):
                    t.wait_ge(p.dg.h, ld_done)
                    t.wait_ge(p.dve.h, d_hp0)
                    t.wait_ge(p.act.h, p.act.n)
                    for m in range(4):
                        for k in range(NCH):
                            instr = t.matmul(
                                ps_h[(tg % 2) * 4 + m][:, 0:TG],
                                W1B[:, wbuf.start + k * 512 + m * 128:
                                   wbuf.start + k * 512 + m * 128 + 128],
                                XNBF[:, k * T + tg * TG:k * T + tg * TG + TG],
                                start=(k == 0), stop=(k == NCH - 1))
                        if m == 3:
                            p.pe.inc(instr)

                pe_h = p.pe.n

                for m in range(4):
                    @blk.scalar
                    def _(s, tg=tg, m=m, pe_h=pe_h):
                        s.wait_ge(p.pe.h, pe_h)
                        if len(hd_marks) >= 2 and m == 0:
                            s.wait_ge(p.ds.h, hd_marks[-2])
                        instr = s.activation(
                            HP[:, (tg % 2) * 2048 + m * TG:
                               (tg % 2) * 2048 + m * TG + TG],
                            ps_h[(tg % 2) * 4 + m][:, 0:TG], AF.Relu)
                        p.act.inc(instr)

                a_relu = p.act.n

                @blk.vector
                def _(v, e=e, tg=tg, a_relu=a_relu, cw_done=cw_done):
                    v.wait_ge(p.act.h, a_relu)
                    v.wait_ge(p.ds.h, cw_done)
                    for m in range(4):
                        hp = HP[:, (tg % 2) * 2048 + m * TG:
                                (tg % 2) * 2048 + m * TG + TG]
                        instr = v.tensor_tensor(
                            hp, hp,
                            CWB[:, (e % 2) * T + tg * TG:
                                (e % 2) * T + tg * TG + TG], ALU.mult)
                        p.dve.inc(instr)

                d_hp = p.dve.n

                @blk.sync
                def _(sync, e=e, tg=tg, d_hp=d_hp):
                    sync.wait_ge(p.dve.h, d_hp)
                    instr = sync.dma_start(
                        out=dr["hd"][4 * e:4 * e + 4, :, tg * TG:tg * TG + TG]
                        .rearrange("m p t -> p m t"),
                        in_=HP[:, (tg % 2) * 2048:(tg % 2) * 2048 + 2048]
                        .rearrange("p (m t) -> p m t", m=4))
                    p.ds.inc(instr, 16)
                hd_marks.append(p.ds.n)

    hd_done = p.ds.n
    es_e.close()

    def w2_src(kc, mh):
        e, sub = kc // 4, kc % 4
        return dr["L1_ffn_W2"][e][sub, :, mh * 512:mh * 512 + 512]

    es_d = contextlib.ExitStack()
    HB = p.tile("mo_hb", [128, 8 * TG], BF16, es_d)
    WD = p.tile("mo_wd", [128, NKC * 512], BF16, es_d)
    with contextlib.ExitStack() as pse:
        ps_y = [pse.enter_context(p.nc.psum_tensor(f"ps_moy{i}", [128, 512], F32))
                for i in range(4)]
        pe_dp0 = p.pe.n
        gl = [0]
        for mh in range(2):
            @blk.gpsimd
            def _(g, mh=mh):
                g.wait_ge(p.pe.h, p.pe.n)
                g.wait_ge(p.dve.h, p.dve.n)
                g.wait_ge(p.act.h, p.act.n)
                g.wait_ge(p.ds.h, hd_done)
                for kc in range(NKC):
                    instr = g.dma_start(out=WD[:, kc * 512:kc * 512 + 512],
                                        in_=w2_src(kc, mh))
                    p.dg.inc(instr, 16)

            wd_ld = p.dg.n
            for tg in range(NTG):
                pe_y0 = p.pe.n

                @blk.sync
                def _(sync, tg=tg, mh=mh, pe_y0=pe_y0):
                    sync.wait_ge(p.ds.h, hd_done)
                    for kc in range(NKC):
                        L = gl[0]
                        if L >= 8:
                            sync.wait_ge(p.pe.h, pe_dp0 + L // 4 - 1)
                        instr = sync.dma_start(
                            out=HB[:, (kc % 8) * TG:(kc % 8) * TG + TG],
                            in_=dr["hd"][kc, :, tg * TG:tg * TG + TG])
                        p.ds.inc(instr, 16)
                        gl[0] += 1

                ds_h0 = p.ds.n - 16 * NKC

                @blk.tensor
                def _(t, tg=tg, mh=mh, wd_ld=wd_ld, ds_h0=ds_h0):
                    t.wait_ge(p.dg.h, wd_ld)
                    t.wait_ge(p.dve.h, p.dve.n)
                    for kc in range(NKC):
                        t.wait_ge(p.ds.h, ds_h0 + 16 * (kc + 1))
                        for m in range(4):
                            instr = t.matmul(
                                ps_y[m][:, 0:TG],
                                WD[:, kc * 512 + m * 128:kc * 512 + m * 128 + 128],
                                HB[:, (kc % 8) * TG:(kc % 8) * TG + TG],
                                start=(kc == 0), stop=(kc == NKC - 1))
                        if kc % 4 == 3 or kc == NKC - 1:
                            p.pe.inc(instr)

                pe_y = p.pe.n

                @blk.vector
                def _(v, tg=tg, mh=mh, pe_y=pe_y):
                    v.wait_ge(p.pe.h, pe_y)
                    for m in range(4):
                        mc = mh * 4 + m
                        sl = slice(mc * T + tg * TG, mc * T + tg * TG + TG)
                        instr = v.tensor_tensor(XOUT[:, sl], XOUT[:, sl],
                                                ps_y[m][:, 0:TG], ALU.add)
                        p.dve.inc(instr)
    es_d.close()
    return dict(done=p.dve.n)



def emit_mha(p, XNBF, XOUT, wq, wk, wv, wo, bq, bk, bv, bo, scale, alibi,
             tag=""):
    blk = p.blk
    C = p.C
    dr = p.dram
    NH = 16
    DH = 64
    NQC = 16
    NKB = 4

    es_m = contextlib.ExitStack()
    QT = [p.tile(f"mh_qt{i}{tag}", [128, T], BF16, es_m) for i in range(2)]
    KT = [p.tile(f"mh_kt{i}{tag}", [128, T], BF16, es_m) for i in range(2)]
    VF = p.tile(f"mh_vf{tag}", [128, T], BF16, es_m)
    VTM = p.tile(f"mh_vtm{tag}", [128, 2 * 16 * DH], BF16, es_m)
    PSB = p.tile(f"mh_psb{tag}", [128, 4 * T], BF16, es_m)
    PTS = p.tile(f"mh_pts{tag}", [128, 16 * 512], BF16, es_m)
    ATT = p.tile(f"mh_att{tag}", [128, NCH * T], BF16, es_m)
    WQ = p.tile(f"mh_wq{tag}", [128, 3 * 1024], BF16, es_m)
    WO = p.tile(f"mh_wo{tag}", [128, 2 * 1024], BF16, es_m)
    BIAS = p.tile(f"mh_bias{tag}", [128, 4 * NCH], F32, es_m)
    SLO = p.tile(f"mh_slo{tag}", [128, 16], F32, es_m)

    DIAG = p.tile(f"mh_diag{tag}", [128, 2 * 128], BF16, es_m)
    DEN = p.tile(f"mh_den{tag}", [128, 24], F32, es_m)
    scol = p.tile(f"mh_scol{tag}", [128, 1], F32, es_m)

    start = p.snap()

    @blk.vector
    def _(v):
        p.wait_snap(v, start)
        v.memset(scol[:], scale)
        v.memset(QT[0][64:128, :], 0.0)
        v.memset(QT[0][64:65, :], 1.0)
        v.memset(QT[1][0:64, :], 0.0)
        v.memset(QT[1][0:1, :], 1.0)
        v.memset(KT[0][64:128, :], 0.0)
        instr = v.memset(KT[1][0:64, :], 0.0)
        p.dve.inc(instr)

    d_const = p.dve.n
    pl_iota = p.pool.n

    @blk.sync
    def _(sync):
        sync.wait_ge(p.dve.h, d_const)
        instr = sync.dma_start(out=QT[0][65:66, :], in_=QT[0][64:65, :])
        p.ds.inc(instr, 16)
        instr = sync.dma_start(out=QT[1][1:2, :], in_=QT[1][0:1, :])
        p.ds.inc(instr, 16)
    ds_qaug = p.ds.n

    @blk.sync
    def _(sync):
        p.wait_snap(sync, start)
        for i, nmb in enumerate((bq, bk, bv, bo)):
            if nmb is not None:
                instr = sync.dma_start(out=BIAS[:, i * NCH:(i + 1) * NCH],
                                       in_=dr[nmb])
                p.ds.inc(instr, 16)
        if alibi is not None:
            instr = sync.dma_start(out=SLO[0:1, :],
                                   in_=dr[alibi].rearrange("h one -> one h"))
            p.ds.inc(instr, 16)
            instr = sync.dma_start(out=SLO[64:65, :],
                                   in_=dr[alibi].rearrange("h one -> one h"))
            p.ds.inc(instr, 16)

    ld_bias = p.ds.n

    zb512 = C["zeros"][:, 0:1].broadcast_to([128, 512])

    with contextlib.ExitStack() as pse:
        ps_q0 = pse.enter_context(p.nc.psum_tensor(f"ps_mhq0{tag}",
                                                    [128, 512], F32))
        ps_s = pse.enter_context(p.nc.psum_tensor(f"ps_mhs{tag}",
                                                  [128, 2048], F32))
        ps_pt = pse.enter_context(p.nc.psum_tensor(f"ps_mhpt{tag}",
                                                   [128, 512], F32))
        ps_pt2 = pse.enter_context(p.nc.psum_tensor(f"ps_mhpt2{tag}",
                                                    [128, 512], F32))
        ps_ctx = pse.enter_context(p.nc.psum_tensor(f"ps_mhctx{tag}",
                                                    [128, 512], F32))
        ps_qkv = [ps_q0, ps_ctx]

        QCN = [0]
        qmeta = {}
        smark = {}
        nm = {}
        em = {}
        emA = {}
        dgm = {}
        trb = {}
        trslot = {}
        slot_ev = {}
        cxm = {}

        for pp in range(8):
            ld0 = p.dg.n

            @blk.gpsimd
            def _(g, pp=pp, fstart=start):
                if pp == 0:
                    p.wait_snap(g, fstart)
                g.wait_ge(p.pe.h, p.pe.n)
                for i, wn in enumerate((wq, wk, wv)):
                    instr = g.dma_start(
                        out=WQ[:, i * 1024:(i + 1) * 1024]
                        .rearrange("p (k m) -> p k m", m=128),
                        in_=dr[wn].rearrange("k p m -> p k m")
                        [:, :, pp * 128:(pp + 1) * 128])
                    p.dg.inc(instr, 16)

            ld_w = p.dg.n

            d_ev0 = p.dve.n
            pe_q0 = p.pe.n

            @blk.tensor
            def _(t, pp=pp, ld_w=ld_w, d_ev0=d_ev0):
                t.wait_ge(p.dg.h, ld_w)
                p.wait_snap(t, start)
                t.wait_ge(p.dve.h, d_ev0)
                n = 0
                for i in range(3):
                    for tg in range(NTG):
                        if n >= 2:
                            t.wait_ge(p.dve.h, d_ev0 + 2 * (n - 1))
                        for k in range(NCH):
                            instr = t.matmul(
                                ps_qkv[n % 2][:, 0:TG],
                                WQ[:, i * 1024 + k * 128:
                                   i * 1024 + k * 128 + 128],
                                XNBF[:, k * T + tg * TG:k * T + tg * TG + TG],
                                start=(k == 0), stop=(k == NCH - 1))
                        p.pe.inc(instr)
                        n += 1

            @blk.vector
            def _(v, pp=pp, pe_q0=pe_q0):
                v.wait_ge(p.ds.h, ld_bias)
                n = 0
                for i in range(3):
                    for tg in range(NTG):
                        v.wait_ge(p.pe.h, pe_q0 + n + 1)
                        ps = ps_qkv[n % 2]
                        tsl = slice(tg * TG, tg * TG + TG)
                        for par in range(2):
                            rows = slice(0, 64) if par == 0 else slice(64, 128)
                            pr = ps[rows, 0:TG]
                            if i == 0:
                                if bq is not None:
                                    instr = v.scalar_tensor_tensor(
                                        QT[par][rows, tsl], pr,
                                        BIAS[rows, 0 * NCH + pp:0 * NCH + pp + 1],
                                        scol[rows, 0:1].broadcast_to([64, TG]),
                                        ALU.add, ALU.mult)
                                else:
                                    instr = v.scalar_tensor_tensor(
                                        QT[par][rows, tsl], pr, float(scale),
                                        zb512[rows, :], ALU.mult, ALU.add)
                            elif i == 1:
                                if bk is not None:
                                    instr = v.scalar_tensor_tensor(
                                        KT[par][rows, tsl], pr,
                                        BIAS[rows, 1 * NCH + pp:1 * NCH + pp + 1],
                                        zb512[rows, :], ALU.add, ALU.add)
                                else:
                                    instr = v.tensor_copy(KT[par][rows, tsl], pr)
                            else:
                                if bv is not None:
                                    instr = v.scalar_tensor_tensor(
                                        VF[rows, tsl], pr,
                                        BIAS[rows, 2 * NCH + pp:2 * NCH + pp + 1],
                                        zb512[rows, :], ALU.add, ALU.add)
                                else:
                                    instr = v.tensor_copy(VF[rows, tsl], pr)
                            p.dve.inc(instr)
                        n += 1

            d_qkv = p.dve.n

            if alibi is not None:
                d_qkv2 = p.dve.n

                @blk.gpsimd
                def _(g, pp=pp, d_qkv2=d_qkv2):
                    g.wait_ge(p.dve.h, d_qkv2)
                    scr = PSB[:].bitcast(F32)
                    g.iota(scr[0:1, T:2 * T], [[1, T]], channel_multiplier=0,
                           allow_small_or_imprecise_dtypes=True)
                    instr = g.iota(scr[64:65, T:2 * T], [[1, T]],
                                   channel_multiplier=0,
                                   allow_small_or_imprecise_dtypes=True)
                    p.pool.inc(instr)

                pl_io = p.pool.n

                @blk.vector
                def _(v, pp=pp, pl_io=pl_io):
                    v.wait_ge(p.ds.h, ld_bias)
                    v.wait_ge(p.pool.h, pl_io)
                    v.wait_ge(p.dve.h, d_qkv)
                    scr = PSB[:].bitcast(F32)
                    for par, rr in ((0, slice(64, 65)), (1, slice(0, 1))):
                        h = 2 * pp + par
                        kt = KT[par]
                        hi_row = kt[64:65, :] if par == 0 else kt[0:1, :]
                        v.scalar_tensor_tensor(
                            scr[rr, 0:T], scr[rr, T:2 * T],
                            SLO[rr, h:h + 1],
                            C["zeros"][rr, 0:1].broadcast_to([1, T]),
                            ALU.mult, ALU.add)
                        v.tensor_copy(hi_row, scr[rr, 0:T])
                        instr = v.scalar_tensor_tensor(
                            VTM[rr, 0:T], hi_row, -1.0, scr[rr, 0:T],
                            ALU.mult, ALU.add)
                        p.dve.inc(instr)

                d_srow_v = p.dve.n

                @blk.sync
                def _(sync, pp=pp, d_srow_v=0):
                    sync.wait_ge(p.dve.h, p.dve.n)
                    instr = sync.dma_start(out=KT[0][65:66, :],
                                           in_=VTM[64:65, 0:T])
                    p.ds.inc(instr, 16)
                    instr = sync.dma_start(out=KT[1][1:2, :],
                                           in_=VTM[0:1, 0:T])
                    p.ds.inc(instr, 16)

            d_srow = p.dve.n
            ds_srow = p.ds.n

            pe_vt0 = p.pe.n
            d_vt0 = p.dve.n

            @blk.tensor
            def _(t, pp=pp, d_qkv=d_qkv, d_vt0=d_vt0):
                t.wait_ge(p.dve.h, d_qkv)
                for half in range(2):
                    if half == 1:
                        t.wait_ge(p.dve.h, d_vt0 + 1)
                    for j in range(8):
                        kc = half * 8 + j
                        instr = t.matmul(
                            ps_pt[:, 0:512].bitcast(BF16)
                            [:, j * 128:j * 128 + 128],
                            VF[:, kc * 128:kc * 128 + 128],
                            C["ident"][:],
                            is_transpose=True, start=True, stop=True)
                    p.pe.inc(instr)

            @blk.vector
            def _(v, pp=pp, pe_vt0=pe_vt0):
                v.wait_ge(p.ds.h, p.ds.n)
                for half in range(2):
                    v.wait_ge(p.pe.h, pe_vt0 + half + 1)
                    instr = v.tensor_copy(
                        VTM[:, half * 1024:half * 1024 + 1024],
                        ps_pt[:, 0:512].bitcast(BF16))
                    p.dve.inc(instr)

            d_vtm = p.dve.n
            slot_ev[0] = ("dve", d_vtm)

            def emit_S(qc, first_of_pair, d_vtm=d_vtm, ds_srow=ds_srow,
                       ds_qaug=ds_qaug):
                hh_, qg_, qi_, pp_, gg_, ql_ = qmeta[qc]
                rows_ap = slice(0, 66) if hh_ == 0 else slice(0, 128)
                pe_s0 = p.pe.n
                warA = emA.get(qc - 1, 0)
                warB = em.get(qc - 1, 0)

                @blk.tensor
                def _(t, qc=qc, hh_=hh_, ql_=ql_, rows_ap=rows_ap, warA=warA,
                      fop=first_of_pair):
                    if fop:
                        t.wait_ge(p.dve.h, d_vtm)
                        t.wait_ge(p.ds.h, ds_srow)
                        t.wait_ge(p.ds.h, ds_qaug)
                    if warA:
                        t.wait_ge(p.act.h, warA)
                    for kb in range(2):
                        instr = t.matmul(
                            ps_s[:, kb * 512:kb * 512 + 512],
                            QT[hh_][rows_ap, ql_ * 128:ql_ * 128 + 128],
                            KT[hh_][rows_ap, kb * 512:kb * 512 + 512],
                            start=True, stop=True)
                        p.pe.inc(instr)

                @blk.tensor
                def _(t, qc=qc, hh_=hh_, ql_=ql_, rows_ap=rows_ap, warB=warB):
                    if warB:
                        t.wait_ge(p.act.h, warB)
                    for kb in range(2, NKB):
                        instr = t.matmul(
                            ps_s[:, kb * 512:kb * 512 + 512],
                            QT[hh_][rows_ap, ql_ * 128:ql_ * 128 + 128],
                            KT[hh_][rows_ap, kb * 512:kb * 512 + 512],
                            start=True, stop=True)
                        p.pe.inc(instr)

                smark[qc] = [pe_s0 + kb + 1 for kb in range(NKB)]

            def emit_max(qc):
                par = qc % 2
                base = par * 12

                @blk.vector
                def _(v, qc=qc, base=base, sm=smark[qc]):
                    for kb in range(NKB):
                        v.wait_ge(p.pe.h, sm[kb])
                        p.dve.inc(v.reduce_max(DEN[:, base + kb:base + kb + 1],
                                               ps_s[:, kb * 512:kb * 512 + 512],
                                               AX.X))
                    v.wait_ge(p.dve.h, p.dve.n)
                    p.dve.inc(v.reduce_max(DEN[:, base + 4:base + 5],
                                           DEN[:, base + 0:base + 4], AX.X))
                    v.wait_ge(p.dve.h, p.dve.n)
                    instr = v.tensor_single_scalar(
                        DEN[:, base + 5:base + 6], DEN[:, base + 4:base + 5],
                        -1.0, ALU.mult)
                    p.dve.inc(instr)

                nm[qc] = p.dve.n

            def emit_exp(qc):
                hh_, qg_, qi_, pp_, gg_, ql_ = qmeta[qc]
                par = qc % 2
                base = par * 12
                slot_war = trslot.get(qi_, 0)
                a0 = p.act.n

                @blk.scalar
                def _(s, qc=qc, qi_=qi_, base=base, slot_war=slot_war):
                    s.wait_ge(p.dve.h, nm[qc])
                    if slot_war:
                        s.wait_ge(p.pe.h, slot_war)
                    for kb in range(NKB):
                        instr = s.activation(
                            PSB[:, qi_ * T + kb * 512:qi_ * T + kb * 512 + 512],
                            ps_s[:, kb * 512:kb * 512 + 512],
                            AF.Exp, bias=DEN[:, base + 5:base + 6],
                            accum_out=DEN[:, base + 6 + kb:base + 7 + kb])
                        p.act.inc(instr)

                emA[qc] = a0 + 2
                em[qc] = p.act.n

            def emit_tail(qc):
                par = qc % 2
                base = par * 12
                war_pe = trb.get(qc - 2, [0])[-1]

                @blk.vector
                def _(v, qc=qc, par=par, base=base, war_pe=war_pe):
                    v.wait_ge(p.act.h, em[qc])
                    if war_pe:
                        v.wait_ge(p.pe.h, war_pe)
                    p.dve.inc(v.reduce_sum(DEN[:, base + 10:base + 11],
                                           DEN[:, base + 6:base + 10], AX.X))
                    v.wait_ge(p.dve.h, p.dve.n)
                    p.dve.inc(v.reciprocal(DEN[:, base + 11:base + 12],
                                           DEN[:, base + 10:base + 11]))
                    v.wait_ge(p.dve.h, p.dve.n)
                    instr = v.scalar_tensor_tensor(
                        DIAG[:, par * 128:par * 128 + 128],
                        C["ident"][:, 0:128], DEN[:, base + 11:base + 12],
                        C["zeros"][:, 0:1].broadcast_to([128, 128]),
                        ALU.mult, ALU.add)
                    p.dve.inc(instr)

                dgm[qc] = p.dve.n

            def emit_tr(qc):
                hh_, qg_, qi_, pp_, gg_, ql_ = qmeta[qc]
                par = qc % 2
                marks = []
                ctx_war = cxm.get(gg_ - 1, 0) if qi_ == 0 else 0
                for bb in range(4):
                    bank = (ps_pt, ps_pt2)[bb % 2]
                    ev_eng, ev_cnt = slot_ev.get(bb % 2, ("dve", 0))

                    @blk.tensor
                    def _(t, qc=qc, bb=bb, par=par, qi_=qi_, bank=bank,
                          ev_eng=ev_eng, ev_cnt=ev_cnt):
                        if bb == 0:
                            t.wait_ge(p.dve.h, dgm[qc])
                        if ev_cnt:
                            t.wait_ge((p.dve if ev_eng == "dve" else p.act).h,
                                      ev_cnt)
                        for j in range(4):
                            kc = bb * 4 + j
                            instr = t.matmul(
                                bank[:, j * 128:j * 128 + 128],
                                PSB[:, qi_ * T + kc * 128:
                                    qi_ * T + kc * 128 + 128],
                                DIAG[:, par * 128:par * 128 + 128],
                                start=True, stop=True)
                        p.pe.inc(instr)

                    marks.append(p.pe.n)
                    pe_need = p.pe.n
                    dst3 = PTS[:].rearrange("p (kc q) -> p kc q", q=512)
                    src3 = bank[:, 0:512].rearrange("p (j q) -> p j q", q=128)

                    @blk.vector
                    def _(v, bb=bb, qi_=qi_, pe_need=pe_need,
                          ctx_war=ctx_war, dst3=dst3, src3=src3):
                        if ctx_war:
                            v.wait_ge(p.pe.h, ctx_war)
                        v.wait_ge(p.pe.h, pe_need)
                        instr = v.tensor_copy(
                            dst3[:, bb * 4:bb * 4 + 4,
                                 qi_ * 128:qi_ * 128 + 128], src3)
                        p.dve.inc(instr)

                    slot_ev[bb % 2] = ("dve", p.dve.n)
                trb[qc] = marks
                trslot[qi_] = marks[-1]

            def emit_ctx(gg, hh_, pp_, qg_):
                d_need = p.dve.n
                a_need = p.act.n

                @blk.tensor
                def _(t, hh_=hh_, d_need=d_need, a_need=a_need):
                    t.wait_ge(p.dve.h, d_need)
                    t.wait_ge(p.act.h, a_need)
                    base = slice(0, 64) if hh_ == 0 else slice(64, 128)
                    for kc in range(16):
                        instr = t.matmul(
                            ps_ctx[base, 0:512],
                            VTM[:, kc * 128 + hh_ * 64:
                                kc * 128 + hh_ * 64 + 64],
                            PTS[:, kc * 512:kc * 512 + 512],
                            start=(kc == 0), stop=(kc == 15))
                    p.pe.inc(instr)

                cxm[gg] = p.pe.n
                pe_c = p.pe.n

                @blk.vector
                def _(v, hh_=hh_, pp_=pp_, qg_=qg_, pe_c=pe_c):
                    v.wait_ge(p.pe.h, pe_c)
                    base = slice(0, 64) if hh_ == 0 else slice(64, 128)
                    instr = v.tensor_copy(
                        ATT[base, pp_ * T + qg_ * 512:pp_ * T + qg_ * 512 + 512],
                        ps_ctx[base, 0:512])
                    p.dve.inc(instr)

            for hh in range(2):
                for qg in range(4):
                    gg = pp * 8 + hh * 4 + qg
                    for qi in range(4):
                        ql = qg * 4 + qi
                        qc = QCN[0]
                        QCN[0] += 1
                        qmeta[qc] = (hh, qg, qi, pp, gg, ql)
                        emit_S(qc, first_of_pair=(qc % 32 == 0))
                        emit_max(qc)
                        emit_exp(qc)
                        if qi > 0:
                            emit_tail(qc - 1)
                            emit_tr(qc - 1)
                    emit_tail(qc)
                    emit_tr(qc)
                    emit_ctx(gg, hh, pp, qg)

        d_att = p.dve.n
        a_att = p.act.n
        if "dbg_att" in dr:
            @blk.sync
            def _(sync):
                sync.wait_ge(p.dve.h, d_att)
                sync.wait_ge(p.act.h, a_att)
                p.ds.inc(sync.dma_start(out=dr["dbg_att"], in_=ATT[:]), 16)
                p.ds.inc(sync.dma_start(out=dr["dbg_qt0"], in_=QT[0][:]), 16)
                p.ds.inc(sync.dma_start(out=dr["dbg_kt0"], in_=KT[0][:]), 16)
                p.ds.inc(sync.dma_start(out=dr["dbg_qt1"], in_=QT[1][:]), 16)
                p.ds.inc(sync.dma_start(out=dr["dbg_kt1"], in_=KT[1][:]), 16)
                p.ds.inc(sync.dma_start(out=dr["dbg_vf"], in_=VF[:]), 16)
                p.ds.inc(sync.dma_start(out=dr["dbg_vtm"], in_=VTM[:]), 16)
                p.ds.inc(sync.dma_start(out=dr["dbg_psb"], in_=PSB[:]), 16)
        pe_o0 = p.pe.n
        ld_wo0 = p.dg.n

        @blk.gpsimd
        def _(g):
            g.wait_ge(p.dve.h, d_att)
            g.wait_ge(p.act.h, a_att)
            for m in range(NCH):
                if m >= 2:
                    g.wait_ge(p.pe.h, pe_o0 + 4 * (m - 1))
                instr = g.dma_start(
                    out=WO[:, (m % 2) * 1024:(m % 2) * 1024 + 1024]
                    .rearrange("p (k c) -> p k c", c=128),
                    in_=dr[wo].rearrange("k p m -> p k m")
                    [:, :, m * 128:(m + 1) * 128])
                p.dg.inc(instr, 16)

        @blk.tensor
        def _(t):
            t.wait_ge(p.dve.h, d_att)
            t.wait_ge(p.act.h, a_att)
            n = 0
            for m in range(NCH):
                t.wait_ge(p.dg.h, ld_wo0 + 16 * (m + 1))
                for tg in range(NTG):
                    if n >= 2:
                        t.wait_ge(p.dve.h, d_att + n - 1)
                    for k in range(NCH):
                        instr = t.matmul(
                            ps_qkv[n % 2][:, 0:TG],
                            WO[:, (m % 2) * 1024 + k * 128:
                               (m % 2) * 1024 + k * 128 + 128],
                            ATT[:, k * T + tg * TG:k * T + tg * TG + TG],
                            start=(k == 0), stop=(k == NCH - 1))
                    p.pe.inc(instr)
                    n += 1

        @blk.vector
        def _(v):
            n = 0
            for m in range(NCH):
                for tg in range(NTG):
                    v.wait_ge(p.pe.h, pe_o0 + n + 1)
                    sl = slice(m * T + tg * TG, m * T + tg * TG + TG)
                    if bo is not None:
                        instr = v.scalar_tensor_tensor(
                            XOUT[:, sl], ps_qkv[n % 2][:, 0:TG],
                            BIAS[:, 3 * NCH + m:3 * NCH + m + 1],
                            XOUT[:, sl], ALU.add, ALU.add)
                    else:
                        instr = v.tensor_tensor(
                            XOUT[:, sl], XOUT[:, sl], ps_qkv[n % 2][:, 0:TG],
                            ALU.add)
                    p.dve.inc(instr)
                    n += 1

    es_m.close()
    return dict(done=p.dve.n)






def build_ln_only():
    nc = bass.Bass("TRN2", target_bir_lowering=False)
    es = contextlib.ExitStack()
    with es:
        xd = nc.dram_tensor("x", [D, T], F32, kind="ExternalInput")
        od = nc.dram_tensor("out", [D, T], F32, kind="ExternalOutput")
        blk = es.enter_context(nc.Block())
        p = P(nc, es, blk)
        p.dram["x"] = xd.ap()
        p.dram["out"] = od.ap()
        X = p.sbuf("X", [128, NCH * T], F32)
        XN = X
        XNBF = p.sbuf("XNBF", [128, NCH * T], BF16)
        emit_consts(p)
        emit_stream_load(p, "x", X[:])
        n = p.ds.n

        @blk.vector
        def _(v):
            v.wait_ge(p.ds.h, n)

        @blk.scalar
        def _(s):
            s.wait_ge(p.ds.h, n)

        @blk.tensor
        def _(t):
            t.wait_ge(p.ds.h, n)

        S = LnScratch(p)
        emit_ln(p, S, X[:], XN[:], XNBF[:])
        emit_stream_store(p, XN[:], "out", waits=p.snap())
        fin = p.ds.n

        @blk.sync
        def _(sync):
            sync.wait_ge(p.ds.h, fin)
    return nc


def build_l0ffn_dev():
    nc = bass.Bass("TRN2", target_bir_lowering=False)
    es = contextlib.ExitStack()
    with es:
        nm = dict(
            x=([D, T], F32, "ExternalInput"),
            out=([D, T], F32, "ExternalOutput"),
            L0_ffn_gate_w=([8, 128, 16], F32, "ExternalInput"),
            L0_ffn_gate_b=([16, 1], F32, "ExternalInput"),
            L0_ffn_shared_wg=([8, 128, 512], F32, "ExternalInput"),
            L0_ffn_shared_wu=([8, 128, 512], F32, "ExternalInput"),
            L0_ffn_shared_wd=([4, 128, 1024], F32, "ExternalInput"),
            L0_ffn_experts_wg=([16, 8, 128, 512], F32, "ExternalInput"),
            L0_ffn_experts_wu=([16, 8, 128, 512], F32, "ExternalInput"),
            L0_ffn_experts_wd=([16, 4, 128, 1024], F32, "ExternalInput"),
            cwd=([16, T], F32, "ExternalOutput"),
            dbg_rfm=([16, T], F32, "ExternalOutput"),
            dbg_rtm=([128, 256], F32, "ExternalOutput"),
            dbg_cw=([128, 256], F32, "ExternalOutput"),
            hd=([68, 128, T], BF16, "ExternalOutput"),
        )
        blk = es.enter_context(nc.Block())
        p = P(nc, es, blk)
        for k, (shp, dt, kind) in nm.items():
            p.dram[k] = nc.dram_tensor(k, shp, dt, kind=kind).ap()
        X = p.sbuf("X", [128, NCH * T], F32)
        XNBF = p.sbuf("XNBF", [128, NCH * T], BF16)
        emit_consts(p)
        emit_stream_load(p, "x", X[:])
        n = p.ds.n

        @blk.vector
        def _(v):
            v.wait_ge(p.ds.h, n)
            for c in range(NCH):
                instr = v.tensor_copy(XNBF[:, c * T:(c + 1) * T],
                                      X[:, c * T:(c + 1) * T])
                p.dve.inc(instr)

        emit_l0ffn(p, X[:], XNBF[:], lambda: X[:])
        emit_stream_store(p, X[:], "out", waits=p.snap())
        fin = p.ds.n

        @blk.sync
        def _(sync):
            sync.wait_ge(p.ds.h, fin)
    return nc


def build_mha_dev(alibi=True):
    nc = bass.Bass("TRN2", target_bir_lowering=False)
    es = contextlib.ExitStack()
    with es:
        nm = dict(
            x=([D, T], F32, "ExternalInput"),
            out=([D, T], F32, "ExternalOutput"),
            wq=([8, 128, 1024], F32, "ExternalInput"),
            wk=([8, 128, 1024], F32, "ExternalInput"),
            wv=([8, 128, 1024], F32, "ExternalInput"),
            wo=([8, 128, 1024], F32, "ExternalInput"),
            bq=([128, 8], F32, "ExternalInput"),
            bk=([128, 8], F32, "ExternalInput"),
            bv=([128, 8], F32, "ExternalInput"),
            bo=([128, 8], F32, "ExternalInput"),
            slopes=([16, 1], F32, "ExternalInput"),
            dbg_att=([128, NCH * T], BF16, "ExternalOutput"),
            dbg_qt0=([128, T], BF16, "ExternalOutput"),
            dbg_kt0=([128, T], BF16, "ExternalOutput"),
            dbg_qt1=([128, T], BF16, "ExternalOutput"),
            dbg_kt1=([128, T], BF16, "ExternalOutput"),
            dbg_vf=([128, T], BF16, "ExternalOutput"),
            dbg_vtm=([128, 2048], BF16, "ExternalOutput"),
            dbg_psb=([128, 4 * T], BF16, "ExternalOutput"),
        )
        blk = es.enter_context(nc.Block())
        p = P(nc, es, blk)
        for k, (shp, dt, kind) in nm.items():
            p.dram[k] = nc.dram_tensor(k, shp, dt, kind=kind).ap()
        X = p.sbuf("X", [128, NCH * T], F32)
        XNBF = p.sbuf("XNBF", [128, NCH * T], BF16)
        emit_consts(p)
        emit_stream_load(p, "x", X[:])
        n = p.ds.n

        @blk.vector
        def _(v):
            v.wait_ge(p.ds.h, n)
            for c in range(NCH):
                instr = v.tensor_copy(XNBF[:, c * T:(c + 1) * T],
                                      X[:, c * T:(c + 1) * T])
                p.dve.inc(instr)
            for c in range(NCH):
                instr = v.memset(X[:, c * T:(c + 1) * T], 0.0)
                p.dve.inc(instr)

        emit_mha(p, XNBF[:], X[:], "wq", "wk", "wv", "wo",
                 "bq", "bk", "bv", "bo", 0.125,
                 "slopes" if alibi else None)
        emit_stream_store(p, X[:], "out", waits=p.snap())
        fin = p.ds.n

        @blk.sync
        def _(sync):
            sync.wait_ge(p.ds.h, fin)
    return nc


def emit_stream_add(p, DST, SRC):
    blk = p.blk
    start = p.snap()

    @blk.vector
    def _(v):
        p.wait_snap(v, start)
        for c in range(NCH):
            instr = v.tensor_tensor(DST[:, c * T:(c + 1) * T],
                                    DST[:, c * T:(c + 1) * T],
                                    SRC[:, c * T:(c + 1) * T], ALU.add)
            p.dve.inc(instr)
    return p.dve.n


def emit_spill(p, SRC, dram_name, load=False):
    blk = p.blk
    start = p.snap()

    @blk.sync
    def _(sync):
        p.wait_snap(sync, start)
        if load:
            instr = sync.dma_start(out=SRC, in_=p.dram[dram_name])
        else:
            instr = sync.dma_start(out=p.dram[dram_name], in_=SRC)
        p.ds.inc(instr, 16)
    return p.ds.n


def emit_ln_params(p, S, SRC, DST, DST_BF, gname, bname, gb_tile):
    blk = p.blk
    start = p.snap()

    @blk.sync
    def _(sync):
        p.wait_snap(sync, start)
        instr = sync.dma_start(out=gb_tile[:, 0:NCH], in_=p.dram[gname])
        p.ds.inc(instr, 16)
        instr = sync.dma_start(out=gb_tile[:, NCH:2 * NCH], in_=p.dram[bname])
        p.ds.inc(instr, 16)

    emit_ln(p, S, SRC, DST, DST_BF, gcol=gb_tile[:, 0:NCH],
            becol=gb_tile[:, NCH:2 * NCH])


def emit_block1(p, XOB, XNBF, GB):
    inn = StreamBuf(p, "b1_inner")
    INNER = inn.alloc()
    with contextlib.ExitStack() as es_s:
        S = LnScratch(p, "b1", es_s)
        emit_ln(p, S, XOB.ap, INNER, XNBF)
    inn.spill("xnspill")
    emit_mha(p, XNBF, XOB.ap, "L1_attn_Wq", "L1_attn_Wk", "L1_attn_Wv",
             "L1_attn_Wo", "L1_attn_bq", "L1_attn_bk", "L1_attn_bv",
             "L1_attn_bo", 0.125, "L1_attn_slopes", tag="a1")
    XOB.spill("xospill")
    INNER = inn.reload("xnspill")
    with contextlib.ExitStack() as es_s:
        S = LnScratch(p, "b1b", es_s)
        emit_ln_params(p, S, INNER, None, XNBF, "L1_ffn_g1", "L1_ffn_b1", GB)
    emit_mha(p, XNBF, INNER, "L1_ffn_Wq", "L1_ffn_Wk", "L1_ffn_Wv",
             "L1_ffn_Wo", None, None, None, None, 0.125, None, tag="f1")
    if "dbg_x1" in p.dram:
        emit_spill(p, INNER, "dbg_x1")
    XO = XOB.reload("xospill")
    emit_stream_add(p, XO, INNER)
    with contextlib.ExitStack() as es_s:
        S = LnScratch(p, "b1c", es_s)
        emit_ln_params(p, S, INNER, INNER, XNBF, "L1_ffn_g2", "L1_ffn_b2", GB)
    if "dbg_xn2" in p.dram:
        emit_spill(p, INNER, "dbg_xn2")
    emit_l1moe(p, INNER, XNBF, XO, between=inn.drop)


def build_block1_dev():
    nc = bass.Bass("TRN2", target_bir_lowering=False)
    es = contextlib.ExitStack()
    with es:
        nm = dict(
            x=([D, T], F32, "ExternalInput"),
            out=([D, T], F32, "ExternalOutput"),
            L1_attn_Wq=([8, 128, 1024], F32, "ExternalInput"),
            L1_attn_Wk=([8, 128, 1024], F32, "ExternalInput"),
            L1_attn_Wv=([8, 128, 1024], F32, "ExternalInput"),
            L1_attn_Wo=([8, 128, 1024], F32, "ExternalInput"),
            L1_attn_bq=([128, 8], F32, "ExternalInput"),
            L1_attn_bk=([128, 8], F32, "ExternalInput"),
            L1_attn_bv=([128, 8], F32, "ExternalInput"),
            L1_attn_bo=([128, 8], F32, "ExternalInput"),
            L1_attn_slopes=([16, 1], F32, "ExternalInput"),
            L1_ffn_Wq=([8, 128, 1024], F32, "ExternalInput"),
            L1_ffn_Wk=([8, 128, 1024], F32, "ExternalInput"),
            L1_ffn_Wv=([8, 128, 1024], F32, "ExternalInput"),
            L1_ffn_Wo=([8, 128, 1024], F32, "ExternalInput"),
            L1_ffn_g1=([128, 8], F32, "ExternalInput"),
            L1_ffn_b1=([128, 8], F32, "ExternalInput"),
            L1_ffn_g2=([128, 8], F32, "ExternalInput"),
            L1_ffn_b2=([128, 8], F32, "ExternalInput"),
            L1_ffn_Wg=([8, 128, 8], F32, "ExternalInput"),
            L1_ffn_W1=([8, 8, 128, 512], F32, "ExternalInput"),
            L1_ffn_W2=([8, 4, 128, 1024], F32, "ExternalInput"),
            xnspill=([128, NCH * T], F32, "ExternalOutput"),
            xospill=([128, NCH * T], F32, "ExternalOutput"),
            dbg_x1=([128, NCH * T], F32, "ExternalOutput"),
            dbg_xn2=([128, NCH * T], F32, "ExternalOutput"),
            cwd=([16, T], F32, "ExternalOutput"),
            hd=([68, 128, T], BF16, "ExternalOutput"),
        )
        blk = es.enter_context(nc.Block())
        p = P(nc, es, blk)
        for k, (shp, dt, kind) in nm.items():
            p.dram[k] = nc.dram_tensor(k, shp, dt, kind=kind).ap()
        XNBF = p.sbuf("XNBF", [128, NCH * T], BF16)
        GB = p.sbuf("GB", [128, 2 * NCH], F32)
        p.init_arena(44050)
        XOB = StreamBuf(p, "XO")
        XOB.alloc()
        emit_consts(p)
        emit_stream_load(p, "x", XOB.ap)
        emit_block1(p, XOB, XNBF[:], GB)
        emit_stream_store(p, XOB.ap, "out", waits=p.snap())
        fin = p.ds.n

        @blk.sync
        def _(sync):
            sync.wait_ge(p.ds.h, fin)
    return nc


def build_l0a_dev(nlayers=1):
    nc = bass.Bass("TRN2", target_bir_lowering=False)
    es = contextlib.ExitStack()
    with es:
        nm = dict(
            x=([D, T], F32, "ExternalInput"),
            out=([D, T], F32, "ExternalOutput"),
            hd=([68, 128, T], BF16, "Internal"),
            dbg_xa=([128, NCH * T], F32, "ExternalOutput"),
            dbg_latt=([128, NCH * T], BF16, "ExternalOutput"),
        )
        for l in range(nlayers):
            nm.update({
                f"A{l}_wq": ([8, 128, 1024], F32, "ExternalInput"),
                f"A{l}_wk": ([8, 128, 1024], F32, "ExternalInput"),
                f"A{l}_wv": ([8, 128, 1024], F32, "ExternalInput"),
                f"A{l}_wo": ([8, 128, 1024], F32, "ExternalInput"),
                f"A{l}_bq": ([128, 8], F32, "ExternalInput"),
                f"A{l}_bk": ([128, 8], F32, "ExternalInput"),
                f"A{l}_bv": ([128, 8], F32, "ExternalInput"),
                f"A{l}_bo": ([128, 8], F32, "ExternalInput"),
                f"A{l}_g1": ([128, 8], F32, "ExternalInput"),
                f"A{l}_be1": ([128, 8], F32, "ExternalInput"),
                f"A{l}_g2": ([128, 8], F32, "ExternalInput"),
                f"A{l}_be2": ([128, 8], F32, "ExternalInput"),
                f"A{l}_w1": ([8, 128, 4096], F32, "ExternalInput"),
                f"A{l}_b1": ([128, 32], F32, "ExternalInput"),
                f"A{l}_w2": ([32, 128, 1024], F32, "ExternalInput"),
                f"A{l}_b2": ([128, 8], F32, "ExternalInput"),
            })
        blk = es.enter_context(nc.Block())
        p = P(nc, es, blk)
        for k, (shp, dt, kind) in nm.items():
            p.dram[k] = nc.dram_tensor(k, shp, dt, kind=kind).ap()
        XBF = p.sbuf("XBF", [128, NCH * T], BF16)
        GB = p.sbuf("GB", [128, 2 * NCH], F32)
        p.init_arena(44050)
        inn = StreamBuf(p, "INNER")
        INNER = inn.alloc()
        emit_consts(p)
        emit_stream_load(p, "x", INNER)
        n = p.ds.n

        @blk.vector
        def _(v):
            v.wait_ge(p.ds.h, n)
            for c in range(NCH):
                instr = v.tensor_copy(XBF[:, c * T:(c + 1) * T],
                                      INNER[:, c * T:(c + 1) * T])
                p.dve.inc(instr)

        import os as _os
        _ph = _os.environ.get("L0A_PH")
        phset = set(_ph.split(",")) if _ph else None
        for l in range(nlayers):
            with contextlib.ExitStack() as es_s:
                S = LnScratch(p, f"l0a{l}", es_s)
                emit_l0attn_layer(p, INNER, XBF[:], f"A{l}", S, GB, phset)
        emit_stream_store(p, INNER, "out", waits=p.snap())
        fin = p.ds.n

        @blk.sync
        def _(sync):
            sync.wait_ge(p.ds.h, fin)
    return nc


def emit_block0(p, XOB, XNBF, GB):
    inn = StreamBuf(p, "b0_inner")
    INNER = inn.alloc()
    with contextlib.ExitStack() as es_s:
        S = LnScratch(p, "b0", es_s)
        emit_ln(p, S, XOB.ap, INNER, XNBF)
    emit_stream_add(p, XOB.ap, INNER)

    def pre_e():
        XOB.spill("xospill")
        inn.spill("xnspill")

    emit_l0ffn(p, INNER, XNBF, xo_get=lambda: XOB.ap,
               pre_e=pre_e,
               pre_d=lambda: XOB.reload("xospill"))
    XOB.spill("xospill")
    INNER = inn.reload("xnspill")
    emit_l0attn(p, INNER, XNBF, GB)
    XOB.reload("xospill")
    emit_stream_add(p, XOB.ap, INNER)
    inn.drop()


def _dram_specs(part):
    nm = dict(
        x=([D, T], F32, "ExternalInput"),
        out=([D, T], F32, "ExternalOutput"),
        xnspill=([128, NCH * T], F32, "Internal"),
        xospill=([128, NCH * T], F32, "Internal"),
        cwd=([16, T], F32, "Internal"),
        hd=([68, 128, T], BF16, "Internal"),
    )
    _dbg = os.environ.get("MK_DBG", "")
    if _dbg in ("1", "x1"):
        nm["dbg_x1"] = ([128, NCH * T], F32, "Internal")
    if _dbg in ("1", "xn2"):
        nm["dbg_xn2"] = ([128, NCH * T], F32, "Internal")
    if part == "b0":
        nm["x"] = ([D, T], F32, "ExternalInput")
    if part in ("all", "b0"):
        for l in range(6):
            nm.update({
                f"A{l}_wq": ([8, 128, 1024], F32, "ExternalInput"),
                f"A{l}_wk": ([8, 128, 1024], F32, "ExternalInput"),
                f"A{l}_wv": ([8, 128, 1024], F32, "ExternalInput"),
                f"A{l}_wo": ([8, 128, 1024], F32, "ExternalInput"),
                f"A{l}_bq": ([128, 8], F32, "ExternalInput"),
                f"A{l}_bk": ([128, 8], F32, "ExternalInput"),
                f"A{l}_bv": ([128, 8], F32, "ExternalInput"),
                f"A{l}_bo": ([128, 8], F32, "ExternalInput"),
                f"A{l}_g1": ([128, 8], F32, "ExternalInput"),
                f"A{l}_be1": ([128, 8], F32, "ExternalInput"),
                f"A{l}_g2": ([128, 8], F32, "ExternalInput"),
                f"A{l}_be2": ([128, 8], F32, "ExternalInput"),
                f"A{l}_w1": ([8, 128, 4096], F32, "ExternalInput"),
                f"A{l}_b1": ([128, 32], F32, "ExternalInput"),
                f"A{l}_w2": ([32, 128, 1024], F32, "ExternalInput"),
                f"A{l}_b2": ([128, 8], F32, "ExternalInput"),
            })
        nm.update({
            "A_gf": ([128, 8], F32, "ExternalInput"),
            "A_bf": ([128, 8], F32, "ExternalInput"),
            "L0_ffn_gate_w": ([8, 128, 16], F32, "ExternalInput"),
            "L0_ffn_gate_b": ([16, 1], F32, "ExternalInput"),
            "L0_ffn_shared_wg": ([8, 128, 512], F32, "ExternalInput"),
            "L0_ffn_shared_wu": ([8, 128, 512], F32, "ExternalInput"),
            "L0_ffn_shared_wd": ([4, 128, 1024], F32, "ExternalInput"),
            "L0_ffn_experts_wg": ([16, 8, 128, 512], F32, "ExternalInput"),
            "L0_ffn_experts_wu": ([16, 8, 128, 512], F32, "ExternalInput"),
            "L0_ffn_experts_wd": ([16, 4, 128, 1024], F32, "ExternalInput"),
        })
    if part in ("all", "b1"):
        nm.update({
            "L1_attn_Wq": ([8, 128, 1024], F32, "ExternalInput"),
            "L1_attn_Wk": ([8, 128, 1024], F32, "ExternalInput"),
            "L1_attn_Wv": ([8, 128, 1024], F32, "ExternalInput"),
            "L1_attn_Wo": ([8, 128, 1024], F32, "ExternalInput"),
            "L1_attn_bq": ([128, 8], F32, "ExternalInput"),
            "L1_attn_bk": ([128, 8], F32, "ExternalInput"),
            "L1_attn_bv": ([128, 8], F32, "ExternalInput"),
            "L1_attn_bo": ([128, 8], F32, "ExternalInput"),
            "L1_attn_slopes": ([16, 1], F32, "ExternalInput"),
            "L1_ffn_Wq": ([8, 128, 1024], F32, "ExternalInput"),
            "L1_ffn_Wk": ([8, 128, 1024], F32, "ExternalInput"),
            "L1_ffn_Wv": ([8, 128, 1024], F32, "ExternalInput"),
            "L1_ffn_Wo": ([8, 128, 1024], F32, "ExternalInput"),
            "L1_ffn_g1": ([128, 8], F32, "ExternalInput"),
            "L1_ffn_b1": ([128, 8], F32, "ExternalInput"),
            "L1_ffn_g2": ([128, 8], F32, "ExternalInput"),
            "L1_ffn_b2": ([128, 8], F32, "ExternalInput"),
            "L1_ffn_Wg": ([8, 128, 8], F32, "ExternalInput"),
            "L1_ffn_W1": ([8, 8, 128, 512], F32, "ExternalInput"),
            "L1_ffn_W2": ([8, 4, 128, 1024], F32, "ExternalInput"),
        })
    return nm


def build_full(part="all"):
    nc = bass.Bass("TRN2", target_bir_lowering=False)
    es = contextlib.ExitStack()
    with es:
        blk = es.enter_context(nc.Block())
        p = P(nc, es, blk)
        for k, (shp, dt, kind) in _dram_specs(part).items():
            p.dram[k] = nc.dram_tensor(k, shp, dt, kind=kind).ap()
        XNBF = p.sbuf("XNBF", [128, NCH * T], BF16)
        GB = p.sbuf("GB", [128, 2 * NCH], F32)
        p.init_arena(44050)
        XOB = StreamBuf(p, "XO")
        XOB.alloc()
        emit_consts(p)
        emit_stream_load(p, "x", XOB.ap)
        if part in ("all", "b0"):
            emit_block0(p, XOB, XNBF[:], GB)
        if part in ("all", "b1"):
            emit_block1(p, XOB, XNBF[:], GB)
        emit_stream_store(p, XOB.ap, "out", waits=p.snap())
        fin = p.ds.n

        @blk.sync
        def _(sync):
            sync.wait_ge(p.ds.h, fin)
    return nc


def _prep_weights(inputs):
    I = inputs
    w = {}
    for l in range(6):
        w[f"A{l}_wq"] = _wchunk(I["L0_attn_Wq"][l])
        w[f"A{l}_wk"] = _wchunk(I["L0_attn_Wk"][l])
        w[f"A{l}_wv"] = _wchunk(I["L0_attn_Wv"][l])
        w[f"A{l}_wo"] = _wchunk(I["L0_attn_Wo"][l])
        w[f"A{l}_bq"] = _colvec(I["L0_attn_bq"][l])
        w[f"A{l}_bk"] = _colvec(I["L0_attn_bk"][l])
        w[f"A{l}_bv"] = _colvec(I["L0_attn_bv"][l])
        w[f"A{l}_bo"] = _colvec(I["L0_attn_bo"][l])
        w[f"A{l}_g1"] = _colvec(I["L0_attn_g1"][l])
        w[f"A{l}_be1"] = _colvec(I["L0_attn_be1"][l])
        w[f"A{l}_g2"] = _colvec(I["L0_attn_g2"][l])
        w[f"A{l}_be2"] = _colvec(I["L0_attn_be2"][l])
        w[f"A{l}_w1"] = _wchunk(I["L0_attn_W1"][l])
        w[f"A{l}_b1"] = _colvec(I["L0_attn_b1"][l])
        w[f"A{l}_w2"] = _wchunk(I["L0_attn_W2"][l])
        w[f"A{l}_b2"] = _colvec(I["L0_attn_b2"][l])
    w["A_gf"] = _colvec(I["L0_attn_gf"])
    w["A_bf"] = _colvec(I["L0_attn_bf"])
    w["L0_ffn_gate_w"] = _wchunk(I["L0_ffn_gate_w"])
    w["L0_ffn_gate_b"] = np.ascontiguousarray(
        np.asarray(I["L0_ffn_gate_b"], np.float32)[:, None])
    w["L0_ffn_shared_wg"] = _wchunk(I["L0_ffn_shared_wg"])
    w["L0_ffn_shared_wu"] = _wchunk(I["L0_ffn_shared_wu"])
    w["L0_ffn_shared_wd"] = _wchunk(I["L0_ffn_shared_wd"])
    for nm2 in ("wg", "wu", "wd"):
        w[f"L0_ffn_experts_{nm2}"] = np.stack(
            [_wchunk(np.asarray(I[f"L0_ffn_experts_{nm2}"][e]))
             for e in range(16)])
    w["L1_attn_Wq"] = _wchunk(np.asarray(I["L1_attn_Wq"], np.float32).T)
    w["L1_attn_Wk"] = _wchunk(np.asarray(I["L1_attn_Wk"], np.float32).T)
    w["L1_attn_Wv"] = _wchunk(np.asarray(I["L1_attn_Wv"], np.float32).T)
    w["L1_attn_Wo"] = _wchunk(np.asarray(I["L1_attn_Wo"], np.float32).T)
    w["L1_attn_bq"] = _colvec(I["L1_attn_bq"])
    w["L1_attn_bk"] = _colvec(I["L1_attn_bk"])
    w["L1_attn_bv"] = _colvec(I["L1_attn_bv"])
    w["L1_attn_bo"] = _colvec(I["L1_attn_bo"])
    w["L1_attn_slopes"] = np.ascontiguousarray(
        np.asarray(I["L1_attn_slopes"], np.float32)[:, None])
    for nm2 in ("Wq", "Wk", "Wv", "Wo"):
        w[f"L1_ffn_{nm2}"] = _wchunk(I[f"L1_ffn_{nm2}"])
    w["L1_ffn_g1"] = _colvec(I["L1_ffn_g1"])
    w["L1_ffn_b1"] = _colvec(I["L1_ffn_b1"])
    w["L1_ffn_g2"] = _colvec(I["L1_ffn_g2"])
    w["L1_ffn_b2"] = _colvec(I["L1_ffn_b2"])
    w["L1_ffn_Wg"] = _wchunk(I["L1_ffn_Wg"])
    w["L1_ffn_W1"] = np.stack(
        [_wchunk(np.asarray(I["L1_ffn_W1"][e])) for e in range(8)])
    w["L1_ffn_W2"] = np.stack(
        [_wchunk(np.asarray(I["L1_ffn_W2"][e])) for e in range(8)])
    return w


def kernel(**inputs):
    from concourse.bass_utils import run_bass_kernel_spmd as _run
    w = _prep_weights(inputs)
    x = np.asarray(inputs["L0_attn_x"], np.float32)
    xcores = [np.ascontiguousarray(x[i].T) for i in range(8)]
    mode = int(os.environ.get("MK_LAUNCHES", "1"))
    if mode == 1:
        nc = build_full("all")
        in_maps = [dict(w, x=xcores[i]) for i in range(8)]
        res = _run(nc, in_maps, core_ids=list(range(8)))
        return np.stack([np.asarray(r["out"]).T for r in res.results])
    else:
        nc0 = build_full("b0")
        keys0 = {k for k, (s, d, kd) in _dram_specs("b0").items()
                 if kd == "ExternalInput"}
        in_maps = [{k: (xcores[i] if k == "x" else w[k]) for k in keys0}
                   for i in range(8)]
        res0 = _run(nc0, in_maps, core_ids=list(range(8)))
        mids = [np.asarray(r["out"]) for r in res0.results]
        nc1 = build_full("b1")
        keys1 = {k for k, (s, d, kd) in _dram_specs("b1").items()
                 if kd == "ExternalInput"}
        in_maps = [{k: (mids[i] if k == "x" else w[k]) for k in keys1}
                   for i in range(8)]
        res1 = _run(nc1, in_maps, core_ids=list(range(8)))
        return np.stack([np.asarray(r["out"]).T for r in res1.results])
